# version 10
# speedup vs baseline: 1.2269x; 1.0019x over previous
import numpy as np
import ml_dtypes

import concourse.bass as bass
import concourse.mybir as mybir
from concourse.bass_utils import run_bass_kernel_spmd
from concourse.ap import AP

F32 = mybir.dt.float32
BF16 = mybir.dt.bfloat16
I32 = mybir.dt.int32
OP = mybir.AluOpType
AF = mybir.ActivationFunctionType
BFNP = ml_dtypes.bfloat16

HID = 2048; KC_H = 16
INTER = 4096; KC_I = 32
HEADS = 64; HD = 64; STN = 64
CONV = 4224; OC_CONV = 33
KCONV = 4
OC_P = 66; OC_DT = 33; OC_GATE0 = 34
T = 1024; TS = 512; NCH = 8; CH = 128
FI = 5504; FI_PAD = 6144; KC_F = 48; KC_FR = 43; PCF = 6
NCORE = 8
LN_EPS = 1e-5; RMS_EPS = 1e-6
ENGS = ("sync", "scalar", "vector", "tensor", "gpsimd")


def _tile_w_in(in_proj_w):
    W = np.asarray(in_proj_w, np.float32)
    perm = np.empty((OC_P * 128, HID), np.float32)
    perm[:CONV] = W[INTER:INTER + CONV]
    perm[CONV:CONV + 64] = W[INTER + CONV:]
    perm[CONV + 64:CONV + 128] = 0.0
    perm[CONV + 128:] = W[:INTER]
    return np.ascontiguousarray(perm.reshape(OC_P, 128, KC_H, 128).transpose(0, 3, 2, 1))

def _tile_w_out(out_proj_w):
    W = np.asarray(out_proj_w, np.float32)
    return np.ascontiguousarray(W.reshape(16, 128, KC_I, 128).transpose(0, 3, 2, 1))

def _tile_gu_idx(idx):
    p = np.zeros((FI_PAD, HID), np.int32)
    p[:FI] = np.asarray(idx, np.int32)
    return np.ascontiguousarray(p.reshape(KC_F, 128, KC_H, 128).transpose(0, 3, 2, 1))

def _tile_d_idx(idx):
    p = np.zeros((HID, FI_PAD), np.int32)
    p[:, :FI] = np.asarray(idx, np.int32)
    t = np.ascontiguousarray(p.T).reshape(KC_F, 128, 4, 512).transpose(0, 2, 1, 3)
    return np.ascontiguousarray(t)

def _conv_diag(conv_w):
    w = np.asarray(conv_w, np.float32)
    d = np.zeros((OC_CONV, KCONV, 128, 128), np.float32)
    ar = np.arange(128)
    for fc in range(OC_CONV):
        for k in range(KCONV):
            d[fc, k, ar, ar] = w[fc * 128:(fc + 1) * 128, k]
    return d

def prep_inputs(inputs):
    x = np.asarray(inputs["L0_attn_hidden_states"], np.float32)
    com = {}
    com["w_in"] = _tile_w_in(inputs["L0_attn_in_proj_w"])
    com["w_out"] = _tile_w_out(inputs["L0_attn_out_proj_w"])
    com["conv_diag"] = _conv_diag(inputs["L0_attn_conv_w"])
    com["conv_b"] = np.ascontiguousarray(
        np.asarray(inputs["L0_attn_conv_b"], np.float32).reshape(OC_CONV, 128).T)
    com["dt_bias"] = np.asarray(inputs["L0_attn_dt_bias"], np.float32).reshape(64, 1)
    com["a_log"] = np.asarray(inputs["L0_attn_A_log"], np.float32).reshape(64, 1)
    com["d_chan"] = np.ascontiguousarray(
        np.repeat(np.asarray(inputs["L0_attn_D"], np.float32), HD).reshape(KC_I, 128).T)
    com["norm_w"] = np.ascontiguousarray(
        np.asarray(inputs["L0_attn_norm_weight"], np.float32).reshape(KC_I, 128).T)
    com["lut"] = np.asarray(inputs["L0_ffn_lut"], np.float32).reshape(1, 16)
    gi = _tile_gu_idx(inputs["L0_ffn_gate_idx"])
    ui = _tile_gu_idx(inputs["L0_ffn_up_idx"])
    di = _tile_d_idx(inputs["L0_ffn_down_idx"])
    com["ident_bf"] = np.eye(128, dtype=BFNP)
    com["ident_f"] = np.eye(128, dtype=np.float32)
    com["ones_bf"] = np.ones((128, 128), dtype=BFNP)
    com["tri_bf"] = (np.arange(128)[:, None] <= np.arange(128)[None, :]).astype(BFNP)
    cores = []
    for c in range(NCORE):
        b, s = c // 4, c % 4
        t0 = s * T
        xf = np.zeros((HID, T + 128), np.float32)
        lo = max(0, t0 - 3)
        seg = x[b, lo:t0 + T].T
        xf[:, (T + 128) - seg.shape[1]:] = seg
        E = np.zeros((8, 8), np.float32)
        m = np.zeros((8, 1), np.float32)
        for j in range(4 * b, 4 * b + s):
            m[j] = 1.0
            for k in range(j + 1, 4 * b + s):
                E[j, k] = 1.0
        d = dict(com)
        d["x_fm"] = np.ascontiguousarray(xf)
        d["gidx"] = np.ascontiguousarray(gi[PCF * c:PCF * (c + 1)])
        d["uidx"] = np.ascontiguousarray(ui[PCF * c:PCF * (c + 1)])
        d["didx"] = np.ascontiguousarray(di[PCF * c:PCF * (c + 1)])
        d["emask"] = np.ascontiguousarray(E.T)
        d["mmask"] = m
        cores.append(d)
    return cores


class Sem:
    __slots__ = ("h", "v", "q")
    def __init__(self, h, q=False):
        self.h = h; self.v = 0; self.q = q

class Bld:
    def __init__(self, nc, sems, taps):
        self.nc = nc
        self.sems = sems
        self.taps = taps
        self.qsems = {(q, kk): sems[f"q_{q}_{kk}"] for q in ("sync", "scalar", "gpsimd")
                      for kk in ("ld", "st")}
        self.e = None
        self.eng = None
    def walk(self, eng_name, fn):
        self.e = eng_name
        self.eng = getattr(self.nc, eng_name)
        for s in self.sems.values():
            s.v = 0
        fn(self)
    def _go(self, eng):
        return self.e == eng
    def wait(self, eng, sem, val=None):
        if self._go(eng):
            self.eng.wait_ge(sem.h, sem.v if val is None else val)
    def waitv(self, eng, sem, val):
        if self._go(eng):
            self.eng.wait_ge(sem.h, val)
    def dma(self, q, out, in_, kind="ld", **kw):
        sem = self.qsems[(q, "st" if kind == "st" else "ld")]
        v = sem.v = sem.v + 16
        if self._go(q):
            self.eng.dma_start(out=out, in_=in_, **kw).then_inc(sem.h, 16)
        return (sem, v)
    def op(self, eng, sem, f, n=1):
        v = None
        if sem is not None:
            v = sem.v = sem.v + n
        if self._go(eng):
            ins = f(self.eng)
            if sem is not None:
                ins.then_inc(sem.h, n)
        return v
    def ops(self, eng, f):
        if self._go(eng):
            f(self.eng)

def ap_of(t):
    return t.ap() if hasattr(t, "ap") else t

def build_nc(comm=True, taps=()):
    nc = bass.Bass("TRN2", target_bir_lowering=False, detect_race_conditions=False)
    dr = {}
    def din(name, shape, dt):
        dr[name] = nc.dram_tensor(name, list(shape), dt, kind="ExternalInput")
    def dint(name, shape, dt, **kw):
        dr[name] = nc.dram_tensor(name, list(shape), dt, kind="Internal", **kw)
    din("x_fm", (HID, T + 128), F32)
    din("w_in", (OC_P, 128, KC_H, 128), F32)
    din("w_out", (16, 128, KC_I, 128), F32)
    din("conv_diag", (OC_CONV, KCONV, 128, 128), F32)
    din("conv_b", (128, OC_CONV), F32)
    din("dt_bias", (64, 1), F32)
    din("a_log", (64, 1), F32)
    din("d_chan", (128, KC_I), F32)
    din("norm_w", (128, KC_I), F32)
    din("lut", (1, 16), F32)
    din("gidx", (PCF, 128, KC_H, 128), I32)
    din("uidx", (PCF, 128, KC_H, 128), I32)
    din("didx", (PCF, 4, 128, 512), I32)
    din("ident_bf", (128, 128), BF16)
    din("ident_f", (128, 128), F32)
    din("ones_bf", (128, 128), BF16)
    din("tri_bf", (128, 128), BF16)
    din("emask", (8, 8), F32)
    din("mmask", (8, 1), F32)
    dr["out"] = nc.dram_tensor("out", [T, HID], F32, kind="ExternalOutput")

    dint("gate_d", (2, 128, 32, TS), BF16)
    dint("hbc_d", (128, OC_CONV, TS + 3), BF16)
    dint("y_d", (NCH, 8, 128, 512), BF16)
    dint("snap_d", (NCH, 64, 4096), BF16)
    dint("cum_d", (NCH, 64, 128), F32)
    dint("x2_d", (8, 128, HID), F32)
    dint("pref_d", (8, 64), F32)
    dint("lam_d", (NCH, 64), F32)
    dint("tot_d", (64, 1), F32)
    dint("coef_d", (8, 64), F32)
    dint("ag1g_in", (PCF, 128, KC_H, 128), BF16)
    dint("ag1u_in", (PCF, 128, KC_H, 128), BF16)
    dint("ag1d_in", (PCF, 4, 128, 512), BF16)
    dint("ag2_in", (64, 4160), F32)
    spc = dict(addr_space="Shared") if comm else {}
    dint("ag1g_out", (KC_F, 128, KC_H, 128), BF16, **spc)
    dint("ag1u_out", (KC_F, 128, KC_H, 128), BF16, **spc)
    dint("ag1d_out", (KC_F, 4, 128, 512), BF16, **spc)
    dint("ag2_out", (NCORE * 64, 4160), F32, **spc)

    tap_t = {}
    for tp in taps:
        tap_t[tp] = nc.dram_tensor("tap_" + tp, list(_TAPS[tp]), F32, kind="ExternalOutput")

    sb = {}
    def sbn(name, shape, dt):
        sb[name] = nc.alloc_sbuf_tensor("s_" + name, list(shape), dt)
        return sb[name]
    for nm, sh, dt in [
        ("ident_bf", (128, 128), BF16), ("ident_f", (128, 128), F32),
        ("ones_bf", (128, 128), BF16), ("tri_bf", (128, 128), BF16),
        ("conv_b", (128, OC_CONV), F32), ("dt_bias", (64, 1), F32),
        ("a_neg", (64, 1), F32), ("d_chan", (128, KC_I), F32),
        ("norm_w", (128, KC_I), F32), ("lut_rep", (128, 16), F32),
        ("emask", (8, 8), F32), ("mmask", (8, 1), F32),
        ("canary", (64, 184), F32),
        ("totals", (64, NCH), F32), ("s_state", (64, 4096), F32),
        ("s_tmp", (64, 512), F32), ("lam_rep", (64, 64), F32),
        ("stat_a", (128, T + 128), BF16), ("stat_b", (128, 520), F32),
        ("stat_c", (128, T + 128), BF16),
        ("bmat", (64, T), BF16), ("cmat", (64, T), BF16),
        ("small64", (64, 128), F32), ("small64b", (64, 128), F32),
        ("wq", (128, 2048), BF16), ("wq2", (128, 2048), BF16),
        ("ystage", (128, 512), BF16), ("sq_st", (128, 1032), BF16),
        ("cst", (128, 8), F32),
    ]:
        sbn(nm, sh, dt)
    from concourse.bass_primitives import MemorySpace
    base = int(nc.bytes_allocated(MemorySpace.SBUF, "left"))
    base = (base + 31) // 32 * 32
    cap = int(nc.sbuf_top)

    def at(name, shape, dt, off):
        t = nc.alloc_sbuf_tensor_at("s_" + name, list(shape), dt, offset=base + off)
        sb[name] = t
        return t
    K = 1024
    at("xhat", (128, KC_H * (T + 128)), BF16, 0)
    at("hbc", (128, OC_CONV * (TS + 3)), BF16, 37 * K)
    at("x_conv", (128, KC_I * 256), BF16, 71 * K)
    at("x_tok", (128, INTER), BF16, 87 * K)
    at("dtx", (128, INTER), BF16, 95 * K)
    at("cum_rep", (128, 16 * 128), F32, 103 * K)
    at("gm", (128, 16 * 128), BF16, 111 * K)
    at("dt_tok", (128, 64), F32, 115 * K)
    at("cum_t", (128, 64), F32, 116 * K)
    at("dec_fm", (64, 128), F32, 117 * K)
    at("dec_tok", (128, 64), F32, 118 * K)
    at("b_tok", (128, 64), BF16, 119 * K)
    at("gtric", (128, 128), BF16, 120 * K)
    at("stage8", (128, 1024), F32, 121 * K)
    at("hbc_sav", (128, OC_CONV * 3), BF16, 125 * K)
    at("dt", (64, T), F32, 126 * K)
    at("gvec", (64, T), F32, 130 * K)
    at("cum", (64, T), F32, 134 * K)
    at("gbuf_i", (128, 2048), BF16, 138 * K)
    at("gbuf_a", (128, 2048), BF16, 142 * K)
    at("gbuf_m", (128, 2048), BF16, 146 * K)
    at("gbuf_i2", (128, 2048), BF16, 150 * K)
    at("cum_repb", (128, 16 * 128), F32, 138 * K)
    at("gmb", (128, 16 * 128), BF16, 146 * K)
    at("y_g", (128, KC_I * T), BF16, 0)
    at("gate_sp", (128, 32 * TS), BF16, 64 * K)
    at("y_ld", (128, 4096), BF16, 96 * K)
    at("carry", (64, 4096), F32, 104 * K)
    at("fin_b", (64, 4096), F32, 120 * K)
    at("snap_ld", (64, 4096), BF16, 120 * K)
    at("trueS", (64, 4096), BF16, 136 * K)
    at("cum_rep2", (128, 16 * 128), F32, 144 * K)
    at("cum_rep2b", (128, 16 * 128), F32, 128 * K)
    at("expc", (64, 16 * 128), BF16, 152 * K)
    at("ctil", (64, 16 * 128), BF16, 156 * K)
    at("p_rep", (64, 64), F32, 160 * K)
    at("coef", (8, 64), F32, 161 * K)
    at("coef_rep", (64, 512), F32, 162 * K)
    at("y_tmp", (128, 512), F32, 160 * K)
    at("mo_bf", (128, KC_H * T), BF16, 64 * K)
    at("x2_fm", (128, KC_H * T), F32, 96 * K)
    at("xhat2", (128, KC_H * T), BF16, 0)
    at("xstage", (128, 1024), F32, 32 * K)
    at("fstage2", (128, 512), F32, 36 * K)
    at("hmat", (128, 22 * T), BF16, 32 * K)
    at("out_acc", (128, 8 * HID), F32, 76 * K)
    at("fstage", (128, 2048), F32, 140 * K)
    at("sq_big", (128, 2048), F32, 64 * K)
    at("wq3", (128, 2048), BF16, 148 * K)
    at("wq4", (128, 2048), BF16, 152 * K)
    assert base + 164 * K <= cap, (base, cap)

    sems = {}
    for s in ["mm", "ev", "sc", "gp", "cc", "ph",
              "q_sync_ld", "q_sync_st", "q_scalar_ld", "q_scalar_st",
              "q_gpsimd_ld", "q_gpsimd_st"]:
        sems[s] = Sem(nc.alloc_semaphore(s), q=s.startswith("q_"))

    meta = dict(dr=dr, sb=sb, sems=sems, taps=tap_t, comm=comm)
    return nc, meta


_TAPS = {
    "decraw": (64, 528),
    "x2d": (8, 128, HID),
    "canary": (64, 184),
    "gatesp": (128, 32 * TS),
    "rrms": (128, 128),
    "dtxd0": (128, INTER),
    "lam0": (64, 64),
    "dec0": (128, 64),
    "btok0": (128, 64),
    "sstate0": (64, 4096),
    "sstate1": (64, 4096),
    "sstate2": (64, 4096),
    "sstate3": (64, 4096),
    "sstate4": (64, 4096),
    "sstate5": (64, 4096),
    "sstate6": (64, 4096),
    "sstate7": (64, 4096),
    "dtxd2": (128, INTER),
    "dec2": (128, 64),
    "btok2": (128, 64),
    "cmat": (64, T),
    "xhat": (128, KC_H * (T + 128)),
    "hbc": (128, OC_CONV * (TS + 3)),
    "dt": (64, T),
    "gate0": (128, TS),
    "cum": (64, T),
    "xconv": (128, KC_I * 256),
    "bmat": (64, T),
    "sstate": (64, 4096),
    "ydc": (8, 128, 512),
    "carry": (64, 4096),
    "yld": (128, 4096),
    "y_g": (128, KC_I * T),
    "mo": (128, KC_H * T),
    "x2": (128, KC_H * T),
    "xhat2": (128, KC_H * T),
    "h0": (128, T),
    "outacc": (128, 8 * HID),
    "ln4": (8, 128, 24),
    "wg0": (128, 2048),
}


def fap(t, levels, off=0):
    return AP(t, off, [list(l) for l in levels])

def emit_program(b, nc, meta, pb, pbt, pb7, pfn):
    dr, sb, taps, comm = meta["dr"], meta["sb"], meta["taps"], meta["comm"]
    S = meta["sems"]
    MM, EV, SC, CC = S["mm"], S["ev"], S["sc"], S["cc"]
    wt = [sb["wq"], sb["wq2"]]

    def wtok(eng, tok):
        if tok is not None:
            b.waitv(eng, tok[0], tok[0].v if getattr(tok[0], "q", False) else tok[1])

    def tap(name, src):
        if name in taps:
            b.waitv("gpsimd", S["ev"], S["ev"].v)
            b.waitv("gpsimd", S["sc"], S["sc"].v)
            b.waitv("gpsimd", S["mm"], S["mm"].v)
            b.dma("gpsimd", taps[name].ap(), src, "st")

    for nm in ["ident_bf", "ident_f", "ones_bf", "tri_bf", "conv_b", "dt_bias",
               "d_chan", "norm_w", "emask", "mmask"]:
        b.dma("sync", sb[nm][:, :], dr[nm].ap(), "ld")
    b.dma("sync", sb["small64"][:, 0:1], dr["a_log"].ap(), "ld")
    consts_tok = b.dma("sync", sb["lut_rep"][:, :], fap(dr["lut"], [[0, 128], [1, 16]]), "ld")
    xsrc = fap(dr["x_fm"], [[T + 128, 128], [128 * (T + 128), KC_H], [1, T + 128]])
    vx = b.dma("gpsimd", sb["xhat"][:, :], xsrc, "ld")
    b.op("vector", EV, lambda E: E.memset(sb["canary"][:, :], 7.25))
    for ci, cv in enumerate([LN_EPS, RMS_EPS, -0.5, -1.0, 1.0 / HID, 1.0 / INTER, 1.0]):
        b.op("vector", EV, lambda E, ci=ci, cv=cv: E.memset(sb["cst"][:, ci:ci + 1], cv))
    cst_tok = (S["ev"], S["ev"].v)
    meta["cst_tok"] = cst_tok
    b.waitv("scalar", EV, cst_tok[1])
    wtok("scalar", consts_tok)
    va = b.op("scalar", SC, lambda E: E.activation(sb["small64b"][:, 0:1], sb["small64"][:, 0:1], AF.Exp))
    b.waitv("vector", SC, va)
    vneg = b.op("vector", EV, lambda E: E.tensor_scalar_mul(sb["a_neg"][:, :], sb["small64b"][:, 0:1], -1.0))

    gtiles = []
    for mat, (inm, onm) in enumerate([("gidx", "ag1g_in"), ("uidx", "ag1u_in"),
                                      ("didx", "ag1d_in")]):
        for ti in range(PCF):
            gtiles.append((mat, inm, onm, ti))
    gst = {"j": 0, "gv": {}, "st": {}, "ld": {}, "cc": 0}
    ibufs = ["gbuf_i", "gbuf_i2"]
    def emit_gcc(nm):
        b.waitv("gpsimd", b.qsems[("sync", "st")], b.qsems[("sync", "st")].v)
        if comm:
            b.op("gpsimd", CC, lambda E, nm=nm: E.collective_compute(
                "AllGather", OP.bypass, replica_groups=[list(range(NCORE))],
                ins=[dr[f"ag1{nm}_in"].ap().opt()], outs=[dr[f"ag1{nm}_out"].ap().opt()]))
            return (CC, CC.v)
        vv = None
        for r in range(NCORE):
            vv = b.dma("gpsimd", dr[f"ag1{nm}_out"].ap()[r * PCF:(r + 1) * PCF, :, :, :],
                  dr[f"ag1{nm}_in"].ap()[:, :, :, :], "st")
        return vv
    def emit_gjob():
        j = gst["j"]
        if j >= 72:
            return None
        gst["j"] = j + 1
        tn, q = j // 4, j % 4
        mat, inm, onm, ti = gtiles[tn]
        ib = sb[ibufs[tn % 2]]
        if q == 0:
            if tn >= 2:
                b.waitv("gpsimd", EV, gst["gv"][(tn - 2) * 4 + 3])
            if mat < 2:
                isrc = dr[inm].ap()[ti, :, :, :]
            else:
                isrc = fap(dr[inm], [[512, 128], [65536, 4], [1, 512]],
                           off=ti * 4 * 128 * 512)
            gst["ld"][tn] = b.dma("gpsimd", ib[:, :], isrc, "ld")
        h = (j % 2) * 512
        if mat < 2:
            dst = dr[onm].ap()[ti, :, 4 * q:4 * q + 4, :]
        else:
            dst = dr[onm].ap()[ti, q, :, :]
        wtok("vector", gst["ld"][tn])
        if j == 0:
            wtok("vector", consts_tok)
        if j >= 2:
            wtok("vector", gst["st"][j - 2])
        def gops(E, h=h, q=q, ib=ib):
            ii = ib[:, q * 512:q * 512 + 512]; aa = sb["gbuf_a"][:, h:h + 512]
            mm_ = sb["gbuf_m"][:, h:h + 512]
            E.tensor_scalar(aa, ii, 0.0, sb["lut_rep"][:, 0:1], OP.is_equal, OP.mult)
            for v in range(1, 16):
                E.tensor_scalar(mm_, ii, float(v), sb["lut_rep"][:, v:v + 1], OP.is_equal, OP.mult)
                last = E.tensor_add(aa, aa, mm_)
            return last
        gst["gv"][j] = b.op("vector", EV, gops)
        b.wait("sync", EV)
        gst["st"][j] = b.dma("sync", dst, sb["gbuf_a"][:, h:h + 512], "st")
        if j + 1 in (24, 48, 72):
            gst["cc"] = emit_gcc("gud"[(j + 1) // 24 - 1])
        return None
    def finish_gather():
        while gst["j"] < 72:
            emit_gjob()
        return gst["cc"]

    segs = [(128, 512), (640, 512), (0, 128)]
    wtok("tensor", vx)
    wtok("scalar", vx)
    for si, (lo, n) in enumerate(segs):
        p_s, p_q = pb[0], pb[1]
        sqv = []
        for kc in range(KC_H):
            xs = sb["xhat"][:, kc * (T + 128) + lo: kc * (T + 128) + lo + n]
            sq = sb["sq_st"][:, (kc % 2) * 516:(kc % 2) * 516 + n]
            if kc >= 2:
                b.waitv("scalar", MM, sqv[kc - 2])
            vsc = b.op("scalar", SC, lambda E, xs=xs, sq=sq: E.activation(sq, xs, AF.Square))
            b.waitv("tensor", SC, vsc)
            vm = b.op("tensor", MM, lambda E, xs=xs, sq=sq, kc=kc, n=n:
                      [E.matmul(p_s[:, :n], sb["ones_bf"][:, :], xs,
                                start=(kc == 0), stop=(kc == KC_H - 1)),
                       E.matmul(p_q[:, :n], sb["ones_bf"][:, :], sq,
                                start=(kc == 0), stop=(kc == KC_H - 1))][-1])
            sqv.append(vm)
        sa = sb["stat_a"][:, lo:lo + n]; sb_ = sb["stat_b"][:, 0:n]; sc_ = sb["stat_c"][:, lo:lo + n]
        b.waitv("scalar", MM, sqv[-1])
        b.op("scalar", SC, lambda E, sa=sa, n=n: E.activation(sa, p_s[:, :n], AF.Copy, scale=sb["cst"][:, 4:5]))
        v2 = b.op("scalar", SC, lambda E, sc_=sc_, n=n: E.activation(sc_, p_q[:, :n], AF.Copy, scale=sb["cst"][:, 4:5]))
        b.waitv("vector", SC, v2)
        b.op("vector", EV, lambda E, sa=sa, sb_=sb_: E.tensor_mul(sb_, sa, sa))
        v3 = b.op("vector", EV, lambda E, sc_=sc_, sb_=sb_: E.tensor_sub(sc_, sc_, sb_))
        b.waitv("scalar", EV, v3)
        b.op("scalar", SC, lambda E, sb_=sb_, sc_=sc_: E.activation(sb_, sc_, AF.Ln, bias=sb["cst"][:, 0:1]))
        v4 = b.op("scalar", SC, lambda E, sc_=sc_, sb_=sb_: E.activation(sc_, sb_, AF.Exp, scale=sb["cst"][:, 2:3]))
        b.waitv("vector", SC, v4)
        for kc in range(KC_H):
            xs = sb["xhat"][:, kc * (T + 128) + lo: kc * (T + 128) + lo + n]
            stg = sb["ystage"][:, :n]
            b.op("vector", EV, lambda E, xs=xs, stg=stg, sa=sa: E.tensor_sub(stg, xs, sa))
            b.op("vector", EV, lambda E, xs=xs, stg=stg, sc_=sc_: E.tensor_mul(xs, stg, sc_))
    tap("xhat", sb["xhat"][:, :])
    ln1_done = (EV, EV.v)

    bank_tok = {0: ln1_done, 1: ln1_done, 2: None, 3: None, 4: None}
    mm_tok = {}
    wv = {}
    for oc in range(2):
        wv[oc] = b.dma("gpsimd", wt[oc % 2][:, :], dr["w_in"].ap()[oc, :, :, :], "ld")
    stage_tok = None
    dtB = None
    for oc in range(OC_P):
        pA, pB = pb[oc % 2], pb[2 + oc % 2]
        wtok("tensor", wv[oc])
        wtok("tensor", bank_tok[oc % 2])
        wtok("tensor", bank_tok[2 + oc % 2])
        if oc <= OC_DT:
            wtok("tensor", bank_tok[4])
        wtile = wt[oc % 2]
        def mmrun(E, wtile=wtile, pA=pA, pB=pB):
            last = None
            for kc in range(KC_H):
                lw = wtile[:, kc * 128:(kc + 1) * 128]
                xa = sb["xhat"][:, kc * (T + 128) + 128: kc * (T + 128) + 640]
                xb2 = sb["xhat"][:, kc * (T + 128) + 640: kc * (T + 128) + 1152]
                E.matmul(pA[:, :], lw, xa, start=(kc == 0), stop=(kc == KC_H - 1))
                last = E.matmul(pB[:, :], lw, xb2, start=(kc == 0), stop=(kc == KC_H - 1))
            return last
        vmm = b.op("tensor", MM, mmrun)
        if oc <= OC_DT:
            def mmh(E, wtile=wtile):
                last = None
                for kc in range(KC_H):
                    lw = wtile[:, kc * 128:(kc + 1) * 128]
                    xh = sb["xhat"][:, kc * (T + 128) + 125: kc * (T + 128) + 128]
                    last = E.matmul(pb[4][:, :3], lw, xh, start=(kc == 0), stop=(kc == KC_H - 1))
                return last
            vmm = b.op("tensor", MM, mmh)
        mm_tok[oc] = vmm
        if oc + 2 < OC_P:
            b.waitv("gpsimd", MM, mm_tok[oc])
            wv[oc + 2] = b.dma("gpsimd", wt[(oc + 2) % 2][:, :], dr["w_in"].ap()[oc + 2, :, :, :], "ld")
        if oc < OC_CONV:
            b.waitv("vector", MM, vmm)
            tA = b.op("vector", EV, lambda E, oc=oc, pA=pA: E.tensor_copy(
                sb["hbc"][:, oc * 515 + 3: oc * 515 + 515], pA[:, :]))
            tH = b.op("vector", EV, lambda E, oc=oc: E.tensor_copy(
                sb["hbc"][:, oc * 515: oc * 515 + 3], pb[4][:, :3]))
            wtok("vector", stage_tok)
            tB = b.op("vector", EV, lambda E, pB=pB: E.tensor_copy(sb["sq_st"][:, :512], pB[:, :]))
            b.wait("sync", EV)
            vst = b.dma("sync", dr["hbc_d"].ap()[:, oc, 3:515], sb["sq_st"][:, :512], "st")
            stage_tok = vst
            bank_tok[oc % 2] = (EV, tA)
            bank_tok[2 + oc % 2] = (EV, tB)
            bank_tok[4] = (EV, tH)
        elif oc == OC_DT:
            b.waitv("scalar", MM, vmm)
            dtA = b.op("scalar", SC, lambda E, pA=pA: E.copy(sb["dt"][0:64, 0:512], pA[0:64, :]))
            dtB = b.op("scalar", SC, lambda E, pB=pB: E.copy(sb["dt"][0:64, 512:1024], pB[0:64, :]))
            bank_tok[oc % 2] = (SC, dtA)
            bank_tok[2 + oc % 2] = (SC, dtB)
            bank_tok[4] = (SC, dtB)
        else:
            hp = oc - OC_GATE0
            b.waitv("scalar", MM, vmm)
            for (ps, sp) in ((pA, 0), (pB, 1)):
                wtok("scalar", stage_tok)
                tg = b.op("scalar", SC, lambda E, ps=ps: E.activation(
                    sb["sq_st"][:, :512], ps[:, :], AF.Silu))
                b.wait("sync", SC)
                vst = b.dma("sync", dr["gate_d"].ap()[sp, :, hp, :], sb["sq_st"][:, :512], "st")
                stage_tok = vst
                bank_tok[(0 if sp == 0 else 2) + oc % 2] = (SC, tg)
        emit_gjob()
        if oc % 10 == 9:
            emit_gjob()
    ag1_tok = finish_gather()
    tap("hbc", sb["hbc"][:, :])

    b.waitv("scalar", SC, dtB)
    b.waitv("vector", SC, dtB)
    b.op("vector", EV, lambda E: E.tensor_scalar(sb["dt"][:, :], sb["dt"][:, :],
                                                 sb["dt_bias"][:, :], None, OP.add))
    b.op("vector", EV, lambda E: E.tensor_scalar_max(sb["gvec"][:, :], sb["dt"][:, :], 0.0))
    b.op("vector", EV, lambda E: E.tensor_scalar_min(sb["dt"][:, :], sb["dt"][:, :], 0.0))
    b.op("vector", EV, lambda E: E.tensor_sub(sb["dt"][:, :], sb["dt"][:, :], sb["gvec"][:, :]))
    vfix = b.op("vector", EV, lambda E: E.tensor_scalar_max(sb["dt"][:, :], sb["dt"][:, :], -80.0))
    b.waitv("scalar", EV, vfix)
    b.op("scalar", SC, lambda E: E.activation(sb["dt"][:, :], sb["dt"][:, :], AF.Exp))
    vln = b.op("scalar", SC, lambda E: E.activation(sb["dt"][:, :], sb["dt"][:, :], AF.Ln,
                                                    bias=sb["cst"][0:64, 6:7]))
    b.waitv("vector", SC, vln)
    vdt = b.op("vector", EV, lambda E: E.tensor_add(sb["dt"][:, :], sb["dt"][:, :], sb["gvec"][:, :]))
    vdt_ev = vdt
    tap("dt", sb["dt"][:, :])
    b.op("vector", EV, lambda E: E.tensor_scalar(sb["gvec"][:, :], sb["dt"][:, :],
                                                 sb["a_neg"][:, :], None, OP.mult))
    cum_wr = []
    for c in range(NCH):
        g = sb["gvec"][:, c * 128:(c + 1) * 128]
        vscn = b.op("vector", EV, lambda E, g=g, c=c: E.tensor_tensor_scan(
            sb["cum"][:, c * 128:(c + 1) * 128], g, g, 0.0, OP.add, OP.bypass))
        b.waitv("vector", EV, vscn)
        vtc = b.op("vector", EV, lambda E, c=c: E.tensor_copy(
            sb["totals"][:, c:c + 1], sb["cum"][:, c * 128 + 127: c * 128 + 128]))
        if c == NCH - 1:
            b.waitv("vector", EV, vtc)
        b.wait("sync", EV)
        b.dma("sync", fap(dr["lam_d"], [[1, 64], [1, 1]], off=c * 64),
              sb["totals"][:, c:c + 1], "st")
        cum_wr.append(b.dma("sync", dr["cum_d"].ap()[c, :, :], sb["cum"][:, c * 128:(c + 1) * 128], "st"))
    vscn2 = b.op("vector", EV, lambda E: E.tensor_tensor_scan(
        sb["small64"][:, 0:8], sb["totals"][:, :], sb["totals"][:, :], 0.0, OP.add, OP.bypass))
    b.waitv("vector", EV, vscn2)
    b.op("vector", EV, lambda E: E.memset(sb["small64b"][:, 0:1], 0.0))
    b.op("vector", EV, lambda E: E.tensor_copy(sb["small64b"][:, 1:8], sb["small64"][:, 0:7]))
    b.wait("sync", EV)
    for c in range(NCH):
        vpref = b.dma("sync", fap(dr["pref_d"], [[1, 64], [1, 1]], off=c * 64),
                      sb["small64b"][:, c:c + 1], "st")
    vtot = b.dma("sync", dr["tot_d"].ap(), sb["small64"][:, 7:8], "st")
    b.op("vector", EV, lambda E: E.tensor_copy(sb["stage8"][0:64, 512:520], sb["totals"][:, :]))
    tap("cum", sb["cum"][:, :])

    ydc_tok = {}
    snap_tok = {}
    st8 = {}
    vz0 = b.op("vector", EV, lambda E: E.memset(sb["s_state"][:, :], 0.0))
    st8["srec"] = (EV, vz0)
    hbc_done = (EV, EV.v)
    for sp in range(2):
        if sp == 1:
            vsv = b.op("vector", EV, lambda E: E.tensor_copy(
                fap(sb["hbc_sav"], [[OC_CONV * 3, 128], [3, OC_CONV], [1, 3]]),
                fap(sb["hbc"], [[OC_CONV * 515, 128], [515, OC_CONV], [1, 3]], off=512)))
            wtok("sync", stage_tok)
            b.waitv("sync", EV, vsv)
            vrl = b.dma("sync", fap(sb["hbc"], [[OC_CONV * 515, 128], [515, OC_CONV], [1, 512]], off=3),
                        dr["hbc_d"].ap()[:, :, 3:515], "ld")
            wtok("vector", vrl)
            vh = b.op("vector", EV, lambda E: E.tensor_copy(
                fap(sb["hbc"], [[OC_CONV * 515, 128], [515, OC_CONV], [1, 3]]),
                sb["hbc_sav"][:, :]))
            hbc_ready = (EV, vh)
        else:
            hbc_ready = None
        for hs in range(2):
            lo = hs * 256
            cw = {}
            def cdsrc(fc):
                return fap(dr["conv_diag"], [[128, 128], [128 * 128, 4], [1, 128]],
                           off=fc * KCONV * 128 * 128)
            b.waitv("gpsimd", MM, MM.v)
            for fc in range(2):
                cw[fc] = b.dma("gpsimd", wt[fc % 2][:, :512], cdsrc(fc), "ld")
            conv_ev = {}
            for fc in range(OC_CONV):
                pc = pb[4 + fc % 2]
                wtok("tensor", cw[fc])
                wtok("tensor", hbc_ready)
                wtok("tensor", conv_ev.get(fc - 2))
                if sp == 0 and hs == 0 and fc < 2:
                    wtok("tensor", bank_tok[4])
                    wtok("tensor", hbc_done)
                def cmm(E, fc=fc, pc=pc, lo=lo):
                    last = None
                    for k in range(KCONV):
                        lw = wt[fc % 2][:, k * 128:(k + 1) * 128]
                        rhs = sb["hbc"][:, fc * 515 + lo + k: fc * 515 + lo + k + 256]
                        last = E.matmul(pc[:, :256], lw, rhs, start=(k == 0), stop=(k == KCONV - 1))
                    return last
                vc = b.op("tensor", MM, cmm)
                if fc + 2 < OC_CONV:
                    b.waitv("gpsimd", MM, vc)
                    cw[fc + 2] = b.dma("gpsimd", wt[fc % 2][:, :512], cdsrc(fc + 2), "ld")
                b.waitv("scalar", MM, vc)
                bias = sb["conv_b"][:, fc:fc + 1]
                if fc < 32:
                    wtok("scalar", st8.get("xconv_free_ev"))
                    wtok("scalar", st8.get("xconv_free_mm"))
                    ve = b.op("scalar", SC, lambda E, fc=fc, pc=pc, bias=bias: E.activation(
                        sb["x_conv"][:, fc * 256: fc * 256 + 256], pc[:, :256], AF.Silu, bias=bias))
                else:
                    tok0 = sp * 512 + lo
                    b.op("scalar", SC, lambda E, pc=pc, tok0=tok0: E.activation(
                        sb["bmat"][0:64, tok0:tok0 + 256], pc[0:64, :256], AF.Silu,
                        bias=sb["conv_b"][0:64, 32:33]))
                    ve = b.op("scalar", SC, lambda E, pc=pc, tok0=tok0: E.activation(
                        sb["cmat"][0:64, tok0:tok0 + 256], pc[64:128, :256], AF.Silu,
                        bias=sb["conv_b"][64:128, 32:33]))
                conv_ev[fc] = (SC, ve)
            if sp == 0 and hs == 0:
                tap("xconv", sb["x_conv"][:, :])
            conv_done = (SC, SC.v)
            if sp == 0 and hs == 0:
                b.op("vector", EV, lambda E: E.tensor_copy(sb["stage8"][0:64, 520:528], sb["totals"][:, :]))
            for c2 in range(2):
                ch = sp * 4 + hs * 2 + c2
                lo2 = c2 * 128
                emit_chunk_inpass(b, nc, meta, pb, pbt, pb7, ch, sp, lo2, conv_done,
                                  cum_wr, ydc_tok, snap_tok, wtok, tap, st8)
            st8["xconv_free_ev"] = (EV, EV.v)
            st8["xconv_free_mm"] = (MM, MM.v)
    meta["_mix"] = dict(ydc_tok=ydc_tok, snap_tok=snap_tok, vpref=vpref, vtot=vtot,
                        ag1_tok=ag1_tok)

YBANKS = [0, 1, 2, 3, 4, 5, 7, 0]

def emit_chunk_inpass(b, nc, meta, pb, pbt, pb7, ch, sp, lo2, conv_done,
                      cum_wr, ydc_tok, snap_tok, wtok, tap, st8):
    dr, sb, S = meta["dr"], meta["sb"], meta["sems"]
    MM, EV, SC, CC = S["mm"], S["ev"], S["sc"], S["cc"]
    tok0 = ch * 128

    prev = st8.get("xtok_free")
    for w in range(4):
        wtok("tensor", conv_done if w == 0 else None)
        wtok("tensor", st8.get("pbt_free"))
        vtr = b.op("tensor", MM, lambda E, w=w: [
            E.transpose(pbt[:, i * 128:(i + 1) * 128],
                        sb["x_conv"][:, (8 * w + i) * 256 + lo2:(8 * w + i) * 256 + lo2 + 128],
                        sb["ident_bf"][:, :]) for i in range(8)][-1])
        b.waitv("vector", MM, vtr)
        wtok("vector", prev)
        ve = b.op("vector", EV, lambda E, w=w: E.tensor_copy(
            sb["x_tok"][:, w * 1024:(w + 1) * 1024], pbt[:, :1024]))
        st8["pbt_free"] = (EV, ve)
    wtok("tensor", st8.get("pbt_free"))
    vtr = b.op("tensor", MM, lambda E: E.transpose(
        pbt[:, 0:64], sb["bmat"][0:64, tok0:tok0 + 128], sb["ident_bf"][0:64, 0:64]))
    b.waitv("vector", MM, vtr)
    vbt = b.op("vector", EV, lambda E: E.tensor_copy(sb["b_tok"][:, :], pbt[:, 0:64]))
    st8["pbt_free"] = (EV, vbt)
    wtok("tensor", st8.get("pb7_free"))
    vt2 = b.op("tensor", MM, lambda E: [
        E.transpose(pb7[:, 0:64], sb["cum"][0:64, tok0:tok0 + 128], sb["ident_f"][0:64, 0:64]),
        E.transpose(pb7[:, 64:128], sb["dt"][0:64, tok0:tok0 + 128], sb["ident_f"][0:64, 0:64])][-1])
    b.waitv("vector", MM, vt2)
    b.op("vector", EV, lambda E: E.tensor_copy(sb["cum_t"][:, :], pb7[:, 0:64]))
    vdt2 = b.op("vector", EV, lambda E: E.tensor_copy(sb["dt_tok"][:, :], pb7[:, 64:128]))
    b.op("vector", EV, lambda E: E.tensor_scalar(
        sb["dec_fm"][:, :], sb["cum"][0:64, tok0:tok0 + 128],
        sb["totals"][:, ch:ch + 1], None, OP.subtract))
    vdf = b.op("vector", EV, lambda E: E.tensor_scalar_min(
        sb["dec_fm"][:, :], sb["dec_fm"][:, :], 80.0))
    if ch == 0:
        b.op("vector", EV, lambda E: E.tensor_copy(sb["stage8"][0:64, 0:128], sb["dec_fm"][:, :]))
        b.op("vector", EV, lambda E: E.tensor_copy(sb["stage8"][0:64, 128:136], sb["totals"][:, :]))
        tap("decraw", sb["stage8"][0:64, 0:528])
    b.waitv("scalar", EV, vdf)
    vde = b.op("scalar", SC, lambda E: E.activation(sb["dec_fm"][:, :], sb["dec_fm"][:, :],
                                                    AF.Exp, scale=sb["cst"][0:64, 3:4]))
    b.waitv("tensor", SC, vde)
    b.waitv("tensor", EV, vdt2)
    vt3 = b.op("tensor", MM, lambda E: E.transpose(
        pb7[:, 128:192], sb["dec_fm"][0:64, :], sb["ident_f"][0:64, 0:64]))
    b.waitv("vector", MM, vt3)
    vdk = b.op("vector", EV, lambda E: E.tensor_copy(sb["dec_tok"][:, :], pb7[:, 128:192]))
    st8["pb7_free"] = (EV, vdk)

    b.waitv("vector", EV, vdt2)
    vdx = b.op("vector", EV, lambda E: E.tensor_tensor(
        sb["dtx"][:, :], sb["x_tok"][:, :],
        fap(sb["dt_tok"], [[64, 128], [1, 64], [0, 64]]), OP.mult))

    wtok("tensor", st8.get("pb7_free"))
    vg = b.op("tensor", MM, lambda E: E.matmul(
        pb7[:, 256:384], sb["bmat"][0:64, tok0:tok0 + 128],
        sb["cmat"][0:64, tok0:tok0 + 128], start=True, stop=True))
    b.waitv("vector", MM, vg)
    vgt = b.op("vector", EV, lambda E: E.tensor_tensor(
        sb["gtric"][:, :], pb7[:, 256:384], sb["tri_bf"][:, :], OP.mult))
    st8["pb7_free"] = (EV, vgt)

    def cum_load(sg2):
        ch2, q2 = divmod(sg2, 4)
        buf2 = sb["cum_rep"] if sg2 % 2 == 0 else sb["cum_repb"]
        wtok("scalar", cum_wr[ch2])
        wtok("scalar", st8.get(f"cumrep_free{sg2 % 2}"))
        if sg2 == 0:
            b.waitv("scalar", EV, S["ev"].v)
        st8[f"crld{sg2}"] = b.dma("scalar", buf2[:, :],
                    fap(dr["cum_d"], [[0, 128], [128, 16], [1, 128]],
                        off=ch2 * 64 * 128 + q2 * 16 * 128), "ld")
    if "crs" not in st8:
        st8["crs"] = 0
    while st8["crs"] <= min(ch * 4 + 1, 31):
        cum_load(st8["crs"]); st8["crs"] += 1
    ydone = {}
    for q in range(4):
        sg = ch * 4 + q
        crb = sb["cum_rep"] if sg % 2 == 0 else sb["cum_repb"]
        gmbuf = sb["gm"] if sg % 2 == 0 else sb["gmb"]
        wtok("vector", st8[f"crld{sg}"])
        b.waitv("vector", EV, vgt if q == 0 else EV.v)
        vd1 = b.op("vector", EV, lambda E, q=q, crb=crb: E.tensor_tensor(
            crb[:, :], crb[:, :],
            fap(sb["cum_t"], [[64, 128], [1, 16], [0, 128]], off=q * 16), OP.subtract))
        vd2 = b.op("vector", EV, lambda E, crb=crb: E.tensor_scalar(
            crb[:, :], crb[:, :], 0.0, -80.0, OP.min, OP.max))
        b.waitv("scalar", EV, vd2)
        wtok("scalar", st8.get(f"gm_free{sg % 2}"))
        if ch == 0 and q <= 1:
            b.waitv("scalar", EV, S["ev"].v)
        vex = b.op("scalar", SC, lambda E, crb=crb, gmbuf=gmbuf: E.activation(
            gmbuf[:, :], crb[:, :], AF.Exp))
        st8[f"cumrep_free{sg % 2}"] = (SC, vex)
        while st8["crs"] <= min(sg + 2, 31):
            cum_load(st8["crs"]); st8["crs"] += 1
        b.waitv("vector", SC, vex)
        vgm = b.op("vector", EV, lambda E, gmbuf=gmbuf: E.tensor_tensor(
            gmbuf[:, :], gmbuf[:, :],
            fap(sb["gtric"], [[128, 128], [0, 16], [1, 128]]), OP.mult))
        b.waitv("tensor", EV, vgm)
        if q == 0:
            b.waitv("tensor", EV, vdx)
        for gl in (2 * q, 2 * q + 1):
            pbank = pb[YBANKS[gl]] if YBANKS[gl] != 7 else pb7
            wtok("tensor", st8.get(f"yb{YBANKS[gl]}"))
            if gl == 6:
                wtok("tensor", st8.get("pb7_free"))
            if YBANKS[gl] == 5:
                wtok("tensor", st8.get("pb5_rec"))
            if gl == 7:
                wtok("tensor", ydone.get(0))
            def ymm(E, gl=gl, q=q, pbank=pbank, gmbuf=gmbuf):
                last = None
                for hh in range(8):
                    h = gl * 8 + hh
                    half = h % 2
                    pr = (h // 2) % 4
                    last = E.matmul(pbank[half * 64:(half + 1) * 64, pr * 128:(pr + 1) * 128],
                                    sb["dtx"][:, h * 64:(h + 1) * 64],
                                    gmbuf[:, (h - 16 * q) * 128:(h - 16 * q + 1) * 128],
                                    start=(hh < 2), stop=(hh >= 6), skip_group_check=True)
                return last
            vy = b.op("tensor", MM, ymm)
            b.waitv("vector", MM, vy)
            wtok("vector", st8.get("ystage_free"))
            def yev(E, gl=gl, pbank=pbank):
                last = None
                for pr in range(4):
                    hp = gl * 4 + pr
                    last = E.scalar_tensor_tensor(
                        sb["ystage"][:, pr * 128:(pr + 1) * 128],
                        sb["x_conv"][:, hp * 256 + lo2: hp * 256 + lo2 + 128],
                        sb["d_chan"][:, hp:hp + 1],
                        pbank[:, pr * 128:(pr + 1) * 128], OP.mult, OP.add)
                return last
            vye = b.op("vector", EV, yev)
            st8[f"yb{YBANKS[gl]}"] = (EV, vye)
            ydone[gl] = (EV, vye)
            b.wait("sync", EV)
            vst = b.dma("sync", dr["y_d"].ap()[ch, gl, :, :], sb["ystage"][:, :], "st")
            st8["ystage_free"] = vst
            ydc_tok[(ch, gl)] = vst
        st8[f"gm_free{sg % 2}"] = (MM, MM.v)
    if ch == 0:
        tap("ydc", dr["y_d"].ap()[0, :, :, :])

    b.waitv("vector", MM, MM.v)
    vdd = b.op("vector", EV, lambda E: E.tensor_tensor(
        sb["dtx"][:, :], sb["dtx"][:, :],
        fap(sb["dec_tok"], [[64, 128], [1, 64], [0, 64]]), OP.mult))
    st8["xtok_free"] = (EV, vdd)

    wtok("gpsimd", st8.get("srec"))
    vsn = b.dma("gpsimd", dr["snap_d"].ap()[ch, :, :], sb["s_state"][:, :], "st")
    snap_tok[ch] = vsn

    wtok("scalar", st8.get("lam_used"))
    wtok("scalar", cum_wr[ch])
    vlr = b.dma("scalar", sb["lam_rep"][:, :],
                fap(dr["lam_d"], [[0, 64], [1, 64]], off=ch * 64), "ld")
    wtok("scalar", vlr)
    wtok("vector", vlr)
    vcl = b.op("vector", EV, lambda E: E.tensor_scalar_max(sb["lam_rep"][:, :], sb["lam_rep"][:, :], -80.0))
    wtok("scalar", vsn)
    b.waitv("scalar", EV, vcl)
    vle = b.op("scalar", SC, lambda E: E.activation(sb["lam_rep"][:, :], sb["lam_rep"][:, :], AF.Exp))

    b.waitv("tensor", EV, vdd)
    last_rec = None
    for g8 in range(8):
        wtok("tensor", st8.get("pb5_rec"))
        if g8 == 0:
            wtok("tensor", st8.get("yb5"))
        def dsmm(E, g8=g8):
            last = None
            for pr in range(4):
                pair = g8 * 4 + pr
                last = E.matmul(pb[5][0:64, pr * 128:(pr + 1) * 128],
                                sb["b_tok"][:, :], sb["dtx"][:, pair * 128:(pair + 1) * 128],
                                start=(pr == 0), stop=(pr == 3))
            return last
        vds = b.op("tensor", MM, dsmm)
        b.waitv("vector", MM, vds)
        b.waitv("vector", SC, vle)
        sl = sb["s_state"][:, g8 * 512:(g8 + 1) * 512]
        b.op("vector", EV, lambda E, sl=sl, g8=g8: E.tensor_tensor(
            sb["s_tmp"][:, :], sl,
            fap(sb["lam_rep"], [[64, 64], [2, 4], [1, 2], [0, 64]], off=8 * g8), OP.mult))
        last_rec = b.op("vector", EV, lambda E, sl=sl: E.tensor_tensor(
            sl, sb["s_tmp"][:, :], pb[5][0:64, :], OP.add))
        st8["pb5_rec"] = (EV, last_rec)
    st8["srec"] = (EV, last_rec)
    st8["lam_used"] = (EV, last_rec)
    if ch == 0:
        tap("dtxd0", sb["dtx"][:, :])
        tap("lam0", sb["lam_rep"][:, :])
        tap("dec0", sb["dec_tok"][:, :])
        tap("sstate0", sb["s_state"][:, :])
        tap("btok0", sb["b_tok"][:, :])
    if ch == 1:
        tap("sstate1", sb["s_state"][:, :])
    if ch in (2, 3, 4, 5, 6, 7):
        tap(f"sstate{ch}", sb["s_state"][:, :])
    if ch == 2:
        tap("dtxd2", sb["dtx"][:, :])
        tap("dec2", sb["dec_tok"][:, :])
        tap("btok2", sb["b_tok"][:, :])

def emit_post(b, nc, meta, pb, pbt, pb7):
    dr, sb, taps, comm = meta["dr"], meta["sb"], meta["taps"], meta["comm"]
    S = meta["sems"]
    MM, EV, SC, CC = S["mm"], S["ev"], S["sc"], S["cc"]
    mix = meta["_mix"]

    def wtok(eng, tok):
        if tok is not None:
            b.waitv(eng, tok[0], tok[0].v if getattr(tok[0], "q", False) else tok[1])

    def tap(name, src):
        if name in taps:
            b.waitv("gpsimd", S["ev"], S["ev"].v)
            b.waitv("gpsimd", S["sc"], S["sc"].v)
            b.waitv("gpsimd", S["mm"], S["mm"].v)
            b.dma("gpsimd", taps[name].ap(), src, "st")

    tap("bmat", sb["bmat"][:, :])
    tap("cmat", sb["cmat"][:, :])
    tap("canary", sb["canary"][:, :])
    vsf = b.dma("sync", dr["ag2_in"].ap()[:, 0:4096], sb["s_state"][:, :], "st")
    wtok("scalar", mix["vtot"])
    b.waitv("scalar", EV, EV.v)
    vtr = b.dma("scalar", sb["lam_rep"][:, :], fap(dr["tot_d"], [[0, 64], [1, 64]]), "ld")
    wtok("sync", vtr)
    vtf = b.dma("sync", dr["ag2_in"].ap()[:, 4096:4160], sb["lam_rep"][:, :], "st")
    wtok("gpsimd", vsf)
    wtok("gpsimd", vtf)
    if comm:
        b.op("gpsimd", CC, lambda E: E.collective_compute(
            "AllGather", OP.bypass, replica_groups=[list(range(NCORE))],
            ins=[dr["ag2_in"].ap().opt()], outs=[dr["ag2_out"].ap().opt()]))
        ag2_tok = (CC, CC.v)
    else:
        vv = None
        for r in range(NCORE):
            vv = b.dma("gpsimd", dr["ag2_out"].ap()[r * 64:(r + 1) * 64, :], dr["ag2_in"].ap()[:, :], "st")
        ag2_tok = vv
    tap("sstate", sb["s_state"][:, :])

    wtok("sync", ag2_tok)
    vgv = b.dma("sync", sb["coef"][:, :],
                fap(dr["ag2_out"], [[64 * 4160, 8], [1, 64]], off=4096), "ld")
    wtok("tensor", vgv)
    vcm = b.op("tensor", MM, lambda E: E.matmul(pb[0][0:8, 0:64], sb["emask"][:, :],
                                                sb["coef"][:, :], start=True, stop=True))
    b.waitv("scalar", MM, vcm)
    vcc0 = b.op("scalar", SC, lambda E: E.copy(sb["coef"][:, :], pb[0][0:8, 0:64]))
    b.waitv("vector", SC, vcc0)
    vcc1 = b.op("vector", EV, lambda E: E.tensor_scalar_max(sb["coef"][:, :], sb["coef"][:, :], -80.0))
    b.waitv("scalar", EV, vcc1)
    vce = b.op("scalar", SC, lambda E: E.activation(sb["coef"][:, :], sb["coef"][:, :], AF.Exp))
    b.waitv("vector", SC, vce)
    vcm2 = b.op("vector", EV, lambda E: E.tensor_scalar(sb["coef"][:, :], sb["coef"][:, :],
                                                        sb["mmask"][:, :], None, OP.mult))
    b.wait("sync", EV)
    vcd = b.dma("sync", dr["coef_d"].ap(), sb["coef"][:, :], "st")
    wtok("scalar", vcd)
    vcr = b.dma("scalar", sb["coef_rep"][:, :], fap(dr["coef_d"], [[0, 64], [1, 512]]), "ld")

    vz = b.op("vector", EV, lambda E: E.memset(sb["carry"][:, :], 0.0))
    wtok("vector", vcr)
    fin_tok = None
    for j in range(NCORE):
        wtok("sync", fin_tok)
        vfj = b.dma("sync", sb["fin_b"][:, :], dr["ag2_out"].ap()[j * 64:(j + 1) * 64, 0:4096], "ld")
        wtok("vector", vfj)
        b.op("vector", EV, lambda E, j=j: E.tensor_tensor(
            sb["fin_b"][:, :], sb["fin_b"][:, :],
            fap(sb["coef_rep"], [[512, 64], [2, 32], [1, 2], [0, 64]], off=64 * j), OP.mult))
        vca = b.op("vector", EV, lambda E: E.tensor_add(sb["carry"][:, :], sb["carry"][:, :],
                                                        sb["fin_b"][:, :]))
        fin_tok = (EV, vca)
    tap("carry", sb["carry"][:, :])

    st8 = {}
    gate_tok = None
    ydc_tok, snap_tok = mix["ydc_tok"], mix["snap_tok"]
    for ch in range(NCH):
        sp, tok0 = ch // 4, ch * 128
        if ch % 4 == 0:
            wtok("sync", gate_tok)
            vgl = b.dma("sync", sb["gate_sp"][:, :], dr["gate_d"].ap()[sp, :, :, :], "ld")
            st8["gate_ld"] = vgl
        wtok("sync", st8.get("yld_free"))
        vyl = b.dma("sync", sb["y_ld"][:, :],
                    fap(dr["y_d"], [[512, 128], [128 * 512, 8], [1, 512]], off=ch * 8 * 128 * 512), "ld")
        wtok("sync", st8.get("snap_free"))
        wtok("sync", snap_tok[ch])
        vsl = b.dma("sync", sb["snap_ld"][:, :], dr["snap_d"].ap()[ch, :, :], "ld")
        wtok("scalar", mix["vpref"])
        wtok("scalar", st8.get("prep_used"))
        vpl = b.dma("scalar", sb["p_rep"][:, :], fap(dr["pref_d"], [[0, 64], [1, 64]], off=ch * 64), "ld")
        wtok("vector", vpl)
        vclp = b.op("vector", EV, lambda E: E.tensor_scalar_max(sb["p_rep"][:, :], sb["p_rep"][:, :], -80.0))
        wtok("scalar", vpl)
        b.waitv("scalar", EV, vclp)
        vpe = b.op("scalar", SC, lambda E: E.activation(sb["p_rep"][:, :], sb["p_rep"][:, :], AF.Exp))
        b.waitv("vector", SC, vpe)
        wtok("vector", vsl)
        wtok("vector", fin_tok)
        wtok("vector", st8.get("trueS_used"))
        b.op("vector", EV, lambda E: E.tensor_tensor(
            sb["trueS"][:, :], sb["carry"][:, :],
            fap(sb["p_rep"], [[64, 64], [2, 32], [1, 2], [0, 64]]), OP.mult))
        vts = b.op("vector", EV, lambda E: E.tensor_add(sb["trueS"][:, :], sb["trueS"][:, :],
                                                        sb["snap_ld"][:, :]))
        st8["snap_free"] = (EV, vts)
        st8["prep_used"] = (EV, vts)
        def cr2_load(sg2):
            ch2, q2 = divmod(sg2, 4)
            buf2 = sb["cum_rep2"] if sg2 % 2 == 0 else sb["cum_rep2b"]
            wtok("scalar", st8.get(f"cr2_free{sg2 % 2}"))
            if sg2 == 0:
                b.waitv("scalar", MM, MM.v)
                b.waitv("scalar", EV, S["ev"].v)
            st8[f"cr2ld{sg2}"] = b.dma("scalar", buf2[:, :],
                         fap(dr["cum_d"], [[0, 128], [128, 16], [1, 128]],
                             off=ch2 * 64 * 128 + q2 * 16 * 128), "ld")
        if "c2s" not in st8:
            st8["c2s"] = 0
        while st8["c2s"] <= min(ch * 4 + 1, 31):
            cr2_load(st8["c2s"]); st8["c2s"] += 1
        wtok("vector", vyl)
        for q in range(4):
            sg = ch * 4 + q
            crb2 = sb["cum_rep2"] if sg % 2 == 0 else sb["cum_rep2b"]
            exb = sb["expc"]
            wtok("vector", st8[f"cr2ld{sg}"])
            vclq = b.op("vector", EV, lambda E, crb2=crb2: E.tensor_scalar_max(
                crb2[0:64, :], crb2[0:64, :], -80.0))
            wtok("scalar", st8.get("expc_used"))
            b.waitv("scalar", EV, vclq)
            vex = b.op("scalar", SC, lambda E, crb2=crb2, exb=exb: E.activation(
                exb[:, :], crb2[0:64, :], AF.Exp))
            st8[f"cr2_free{sg % 2}"] = (SC, vex)
            while st8["c2s"] <= min(sg + 2, 31):
                cr2_load(st8["c2s"]); st8["c2s"] += 1
            b.waitv("vector", SC, vex)
            wtok("vector", st8.get("ctil_used"))
            vct = b.op("vector", EV, lambda E, tok0=tok0, exb=exb: E.tensor_tensor(
                sb["ctil"][:, :],
                fap(sb["cmat"], [[T, 64], [0, 16], [1, 128]], off=tok0),
                exb[:, :], OP.mult))
            b.waitv("tensor", EV, vct)
            if q == 0:
                b.waitv("tensor", EV, vts)
            for gl in (2 * q, 2 * q + 1):
                pbank = pb[YBANKS[gl]] if YBANKS[gl] != 7 else pb7
                wtok("tensor", st8.get(f"pyb{YBANKS[gl]}"))
                if gl == 7:
                    wtok("tensor", st8.get("pyb0"))
                def ymm2(E, gl=gl, q=q, pbank=pbank):
                    last = None
                    for hh in range(8):
                        h = gl * 8 + hh
                        half, pr = h % 2, (h // 2) % 4
                        last = E.matmul(pbank[half * 64:(half + 1) * 64, pr * 128:(pr + 1) * 128],
                                        sb["trueS"][0:64, h * 64:(h + 1) * 64],
                                        sb["ctil"][0:64, (h - 16 * q) * 128:(h - 16 * q + 1) * 128],
                                        start=(hh < 2), stop=(hh >= 6), skip_group_check=True)
                    return last
                vy2 = b.op("tensor", MM, ymm2)
                b.waitv("vector", MM, vy2)
                def yadd(E, gl=gl, pbank=pbank):
                    last = None
                    for pr in range(4):
                        hp = gl * 4 + pr
                        sl = sb["y_ld"][:, hp * 128:(hp + 1) * 128]
                        last = E.tensor_tensor(sl, sl, pbank[:, pr * 128:(pr + 1) * 128], OP.add)
                    return last
                vya = b.op("vector", EV, yadd)
                st8[f"pyb{YBANKS[gl]}"] = (EV, vya)
            st8["expc_used"] = (EV, EV.v)
            st8["ctil_used"] = (MM, MM.v)
        st8["trueS_used"] = (MM, MM.v)
        if ch == 0:
            tap("yld", sb["y_ld"][:, :])
        wtok("scalar", st8.get("yld_free"))
        sqv = []
        for hp in range(KC_I):
            sq = sb["sq_st"][:, (hp % 2) * 516:(hp % 2) * 516 + 128]
            if hp >= 2:
                b.waitv("scalar", MM, sqv[hp - 2])
            vsq = b.op("scalar", SC, lambda E, hp=hp, sq=sq: E.activation(
                sq, sb["y_ld"][:, hp * 128:(hp + 1) * 128], AF.Square))
            b.waitv("tensor", SC, vsq)
            if hp == 0:
                wtok("tensor", st8.get("pyb1"))
            vms = b.op("tensor", MM, lambda E, hp=hp, sq=sq: E.matmul(
                pb[1][:, 0:128], sb["ones_bf"][:, :], sq, start=(hp == 0), stop=(hp == KC_I - 1)))
            sqv.append(vms)
        b.waitv("scalar", MM, sqv[-1])
        v1 = b.op("scalar", SC, lambda E: E.activation(sb["stat_a"][:, 0:128], pb[1][:, 0:128],
                                                       AF.Copy, scale=sb["cst"][:, 5:6]))
        b.op("scalar", SC, lambda E: E.activation(sb["stat_b"][:, 0:128], sb["stat_a"][:, 0:128],
                                                  AF.Ln, bias=sb["cst"][:, 1:2]))
        vr = b.op("scalar", SC, lambda E: E.activation(sb["stat_c"][:, 0:128], sb["stat_b"][:, 0:128],
                                                       AF.Exp, scale=sb["cst"][:, 2:3]))
        st8["pyb1"] = (SC, v1)
        b.waitv("vector", SC, vr)
        wtok("vector", st8["gate_ld"])
        if ch == 0:
            tap("gatesp", sb["gate_sp"][:, :])
            tap("rrms", sb["stat_c"][:, 0:128])
        vlast = None
        for hp in range(KC_I):
            b.op("vector", EV, lambda E, hp=hp: E.scalar_tensor_tensor(
                sb["ystage"][:, 0:128], sb["y_ld"][:, hp * 128:(hp + 1) * 128],
                sb["norm_w"][:, hp:hp + 1], sb["stat_c"][:, 0:128], OP.mult, OP.mult))
            vlast = b.op("vector", EV, lambda E, hp=hp, ch=ch, sp=sp: E.tensor_tensor(
                sb["y_g"][:, hp * T + ch * 128: hp * T + (ch + 1) * 128],
                sb["ystage"][:, 0:128],
                sb["gate_sp"][:, hp * 512 + (ch % 4) * 128: hp * 512 + (ch % 4) * 128 + 128],
                OP.mult))
        st8["yld_free"] = (EV, vlast)
        if ch == 3:
            gate_tok = (EV, vlast)
    tap("y_g", sb["y_g"][:, :])
    meta["_post"] = dict(ygdone=(S["ev"], S["ev"].v))

def emit_tail(b, nc, meta, pb, pbt, pb7):
    dr, sb, taps, comm = meta["dr"], meta["sb"], meta["taps"], meta["comm"]
    S = meta["sems"]
    MM, EV, SC, CC = S["mm"], S["ev"], S["sc"], S["cc"]
    def wtok(eng, tok):
        if tok is not None:
            b.waitv(eng, tok[0], tok[0].v if getattr(tok[0], "q", False) else tok[1])
    def tap(name, src):
        if name in taps:
            b.waitv("gpsimd", S["ev"], S["ev"].v)
            b.waitv("gpsimd", S["sc"], S["sc"].v)
            b.waitv("gpsimd", S["mm"], S["mm"].v)
            b.dma("gpsimd", taps[name].ap(), src, "st")
    wt = [sb["wq"], sb["wq2"]]
    ygdone = meta["_post"]["ygdone"]

    wv = {}
    b.waitv("gpsimd", MM, MM.v)
    for h in range(2):
        wv[(0, h)] = b.dma("gpsimd", wt[h][:, :], dr["w_out"].ap()[0, :, h * 16:(h + 1) * 16, :], "ld")
    bank_tok = {}
    mo_ev = None
    for oc in range(16):
        pN = (pb[2], pb[3]) if oc % 2 == 0 else (pb[4], pb[5])
        wtok("tensor", wv[(oc, 1)])
        if oc == 0:
            wtok("tensor", ygdone)
        wtok("tensor", bank_tok.get(oc % 2))
        def omm(E, pN=pN):
            last = None
            for kc in range(KC_I):
                lw = wt[kc // 16][:, (kc % 16) * 128:(kc % 16 + 1) * 128]
                for nt in range(2):
                    last = E.matmul(pN[nt][:, :], lw,
                                    sb["y_g"][:, kc * T + nt * 512: kc * T + nt * 512 + 512],
                                    start=(kc == 0), stop=(kc == KC_I - 1))
            return last
        vo = b.op("tensor", MM, omm)
        if oc + 1 < 16:
            b.waitv("gpsimd", MM, vo)
            for h in range(2):
                wv[(oc + 1, h)] = b.dma("gpsimd", wt[h][:, :],
                                        dr["w_out"].ap()[oc + 1, :, h * 16:(h + 1) * 16, :], "ld")
        b.waitv("scalar", MM, vo)
        if oc == 0:
            wtok("scalar", ygdone)
        for nt in range(2):
            mo_ev = b.op("scalar", SC, lambda E, oc=oc, nt=nt, pN=pN: E.copy(
                sb["mo_bf"][:, oc * T + nt * 512: oc * T + nt * 512 + 512], pN[nt][:, :]))
        bank_tok[oc % 2] = (SC, mo_ev)
    tap("mo", sb["mo_bf"][:, :])

    def ln_stats(src_getter, nseg, kcn, dsize, pm, pq, csteps, cast_src=False):
        cstsc = sb["cst"][:, 4:5]
        """generic feature-major LN stats: returns per-seg tokens; stats in
        stat_a (mean) and stat_c (r), segment-sliced."""
        toks = []
        for si in range(nseg):
            lo = si * 512
            sqv = []
            for kc in range(kcn):
                xs = src_getter(kc, lo)
                sq = sb["sq_st"][:, (kc % 2) * 516:(kc % 2) * 516 + 512]
                if kc >= 2:
                    b.waitv("scalar", MM, sqv[kc - 2])
                if cast_src:
                    if kc >= 1:
                        b.waitv("scalar", MM, sqv[kc - 1])
                    b.op("scalar", SC, lambda E, xs=xs: E.activation(sb["ystage"][:, 0:512], xs, AF.Copy))
                    xs = sb["ystage"][:, 0:512]
                vsc = b.op("scalar", SC, lambda E, xs=xs, sq=sq: E.activation(sq, xs, AF.Square))
                b.waitv("tensor", SC, vsc)
                vm = b.op("tensor", MM, lambda E, xs=xs, sq=sq, kc=kc:
                          [E.matmul(pm[:, 0:512], sb["ones_bf"][:, :], xs,
                                    start=(kc == 0), stop=(kc == kcn - 1)),
                           E.matmul(pq[:, 0:512], sb["ones_bf"][:, :], sq,
                                    start=(kc == 0), stop=(kc == kcn - 1))][-1])
                sqv.append(vm)
            sa = sb["stat_a"][:, lo:lo + 512]; sb_ = sb["stat_b"][:, 0:512]
            sc_ = sb["stat_c"][:, lo:lo + 512]
            b.waitv("scalar", MM, sqv[-1])
            b.op("scalar", SC, lambda E, sa=sa: E.activation(sa, pm[:, 0:512], AF.Copy, scale=cstsc))
            v2 = b.op("scalar", SC, lambda E, sc_=sc_: E.activation(sc_, pq[:, 0:512], AF.Copy, scale=cstsc))
            b.waitv("vector", SC, v2)
            b.op("vector", EV, lambda E, sa=sa, sb_=sb_: E.tensor_mul(sb_, sa, sa))
            v3 = b.op("vector", EV, lambda E, sc_=sc_, sb_=sb_: E.tensor_sub(sc_, sc_, sb_))
            b.waitv("scalar", EV, v3)
            b.op("scalar", SC, lambda E, sb_=sb_, sc_=sc_: E.activation(sb_, sc_, AF.Ln, bias=csteps))
            v4 = b.op("scalar", SC, lambda E, sc_=sc_, sb_=sb_: E.activation(sc_, sb_, AF.Exp, scale=sb["cst"][:, 2:3]))
            toks.append(v4)
        return toks
    wtok("scalar", bank_tok.get(0))
    wtok("scalar", bank_tok.get(1))
    wtok("tensor", bank_tok.get(0))
    wtok("tensor", bank_tok.get(1))
    ln2 = ln_stats(lambda kc, lo: sb["mo_bf"][:, kc * T + lo: kc * T + lo + 512],
                   2, KC_H, HID, pb[0], pb[1], sb["cst"][:, 0:1])
    xv = {}
    x2_tr = {}
    for kc in range(KC_H):
        wtok("sync", x2_tr.get(kc - 1))
        if kc < 2:
            wtok("sync", ygdone)
            b.waitv("sync", MM, MM.v)
        xv[kc] = b.dma("sync", sb["xstage"][:, :],
                       fap(dr["x_fm"], [[T + 128, 128], [1, T]], off=kc * 128 * (T + 128) + 128), "ld")
        wtok("vector", xv[kc])
        if kc == 0:
            b.waitv("vector", SC, ln2[-1])
        vlast = None
        for si in range(2):
            lo = si * 512
            mo = sb["mo_bf"][:, kc * T + lo: kc * T + lo + 512]
            b.op("vector", EV, lambda E, mo=mo, lo=lo: E.tensor_sub(
                sb["y_tmp"][:, :], mo, sb["stat_a"][:, lo:lo + 512]))
            b.op("vector", EV, lambda E, lo=lo: E.tensor_mul(
                sb["y_tmp"][:, :], sb["y_tmp"][:, :], sb["stat_c"][:, lo:lo + 512]))
            vlast = b.op("vector", EV, lambda E, kc=kc, lo=lo: E.tensor_add(
                sb["x2_fm"][:, kc * T + lo: kc * T + lo + 512],
                sb["xstage"][:, lo:lo + 512], sb["y_tmp"][:, :]))
        x2_tr[kc] = (EV, vlast)
    tap("x2", sb["x2_fm"][:, :])
    xtd = None
    for kc in range(KC_H):
        for wv2 in range(2):
            wtok("tensor", x2_tr[kc])
            wtok("tensor", xtd)
            vt = b.op("tensor", MM, lambda E, kc=kc, wv2=wv2: [
                E.transpose(pb7[:, i * 128:(i + 1) * 128],
                            sb["x2_fm"][:, kc * T + (wv2 * 4 + i) * 128: kc * T + (wv2 * 4 + i + 1) * 128],
                            sb["ident_f"][:, :]) for i in range(4)][-1])
            b.waitv("vector", MM, vt)
            ve = b.op("vector", EV, lambda E: E.tensor_copy(sb["fstage2"][:, 0:512], pb7[:, :]))
            b.wait("sync", EV)
            vst = b.dma("sync",
                        fap(dr["x2_d"], [[HID, 128], [128 * HID, 4], [1, 128]],
                            off=wv2 * 4 * 128 * HID + kc * 128),
                        sb["fstage2"][:, 0:512], "st")
            xtd = vst
    x2d_tok = xtd
    if "x2d" in taps:
        wtok("gpsimd", x2d_tok)
        b.dma("gpsimd", taps["x2d"].ap(), dr["x2_d"].ap()[:, :, :], "st")
    b.waitv("scalar", EV, x2_tr[KC_H - 1][1])
    b.waitv("tensor", EV, x2_tr[KC_H - 1][1])
    sm_g = b.qsems[("gpsimd", "st")]
    b.waitv("vector", sm_g, sm_g.v)
    ln3 = ln_stats(lambda kc, lo: sb["x2_fm"][:, kc * T + lo: kc * T + lo + 512],
                   2, KC_H, HID, pb[0], pb[1], sb["cst"][:, 0:1], cast_src=True)
    b.waitv("vector", SC, ln3[-1])
    b.waitv("vector", MM, MM.v)
    for kc in range(KC_H):
        for si in range(2):
            lo = si * 512
            x2s = sb["x2_fm"][:, kc * T + lo: kc * T + lo + 512]
            b.op("vector", EV, lambda E, x2s=x2s, lo=lo: E.tensor_sub(
                sb["y_tmp"][:, :], x2s, sb["stat_a"][:, lo:lo + 512]))
            b.op("vector", EV, lambda E, kc=kc, lo=lo: E.tensor_mul(
                sb["xhat2"][:, kc * T + lo: kc * T + lo + 512],
                sb["y_tmp"][:, :], sb["stat_c"][:, lo:lo + 512]))
    tap("xhat2", sb["xhat2"][:, :])
    xh2_tok = (EV, S["ev"].v)

    wt4 = [sb["wq"], sb["wq2"], sb["wq3"], sb["wq4"]]
    ag1 = meta["_mix"]["ag1_tok"]
    wtok("vector", x2d_tok)
    halves = [(0, 22), (22, 21)]
    acc_tok = None
    for hi, (k0, nk) in enumerate(halves):
        guv = {}
        wtok("sync", ag1)
        b.waitv("sync", MM, MM.v)
        guv[0] = (b.dma("sync", wt4[0][:, :], dr["ag1g_out"].ap()[k0, :, :, :], "ld"),
                  b.dma("sync", wt4[1][:, :], dr["ag1u_out"].ap()[k0, :, :, :], "ld"))
        h_tok = {}
        for ol in range(nk):
            oc = k0 + ol
            pg, pu = (pb[0], pb[2]), (pb[1], pb[3])
            bg = wt4[(ol % 2) * 2]; bu = wt4[(ol % 2) * 2 + 1]
            wtok("tensor", guv[ol][1])
            wtok("tensor", h_tok.get(ol - 1))
            if hi == 0 and ol == 0:
                wtok("tensor", xh2_tok)
            def gmm(E, bg=bg, bu=bu):
                last = None
                for kc in range(KC_H):
                    for nt in range(2):
                        xs = sb["xhat2"][:, kc * T + nt * 512: kc * T + nt * 512 + 512]
                        E.matmul(pb[nt][:, :], bg[:, kc * 128:(kc + 1) * 128], xs,
                                 start=(kc == 0), stop=(kc == KC_H - 1))
                        last = E.matmul(pb[2 + nt][:, :], bu[:, kc * 128:(kc + 1) * 128], xs,
                                        start=(kc == 0), stop=(kc == KC_H - 1))
                return last
            vgu = b.op("tensor", MM, gmm)
            if ol + 1 < nk:
                b.waitv("sync", MM, vgu)
                guv[ol + 1] = (
                    b.dma("sync", wt4[((ol + 1) % 2) * 2][:, :], dr["ag1g_out"].ap()[oc + 1, :, :, :], "ld"),
                    b.dma("sync", wt4[((ol + 1) % 2) * 2 + 1][:, :], dr["ag1u_out"].ap()[oc + 1, :, :, :], "ld"))
            b.waitv("scalar", MM, vgu)
            if ol >= 1:
                wtok("scalar", h_tok[ol - 1])
            vh = None
            for nt in range(2):
                vsl = b.op("scalar", SC, lambda E, nt=nt: E.activation(
                    sb["sq_st"][:, nt * 516: nt * 516 + 512], pb[nt][:, :], AF.Silu))
                b.waitv("vector", SC, vsl)
                vh = b.op("vector", EV, lambda E, ol=ol, nt=nt: E.tensor_tensor(
                    sb["hmat"][:, ol * T + nt * 512: ol * T + nt * 512 + 512],
                    sb["sq_st"][:, nt * 516: nt * 516 + 512], pb[2 + nt][:, :], OP.mult))
            h_tok[ol] = (EV, vh)
            if hi == 0 and ol == 0:
                tap("h0", sb["hmat"][:, 0:T])
                tap("wg0", wt4[0][:, :])
        hdone = (EV, S["ev"].v)
        steps = [(cq, tbg, kl) for cq in range(4) for tbg in range(2) for kl in range(nk)]
        NS = len(steps)
        dq = lambda s: "sync" if s % 2 == 0 else "scalar"
        def dload(s):
            cq, tbg, kl = steps[s]
            return b.dma(dq(s), wt4[s % 4][:, 0:512],
                         dr["ag1d_out"].ap()[k0 + kl, cq, :, :], "ld")
        dv = {}
        vdm_tok = {}
        b.waitv("sync", MM, MM.v)
        b.waitv("scalar", MM, MM.v)
        for s in range(min(4, NS)):
            dv[s] = dload(s)
        ev_tok = None
        for s, (cq, tbg, kl) in enumerate(steps):
            wb = wt4[s % 4]
            wtok("tensor", dv[s])
            if kl == 0:
                if s == 0:
                    wtok("tensor", hdone)
                wtok("tensor", acc_tok)
            def dmm(E, wb=wb, kl=kl, tbg=tbg):
                last = None
                for tl in range(4):
                    tb = tbg * 4 + tl
                    last = E.matmul(pb[tl][:, :],
                                    sb["hmat"][:, kl * T + tb * 128:(kl * T + tb * 128) + 128],
                                    wb[:, 0:512], start=(kl == 0), stop=(kl == nk - 1))
                return last
            vdm = b.op("tensor", MM, dmm)
            vdm_tok[s] = vdm
            if s + 4 < NS:
                b.waitv(dq(s + 4), MM, vdm)
                dv[s + 4] = dload(s + 4)
            if kl == nk - 1:
                for tl in range(4):
                    tb = tbg * 4 + tl
                    dst = sb["out_acc"][:, tb * HID + cq * 512: tb * HID + cq * 512 + 512]
                    if hi == 0:
                        b.waitv("scalar", MM, vdm)
                        ev_tok = b.op("scalar", SC, lambda E, dst=dst, tl=tl: E.copy(dst, pb[tl][:, :]))
                    else:
                        b.waitv("vector", MM, vdm)
                        ev_tok = b.op("vector", EV, lambda E, dst=dst, tl=tl: E.tensor_add(
                            dst, dst, pb[tl][:, :]))
                acc_tok = ((SC, ev_tok) if hi == 0 else (EV, ev_tok))
    tap("outacc", sb["out_acc"][:, :])

    if "outacc" in taps:
        sg2 = b.qsems[("gpsimd", "st")]
        b.waitv("vector", sg2, sg2.v)
    x2l = None
    for tb in range(8):
        wtok("sync", x2d_tok)
        wtok("sync", x2l)
        vxf = b.dma("sync", sb["fstage"][:, :], dr["x2_d"].ap()[tb, :, :], "ld")
        osl = sb["out_acc"][:, tb * HID: (tb + 1) * HID]
        wtok("vector", acc_tok)
        vm_ = b.op("vector", EV, lambda E, osl=osl: E.reduce_sum(
            sb["stat_b"][:, 16:17], osl, mybir.AxisListType.X))
        b.waitv("vector", EV, vm_)
        vsq = None
        for si in range(4):
            if tb == 0 and si == 0:
                b.waitv("scalar", EV, vm_)
            vsq = b.op("scalar", SC, lambda E, osl=osl, si=si: E.activation(
                sb["sq_big"][:, si * 512:(si + 1) * 512],
                osl[:, si * 512:(si + 1) * 512], AF.Square))
        b.waitv("vector", SC, vsq)
        vq2b = b.op("vector", EV, lambda E: E.reduce_sum(
            sb["stat_b"][:, 8:9], sb["sq_big"][:, 0:2048], mybir.AxisListType.X))
        b.waitv("vector", EV, vq2b)
        b.waitv("scalar", EV, vq2b)
        vt1 = b.op("vector", EV, lambda E: E.tensor_scalar_mul(
            sb["stat_b"][:, 16:17], sb["stat_b"][:, 16:17], 1.0 / HID))
        b.waitv("vector", EV, vt1)
        vt2 = b.op("vector", EV, lambda E: E.tensor_scalar_mul(
            sb["stat_b"][:, 8:9], sb["stat_b"][:, 8:9], 1.0 / HID))
        b.waitv("vector", EV, vt2)
        vt3 = b.op("vector", EV, lambda E: E.tensor_mul(
            sb["stat_b"][:, 10:11], sb["stat_b"][:, 16:17], sb["stat_b"][:, 16:17]))
        b.waitv("vector", EV, vt3)
        vv2 = b.op("vector", EV, lambda E: E.tensor_sub(
            sb["stat_b"][:, 8:9], sb["stat_b"][:, 8:9], sb["stat_b"][:, 10:11]))
        b.waitv("scalar", EV, vv2)
        b.op("scalar", SC, lambda E: E.activation(
            sb["stat_b"][:, 9:10], sb["stat_b"][:, 8:9], AF.Ln, bias=sb["cst"][:, 0:1]))
        vr = b.op("scalar", SC, lambda E: E.activation(
            sb["stat_b"][:, 17:18], sb["stat_b"][:, 9:10], AF.Exp, scale=sb["cst"][:, 2:3]))
        b.waitv("vector", SC, vr)
        if "ln4" in taps:
            b.waitv("gpsimd", SC, vr)
            b.waitv("gpsimd", EV, S["ev"].v)
            b.dma("gpsimd", taps["ln4"].ap()[tb], sb["stat_b"][:, 0:24], "st")
            sgq = b.qsems[("gpsimd", "st")]
            b.waitv("vector", sgq, sgq.v)
        wtok("vector", vxf)
        vap1 = b.op("vector", EV, lambda E, osl=osl: E.tensor_scalar(
            osl, osl, sb["stat_b"][:, 16:17], sb["stat_b"][:, 17:18], OP.subtract, OP.mult))
        b.waitv("vector", EV, vap1)
        b.op("vector", EV, lambda E, osl=osl: E.tensor_add(osl, osl, sb["fstage"][:, :]))
        b.wait("sync", EV)
        vout = b.dma("sync", dr["out"].ap()[tb * 128:(tb + 1) * 128, :], osl, "st")
        x2l = vout
    b.waitv("gpsimd", b.qsems[("sync", "st")], b.qsems[("sync", "st")].v)
    b.wait("gpsimd", CC)

    for qn in ("sync", "scalar", "gpsimd"):
        for kk in ("ld", "st"):
            sm = b.qsems[(qn, kk)]
            b.waitv("sync", sm, sm.v)
            b.waitv("gpsimd", sm, sm.v)
    b.waitv("gpsimd", CC, CC.v)
    b.waitv("sync", CC, CC.v)

def finish_build(nc, meta):
    pb = [nc.alloc_psum_tensor(f"pb{i}", [128, 512], F32) for i in range(6)]
    pbt = nc.alloc_psum_tensor("pbt", [128, 1024], BF16)
    pb7 = nc.alloc_psum_tensor("pb7", [128, 512], F32)
    bld = Bld(nc, meta["sems"], meta["taps"])
    def body(b):
        emit_program(b, nc, meta, pb, pbt, pb7, None)
        emit_post(b, nc, meta, pb, pbt, pb7)
        emit_tail(b, nc, meta, pb, pbt, pb7)
    with nc.Block() as block:
        block.sync(lambda e: bld.walk("sync", body))
        block.scalar(lambda e: bld.walk("scalar", body))
        block.vector(lambda e: bld.walk("vector", body))
        block.tensor(lambda e: bld.walk("tensor", body))
        block.gpsimd(lambda e: bld.walk("gpsimd", body))
    return nc

_BUILT = {}

def kernel(**inputs):
    cores = prep_inputs(inputs)
    key = "main"
    if key not in _BUILT:
        nc, meta = build_nc(comm=True)
        finish_build(nc, meta)
        _BUILT[key] = nc
    nc = _BUILT[key]
    res = run_bass_kernel_spmd(nc, cores, core_ids=list(range(NCORE)))
    out = np.empty((2, 4096, HID), np.float32)
    for c in range(NCORE):
        bt, s = c // 4, c % 4
        out[bt, s * T:(s + 1) * T] = res.results[c]["out"]
    return out

# version 12
# speedup vs baseline: 1.2498x; 1.0187x over previous
import numpy as np
import ml_dtypes

import concourse.bass as bass
import concourse.mybir as mybir
from concourse.bass_utils import run_bass_kernel_spmd
from concourse.ap import AP

F32 = mybir.dt.float32
BF16 = mybir.dt.bfloat16
I32 = mybir.dt.int32
OP = mybir.AluOpType
AF = mybir.ActivationFunctionType
BFNP = ml_dtypes.bfloat16

HID = 2048; KC_H = 16
INTER = 4096; KC_I = 32
HEADS = 64; HD = 64; STN = 64
CONV = 4224; OC_CONV = 33
KCONV = 4
OC_P = 66; OC_DT = 33; OC_GATE0 = 34
T = 1024; TS = 512; NCH = 8; CH = 128
FI = 5504; FI_PAD = 6144; KC_F = 48; KC_FR = 43; PCF = 6
NCORE = 8
LN_EPS = 1e-5; RMS_EPS = 1e-6
ENGS = ("sync", "scalar", "vector", "tensor", "gpsimd")


def _tile_w_in(in_proj_w):
    W = np.asarray(in_proj_w, np.float32)
    perm = np.empty((OC_P * 128, HID), np.float32)
    perm[:CONV] = W[INTER:INTER + CONV]
    perm[CONV:CONV + 64] = W[INTER + CONV:]
    perm[CONV + 64:CONV + 128] = 0.0
    perm[CONV + 128:] = W[:INTER]
    return np.ascontiguousarray(perm.reshape(OC_P, 128, KC_H, 128).transpose(0, 3, 2, 1))

def _tile_w_out(out_proj_w):
    W = np.asarray(out_proj_w, np.float32)
    return np.ascontiguousarray(W.reshape(16, 128, KC_I, 128).transpose(0, 3, 2, 1))

def _tile_gu_idx(idx):
    p = np.zeros((FI_PAD, HID), np.int32)
    p[:FI] = np.asarray(idx, np.int32)
    return np.ascontiguousarray(p.reshape(KC_F, 128, KC_H, 128).transpose(0, 3, 2, 1))

def _tile_d_idx(idx):
    p = np.zeros((HID, FI_PAD), np.int32)
    p[:, :FI] = np.asarray(idx, np.int32)
    t = np.ascontiguousarray(p.T).reshape(KC_F, 128, 4, 512).transpose(0, 2, 1, 3)
    return np.ascontiguousarray(t)

def _conv_diag(conv_w):
    w = np.asarray(conv_w, np.float32)
    d = np.zeros((OC_CONV, KCONV, 128, 128), np.float32)
    ar = np.arange(128)
    for fc in range(OC_CONV):
        for k in range(KCONV):
            d[fc, k, ar, ar] = w[fc * 128:(fc + 1) * 128, k]
    return d

def prep_inputs(inputs):
    x = np.asarray(inputs["L0_attn_hidden_states"], np.float32)
    com = {}
    com["w_in"] = _tile_w_in(inputs["L0_attn_in_proj_w"])
    com["w_out"] = _tile_w_out(inputs["L0_attn_out_proj_w"])
    com["conv_diag"] = _conv_diag(inputs["L0_attn_conv_w"])
    com["conv_b"] = np.ascontiguousarray(
        np.asarray(inputs["L0_attn_conv_b"], np.float32).reshape(OC_CONV, 128).T)
    com["dt_bias"] = np.asarray(inputs["L0_attn_dt_bias"], np.float32).reshape(64, 1)
    com["a_log"] = np.asarray(inputs["L0_attn_A_log"], np.float32).reshape(64, 1)
    com["d_chan"] = np.ascontiguousarray(
        np.repeat(np.asarray(inputs["L0_attn_D"], np.float32), HD).reshape(KC_I, 128).T)
    com["norm_w"] = np.ascontiguousarray(
        np.asarray(inputs["L0_attn_norm_weight"], np.float32).reshape(KC_I, 128).T)
    com["lut"] = np.asarray(inputs["L0_ffn_lut"], np.float32).reshape(1, 16)
    gi = _tile_gu_idx(inputs["L0_ffn_gate_idx"])
    ui = _tile_gu_idx(inputs["L0_ffn_up_idx"])
    di = _tile_d_idx(inputs["L0_ffn_down_idx"])
    com["ident_bf"] = np.eye(128, dtype=BFNP)
    com["ident_f"] = np.eye(128, dtype=np.float32)
    com["ones_bf"] = np.ones((128, 128), dtype=BFNP)
    com["tri_bf"] = (np.arange(128)[:, None] <= np.arange(128)[None, :]).astype(BFNP)
    cores = []
    for c in range(NCORE):
        b, s = c // 4, c % 4
        t0 = s * T
        xf = np.zeros((HID, T + 128), np.float32)
        lo = max(0, t0 - 3)
        seg = x[b, lo:t0 + T].T
        xf[:, (T + 128) - seg.shape[1]:] = seg
        E = np.zeros((8, 8), np.float32)
        m = np.zeros((8, 1), np.float32)
        for j in range(4 * b, 4 * b + s):
            m[j] = 1.0
            for k in range(j + 1, 4 * b + s):
                E[j, k] = 1.0
        d = dict(com)
        d["x_fm"] = np.ascontiguousarray(xf)
        d["gidx"] = np.ascontiguousarray(gi[PCF * c:PCF * (c + 1)])
        d["uidx"] = np.ascontiguousarray(ui[PCF * c:PCF * (c + 1)])
        d["didx"] = np.ascontiguousarray(di[PCF * c:PCF * (c + 1)])
        d["emask"] = np.ascontiguousarray(E.T)
        d["mmask"] = m
        cores.append(d)
    return cores


class Sem:
    __slots__ = ("h", "v", "q")
    def __init__(self, h, q=False):
        self.h = h; self.v = 0; self.q = q

class Bld:
    def __init__(self, nc, sems, taps):
        self.nc = nc
        self.sems = sems
        self.taps = taps
        self.qsems = {(q, kk): sems[f"q_{q}_{kk}"] for q in ("sync", "scalar", "gpsimd")
                      for kk in ("ld", "st")}
        self.e = None
        self.eng = None
    def walk(self, eng_name, fn):
        self.e = eng_name
        self.eng = getattr(self.nc, eng_name)
        for s in self.sems.values():
            s.v = 0
        fn(self)
    def _go(self, eng):
        return self.e == eng
    def wait(self, eng, sem, val=None):
        if self._go(eng):
            self.eng.wait_ge(sem.h, sem.v if val is None else val)
    def waitv(self, eng, sem, val):
        if self._go(eng):
            self.eng.wait_ge(sem.h, val)
    def dma(self, q, out, in_, kind="ld", **kw):
        sem = self.qsems[(q, "st" if kind == "st" else "ld")]
        v = sem.v = sem.v + 16
        if self._go(q):
            self.eng.dma_start(out=out, in_=in_, **kw).then_inc(sem.h, 16)
        return (sem, v)
    def op(self, eng, sem, f, n=1):
        v = None
        if sem is not None:
            v = sem.v = sem.v + n
        if self._go(eng):
            ins = f(self.eng)
            if sem is not None:
                ins.then_inc(sem.h, n)
        return v
    def ops(self, eng, f):
        if self._go(eng):
            f(self.eng)

def ap_of(t):
    return t.ap() if hasattr(t, "ap") else t

def build_nc(comm=True, taps=()):
    nc = bass.Bass("TRN2", target_bir_lowering=False, detect_race_conditions=False)
    dr = {}
    def din(name, shape, dt):
        dr[name] = nc.dram_tensor(name, list(shape), dt, kind="ExternalInput")
    def dint(name, shape, dt, **kw):
        dr[name] = nc.dram_tensor(name, list(shape), dt, kind="Internal", **kw)
    din("x_fm", (HID, T + 128), F32)
    din("w_in", (OC_P, 128, KC_H, 128), F32)
    din("w_out", (16, 128, KC_I, 128), F32)
    din("conv_diag", (OC_CONV, KCONV, 128, 128), F32)
    din("conv_b", (128, OC_CONV), F32)
    din("dt_bias", (64, 1), F32)
    din("a_log", (64, 1), F32)
    din("d_chan", (128, KC_I), F32)
    din("norm_w", (128, KC_I), F32)
    din("lut", (1, 16), F32)
    din("gidx", (PCF, 128, KC_H, 128), I32)
    din("uidx", (PCF, 128, KC_H, 128), I32)
    din("didx", (PCF, 4, 128, 512), I32)
    din("ident_bf", (128, 128), BF16)
    din("ident_f", (128, 128), F32)
    din("ones_bf", (128, 128), BF16)
    din("tri_bf", (128, 128), BF16)
    din("emask", (8, 8), F32)
    din("mmask", (8, 1), F32)
    dr["out"] = nc.dram_tensor("out", [T, HID], F32, kind="ExternalOutput")

    dint("gate_d", (2, 128, 32, TS), BF16)
    dint("hbc_d", (128, OC_CONV, TS + 3), BF16)
    dint("y_d", (NCH, 8, 128, 512), BF16)
    dint("snap_d", (NCH, 64, 4096), BF16)
    dint("cum_d", (NCH, 64, 128), F32)
    dint("x2_d", (8, 128, HID), F32)
    dint("pref_d", (8, 64), F32)
    dint("lam_d", (NCH, 64), F32)
    dint("tot_d", (64, 1), F32)
    dint("coef_d", (8, 64), F32)
    dint("ag1g_in", (PCF, 128, KC_H, 128), BF16)
    dint("ag1u_in", (PCF, 128, KC_H, 128), BF16)
    dint("ag1d_in", (PCF, 4, 128, 512), BF16)
    dint("ag2_in", (64, 4160), F32)
    spc = dict(addr_space="Shared") if comm else {}
    dint("ag1g_out", (KC_F, 128, KC_H, 128), BF16, **spc)
    dint("ag1u_out", (KC_F, 128, KC_H, 128), BF16, **spc)
    dint("ag1d_out", (KC_F, 4, 128, 512), BF16, **spc)
    dint("ag2_out", (NCORE * 64, 4160), F32, **spc)

    tap_t = {}
    for tp in taps:
        tap_t[tp] = nc.dram_tensor("tap_" + tp, list(_TAPS[tp]), F32, kind="ExternalOutput")

    sb = {}
    def sbn(name, shape, dt):
        sb[name] = nc.alloc_sbuf_tensor("s_" + name, list(shape), dt)
        return sb[name]
    for nm, sh, dt in [
        ("ident_bf", (128, 128), BF16), ("ident_f", (128, 128), F32),
        ("ones_bf", (128, 128), BF16), ("tri_bf", (128, 128), BF16),
        ("conv_b", (128, OC_CONV), F32), ("dt_bias", (64, 1), F32),
        ("a_neg", (64, 1), F32), ("d_chan", (128, KC_I), F32),
        ("norm_w", (128, KC_I), F32), ("lut_rep", (128, 16), F32),
        ("emask", (8, 8), F32), ("mmask", (8, 1), F32),
        ("canary", (64, 184), F32),
        ("totals", (64, NCH), F32), ("s_state", (64, 4096), F32),
        ("s_tmp", (64, 512), F32), ("lam_rep", (64, 64), F32),
        ("stat_a", (128, T + 128), BF16), ("stat_b", (128, 520), F32),
        ("stat_c", (128, T + 128), BF16),
        ("bmat", (64, T), BF16), ("cmat", (64, T), BF16),
        ("small64", (64, 128), F32), ("small64b", (64, 128), F32),
        ("wq", (128, 2048), BF16), ("wq2", (128, 2048), BF16),
        ("ystage", (128, 512), BF16), ("sq_st", (128, 1032), BF16),
        ("cst", (128, 8), F32),
    ]:
        sbn(nm, sh, dt)
    from concourse.bass_primitives import MemorySpace
    base = int(nc.bytes_allocated(MemorySpace.SBUF, "left"))
    base = (base + 31) // 32 * 32
    cap = int(nc.sbuf_top)

    def at(name, shape, dt, off):
        t = nc.alloc_sbuf_tensor_at("s_" + name, list(shape), dt, offset=base + off)
        sb[name] = t
        return t
    K = 1024
    at("xhat", (128, KC_H * (T + 128)), BF16, 0)
    at("hbc", (128, OC_CONV * (TS + 3)), BF16, 37 * K)
    at("x_conv", (128, KC_I * 256), BF16, 71 * K)
    at("x_tok", (128, INTER), BF16, 87 * K)
    at("dtx", (128, INTER), BF16, 95 * K)
    at("cum_rep", (128, 16 * 128), F32, 103 * K)
    at("gm", (128, 16 * 128), BF16, 111 * K)
    at("dt_tok", (128, 64), F32, 115 * K)
    at("cum_t", (128, 64), F32, 116 * K)
    at("dec_fm", (64, 128), F32, 117 * K)
    at("dec_tok", (128, 64), F32, 118 * K)
    at("b_tok", (128, 64), BF16, 119 * K)
    at("gtric", (128, 128), BF16, 120 * K)
    at("stage8", (128, 1024), F32, 121 * K)
    at("hbc_sav", (128, OC_CONV * 3), BF16, 125 * K)
    at("dt", (64, T), F32, 126 * K)
    at("gvec", (64, T), F32, 130 * K)
    at("cum", (64, T), F32, 134 * K)
    at("gbuf_i", (128, 2048), BF16, 138 * K)
    at("gbuf_a", (128, 2048), BF16, 142 * K)
    at("gbuf_m", (128, 2048), BF16, 146 * K)
    at("gbuf_i2", (128, 2048), BF16, 150 * K)
    at("cum_repb", (128, 16 * 128), F32, 138 * K)
    at("gmb", (128, 16 * 128), BF16, 146 * K)
    at("y_g", (128, KC_I * T), BF16, 0)
    at("gate_sp", (128, 32 * TS), BF16, 64 * K)
    at("y_ld", (128, 4096), BF16, 96 * K)
    at("carry", (64, 4096), F32, 104 * K)
    at("fin_b", (64, 4096), F32, 120 * K)
    at("snap_ld", (64, 4096), BF16, 120 * K)
    at("trueS", (64, 4096), BF16, 136 * K)
    at("cum_rep2", (128, 16 * 128), F32, 144 * K)
    at("cum_rep2b", (128, 16 * 128), F32, 128 * K)
    at("expc", (64, 16 * 128), BF16, 152 * K)
    at("ctil", (64, 16 * 128), BF16, 156 * K)
    at("p_rep", (64, 64), F32, 160 * K)
    at("coef", (8, 64), F32, 161 * K)
    at("coef_rep", (64, 512), F32, 162 * K)
    at("y_tmp", (128, 512), F32, 160 * K)
    at("mo_bf", (128, KC_H * T), BF16, 64 * K)
    at("x2_fm", (128, KC_H * T), F32, 96 * K)
    at("xhat2", (128, KC_H * T), BF16, 0)
    at("xstage", (128, 1024), F32, 32 * K)
    at("fstage2", (128, 512), F32, 36 * K)
    at("hmat", (128, 22 * T), BF16, 32 * K)
    at("out_acc", (128, 8 * HID), F32, 76 * K)
    at("fstage", (128, 2048), F32, 140 * K)
    at("sq_big", (128, 2048), F32, 64 * K)
    at("wq3", (128, 2048), BF16, 148 * K)
    at("wq4", (128, 2048), BF16, 152 * K)
    assert base + 164 * K <= cap, (base, cap)

    sems = {}
    for s in ["mm", "ev", "sc", "gp", "cc", "ph",
              "q_sync_ld", "q_sync_st", "q_scalar_ld", "q_scalar_st",
              "q_gpsimd_ld", "q_gpsimd_st"]:
        sems[s] = Sem(nc.alloc_semaphore(s), q=s.startswith("q_"))

    meta = dict(dr=dr, sb=sb, sems=sems, taps=tap_t, comm=comm)
    return nc, meta


_TAPS = {
    "decraw": (64, 528),
    "x2d": (8, 128, HID),
    "canary": (64, 184),
    "gatesp": (128, 32 * TS),
    "rrms": (128, 128),
    "dtxd0": (128, INTER),
    "lam0": (64, 64),
    "dec0": (128, 64),
    "btok0": (128, 64),
    "sstate0": (64, 4096),
    "sstate1": (64, 4096),
    "sstate2": (64, 4096),
    "sstate3": (64, 4096),
    "sstate4": (64, 4096),
    "sstate5": (64, 4096),
    "sstate6": (64, 4096),
    "sstate7": (64, 4096),
    "dtxd2": (128, INTER),
    "dec2": (128, 64),
    "btok2": (128, 64),
    "cmat": (64, T),
    "xhat": (128, KC_H * (T + 128)),
    "hbc": (128, OC_CONV * (TS + 3)),
    "dt": (64, T),
    "gate0": (128, TS),
    "cum": (64, T),
    "xconv": (128, KC_I * 256),
    "bmat": (64, T),
    "sstate": (64, 4096),
    "ydc": (8, 128, 512),
    "carry": (64, 4096),
    "yld": (128, 4096),
    "y_g": (128, KC_I * T),
    "mo": (128, KC_H * T),
    "x2": (128, KC_H * T),
    "xhat2": (128, KC_H * T),
    "h0": (128, T),
    "outacc": (128, 8 * HID),
    "ln4": (8, 128, 24),
    "wg0": (128, 2048),
}


def fap(t, levels, off=0):
    return AP(t, off, [list(l) for l in levels])

def emit_program(b, nc, meta, pb, pbt, pb7, pfn):
    dr, sb, taps, comm = meta["dr"], meta["sb"], meta["taps"], meta["comm"]
    S = meta["sems"]
    MM, EV, SC, CC = S["mm"], S["ev"], S["sc"], S["cc"]
    wt = [sb["wq"], sb["wq2"]]

    def wtok(eng, tok):
        if tok is not None:
            b.waitv(eng, tok[0], tok[0].v if getattr(tok[0], "q", False) else tok[1])

    def tap(name, src):
        if name in taps:
            b.waitv("gpsimd", S["ev"], S["ev"].v)
            b.waitv("gpsimd", S["sc"], S["sc"].v)
            b.waitv("gpsimd", S["mm"], S["mm"].v)
            b.dma("gpsimd", taps[name].ap(), src, "st")

    for nm in ["ident_bf", "ident_f", "ones_bf", "tri_bf", "conv_b", "dt_bias",
               "d_chan", "norm_w", "emask", "mmask"]:
        b.dma("sync", sb[nm][:, :], dr[nm].ap(), "ld")
    b.dma("sync", sb["small64"][:, 0:1], dr["a_log"].ap(), "ld")
    consts_tok = b.dma("sync", sb["lut_rep"][:, :], fap(dr["lut"], [[0, 128], [1, 16]]), "ld")
    xsrc = fap(dr["x_fm"], [[T + 128, 128], [128 * (T + 128), KC_H], [1, T + 128]])
    vx = b.dma("gpsimd", sb["xhat"][:, :], xsrc, "ld")
    b.op("vector", EV, lambda E: E.memset(sb["canary"][:, :], 7.25))
    for ci, cv in enumerate([LN_EPS, RMS_EPS, -0.5, -1.0, 1.0 / HID, 1.0 / INTER, 1.0]):
        b.op("vector", EV, lambda E, ci=ci, cv=cv: E.memset(sb["cst"][:, ci:ci + 1], cv))
    cst_tok = (S["ev"], S["ev"].v)
    meta["cst_tok"] = cst_tok
    b.waitv("scalar", EV, cst_tok[1])
    wtok("scalar", consts_tok)
    va = b.op("scalar", SC, lambda E: E.activation(sb["small64b"][:, 0:1], sb["small64"][:, 0:1], AF.Exp))
    b.waitv("vector", SC, va)
    vneg = b.op("vector", EV, lambda E: E.tensor_scalar_mul(sb["a_neg"][:, :], sb["small64b"][:, 0:1], -1.0))

    gtiles = []
    for mat, (inm, onm) in enumerate([("gidx", "ag1g_in"), ("uidx", "ag1u_in"),
                                      ("didx", "ag1d_in")]):
        for ti in range(PCF):
            gtiles.append((mat, inm, onm, ti))
    gst = {"j": 0, "gv": {}, "st": {}, "ld": {}, "cc": 0}
    ibufs = ["gbuf_i", "gbuf_i2"]
    def emit_gcc(nm):
        b.waitv("gpsimd", b.qsems[("sync", "st")], b.qsems[("sync", "st")].v)
        if comm:
            b.op("gpsimd", CC, lambda E, nm=nm: E.collective_compute(
                "AllGather", OP.bypass, replica_groups=[list(range(NCORE))],
                ins=[dr[f"ag1{nm}_in"].ap().opt()], outs=[dr[f"ag1{nm}_out"].ap().opt()]))
            return (CC, CC.v)
        vv = None
        for r in range(NCORE):
            vv = b.dma("gpsimd", dr[f"ag1{nm}_out"].ap()[r * PCF:(r + 1) * PCF, :, :, :],
                  dr[f"ag1{nm}_in"].ap()[:, :, :, :], "st")
        return vv
    def emit_gjob():
        j = gst["j"]
        if j >= 72:
            return None
        gst["j"] = j + 1
        tn, q = j // 4, j % 4
        mat, inm, onm, ti = gtiles[tn]
        ib = sb[ibufs[tn % 2]]
        if q == 0:
            if tn >= 2:
                b.waitv("gpsimd", EV, gst["gv"][(tn - 2) * 4 + 3])
            if mat < 2:
                isrc = dr[inm].ap()[ti, :, :, :]
            else:
                isrc = fap(dr[inm], [[512, 128], [65536, 4], [1, 512]],
                           off=ti * 4 * 128 * 512)
            gst["ld"][tn] = b.dma("gpsimd", ib[:, :], isrc, "ld")
        h = (j % 2) * 512
        if mat < 2:
            dst = dr[onm].ap()[ti, :, 4 * q:4 * q + 4, :]
        else:
            dst = dr[onm].ap()[ti, q, :, :]
        wtok("vector", gst["ld"][tn])
        if j == 0:
            wtok("vector", consts_tok)
        if j >= 2:
            wtok("vector", gst["st"][j - 2])
        def gops(E, h=h, q=q, ib=ib):
            ii = ib[:, q * 512:q * 512 + 512]; aa = sb["gbuf_a"][:, h:h + 512]
            mm_ = sb["gbuf_m"][:, h:h + 512]
            E.tensor_scalar(aa, ii, 0.0, sb["lut_rep"][:, 0:1], OP.is_equal, OP.mult)
            for v in range(1, 16):
                E.tensor_scalar(mm_, ii, float(v), sb["lut_rep"][:, v:v + 1], OP.is_equal, OP.mult)
                last = E.tensor_add(aa, aa, mm_)
            return last
        gst["gv"][j] = b.op("vector", EV, gops)
        b.wait("sync", EV)
        gst["st"][j] = b.dma("sync", dst, sb["gbuf_a"][:, h:h + 512], "st")
        if j + 1 in (24, 48, 72):
            gst["cc"] = emit_gcc("gud"[(j + 1) // 24 - 1])
        return None
    def finish_gather():
        while gst["j"] < 72:
            emit_gjob()
        return gst["cc"]

    segs = [(128, 512), (640, 512), (0, 128)]
    wtok("tensor", vx)
    wtok("scalar", vx)
    for si, (lo, n) in enumerate(segs):
        p_s, p_q = pb[0], pb[1]
        sqv = []
        for kc in range(KC_H):
            xs = sb["xhat"][:, kc * (T + 128) + lo: kc * (T + 128) + lo + n]
            sq = sb["sq_st"][:, (kc % 2) * 516:(kc % 2) * 516 + n]
            if kc >= 2:
                b.waitv("scalar", MM, sqv[kc - 2])
            vsc = b.op("scalar", SC, lambda E, xs=xs, sq=sq: E.activation(sq, xs, AF.Square))
            b.waitv("tensor", SC, vsc)
            vm = b.op("tensor", MM, lambda E, xs=xs, sq=sq, kc=kc, n=n:
                      [E.matmul(p_s[:, :n], sb["ones_bf"][:, :], xs,
                                start=(kc == 0), stop=(kc == KC_H - 1)),
                       E.matmul(p_q[:, :n], sb["ones_bf"][:, :], sq,
                                start=(kc == 0), stop=(kc == KC_H - 1))][-1])
            sqv.append(vm)
        sa = sb["stat_a"][:, lo:lo + n]; sb_ = sb["stat_b"][:, 0:n]; sc_ = sb["stat_c"][:, lo:lo + n]
        b.waitv("scalar", MM, sqv[-1])
        b.op("scalar", SC, lambda E, sa=sa, n=n: E.activation(sa, p_s[:, :n], AF.Copy, scale=sb["cst"][:, 4:5]))
        v2 = b.op("scalar", SC, lambda E, sc_=sc_, n=n: E.activation(sc_, p_q[:, :n], AF.Copy, scale=sb["cst"][:, 4:5]))
        b.waitv("vector", SC, v2)
        b.op("vector", EV, lambda E, sa=sa, sb_=sb_: E.tensor_mul(sb_, sa, sa))
        v3 = b.op("vector", EV, lambda E, sc_=sc_, sb_=sb_: E.tensor_sub(sc_, sc_, sb_))
        b.waitv("scalar", EV, v3)
        b.op("scalar", SC, lambda E, sb_=sb_, sc_=sc_: E.activation(sb_, sc_, AF.Ln, bias=sb["cst"][:, 0:1]))
        v4 = b.op("scalar", SC, lambda E, sc_=sc_, sb_=sb_: E.activation(sc_, sb_, AF.Exp, scale=sb["cst"][:, 2:3]))
        b.waitv("vector", SC, v4)
        for kc in range(KC_H):
            xs = sb["xhat"][:, kc * (T + 128) + lo: kc * (T + 128) + lo + n]
            stg = sb["ystage"][:, :n]
            b.op("vector", EV, lambda E, xs=xs, stg=stg, sa=sa: E.tensor_sub(stg, xs, sa))
            b.op("vector", EV, lambda E, xs=xs, stg=stg, sc_=sc_: E.tensor_mul(xs, stg, sc_))
    tap("xhat", sb["xhat"][:, :])
    ln1_done = (EV, EV.v)

    bank_tok = {0: ln1_done, 1: ln1_done, 2: None, 3: None, 4: None}
    mm_tok = {}
    wv = {}
    for oc in range(2):
        wv[oc] = b.dma("gpsimd", wt[oc % 2][:, :], dr["w_in"].ap()[oc, :, :, :], "ld")
    stage_tok = None
    dtB = None
    for oc in range(OC_P):
        pA, pB = pb[oc % 2], pb[2 + oc % 2]
        wtok("tensor", wv[oc])
        wtok("tensor", bank_tok[oc % 2])
        wtok("tensor", bank_tok[2 + oc % 2])
        if oc <= OC_DT:
            wtok("tensor", bank_tok[4])
        wtile = wt[oc % 2]
        def mmrun(E, wtile=wtile, pA=pA, pB=pB):
            last = None
            for kc in range(KC_H):
                lw = wtile[:, kc * 128:(kc + 1) * 128]
                xa = sb["xhat"][:, kc * (T + 128) + 128: kc * (T + 128) + 640]
                xb2 = sb["xhat"][:, kc * (T + 128) + 640: kc * (T + 128) + 1152]
                E.matmul(pA[:, :], lw, xa, start=(kc == 0), stop=(kc == KC_H - 1))
                last = E.matmul(pB[:, :], lw, xb2, start=(kc == 0), stop=(kc == KC_H - 1))
            return last
        vmm = b.op("tensor", MM, mmrun)
        if oc <= OC_DT:
            def mmh(E, wtile=wtile):
                last = None
                for kc in range(KC_H):
                    lw = wtile[:, kc * 128:(kc + 1) * 128]
                    xh = sb["xhat"][:, kc * (T + 128) + 125: kc * (T + 128) + 128]
                    last = E.matmul(pb[4][:, :3], lw, xh, start=(kc == 0), stop=(kc == KC_H - 1))
                return last
            vmm = b.op("tensor", MM, mmh)
        mm_tok[oc] = vmm
        if oc + 2 < OC_P:
            b.waitv("gpsimd", MM, mm_tok[oc])
            wv[oc + 2] = b.dma("gpsimd", wt[(oc + 2) % 2][:, :], dr["w_in"].ap()[oc + 2, :, :, :], "ld")
        if oc < OC_CONV:
            b.waitv("vector", MM, vmm)
            tA = b.op("vector", EV, lambda E, oc=oc, pA=pA: E.tensor_copy(
                sb["hbc"][:, oc * 515 + 3: oc * 515 + 515], pA[:, :]))
            tH = b.op("vector", EV, lambda E, oc=oc: E.tensor_copy(
                sb["hbc"][:, oc * 515: oc * 515 + 3], pb[4][:, :3]))
            wtok("vector", stage_tok)
            tB = b.op("vector", EV, lambda E, pB=pB: E.tensor_copy(sb["sq_st"][:, :512], pB[:, :]))
            b.wait("sync", EV)
            vst = b.dma("sync", dr["hbc_d"].ap()[:, oc, 3:515], sb["sq_st"][:, :512], "st")
            stage_tok = vst
            bank_tok[oc % 2] = (EV, tA)
            bank_tok[2 + oc % 2] = (EV, tB)
            bank_tok[4] = (EV, tH)
        elif oc == OC_DT:
            b.waitv("scalar", MM, vmm)
            dtA = b.op("scalar", SC, lambda E, pA=pA: E.copy(sb["dt"][0:64, 0:512], pA[0:64, :]))
            dtB = b.op("scalar", SC, lambda E, pB=pB: E.copy(sb["dt"][0:64, 512:1024], pB[0:64, :]))
            bank_tok[oc % 2] = (SC, dtA)
            bank_tok[2 + oc % 2] = (SC, dtB)
            bank_tok[4] = (SC, dtB)
        else:
            hp = oc - OC_GATE0
            b.waitv("scalar", MM, vmm)
            for (ps, sp) in ((pA, 0), (pB, 1)):
                wtok("scalar", stage_tok)
                tg = b.op("scalar", SC, lambda E, ps=ps: E.activation(
                    sb["sq_st"][:, :512], ps[:, :], AF.Silu))
                b.wait("sync", SC)
                vst = b.dma("sync", dr["gate_d"].ap()[sp, :, hp, :], sb["sq_st"][:, :512], "st")
                stage_tok = vst
                bank_tok[(0 if sp == 0 else 2) + oc % 2] = (SC, tg)
        emit_gjob()
        if oc % 10 == 9:
            emit_gjob()
    ag1_tok = finish_gather()
    tap("hbc", sb["hbc"][:, :])

    b.waitv("scalar", SC, dtB)
    b.waitv("vector", SC, dtB)
    b.op("vector", EV, lambda E: E.tensor_scalar(sb["dt"][:, :], sb["dt"][:, :],
                                                 sb["dt_bias"][:, :], None, OP.add))
    b.op("vector", EV, lambda E: E.tensor_scalar_max(sb["gvec"][:, :], sb["dt"][:, :], 0.0))
    b.op("vector", EV, lambda E: E.tensor_scalar_min(sb["dt"][:, :], sb["dt"][:, :], 0.0))
    b.op("vector", EV, lambda E: E.tensor_sub(sb["dt"][:, :], sb["dt"][:, :], sb["gvec"][:, :]))
    vfix = b.op("vector", EV, lambda E: E.tensor_scalar_max(sb["dt"][:, :], sb["dt"][:, :], -80.0))
    b.waitv("scalar", EV, vfix)
    b.op("scalar", SC, lambda E: E.activation(sb["dt"][:, :], sb["dt"][:, :], AF.Exp))
    vln = b.op("scalar", SC, lambda E: E.activation(sb["dt"][:, :], sb["dt"][:, :], AF.Ln,
                                                    bias=sb["cst"][0:64, 6:7]))
    b.waitv("vector", SC, vln)
    vdt = b.op("vector", EV, lambda E: E.tensor_add(sb["dt"][:, :], sb["dt"][:, :], sb["gvec"][:, :]))
    vdt_ev = vdt
    tap("dt", sb["dt"][:, :])
    b.op("vector", EV, lambda E: E.tensor_scalar(sb["gvec"][:, :], sb["dt"][:, :],
                                                 sb["a_neg"][:, :], None, OP.mult))
    cum_wr = []
    for c in range(NCH):
        g = sb["gvec"][:, c * 128:(c + 1) * 128]
        vscn = b.op("vector", EV, lambda E, g=g, c=c: E.tensor_tensor_scan(
            sb["cum"][:, c * 128:(c + 1) * 128], g, g, 0.0, OP.add, OP.bypass))
        b.waitv("vector", EV, vscn)
        vtc = b.op("vector", EV, lambda E, c=c: E.tensor_copy(
            sb["totals"][:, c:c + 1], sb["cum"][:, c * 128 + 127: c * 128 + 128]))
        if c == NCH - 1:
            b.waitv("vector", EV, vtc)
        b.wait("sync", EV)
        b.dma("sync", fap(dr["lam_d"], [[1, 64], [1, 1]], off=c * 64),
              sb["totals"][:, c:c + 1], "st")
        cum_wr.append(b.dma("sync", dr["cum_d"].ap()[c, :, :], sb["cum"][:, c * 128:(c + 1) * 128], "st"))
    vscn2 = b.op("vector", EV, lambda E: E.tensor_tensor_scan(
        sb["small64"][:, 0:8], sb["totals"][:, :], sb["totals"][:, :], 0.0, OP.add, OP.bypass))
    b.waitv("vector", EV, vscn2)
    b.op("vector", EV, lambda E: E.memset(sb["small64b"][:, 0:1], 0.0))
    b.op("vector", EV, lambda E: E.tensor_copy(sb["small64b"][:, 1:8], sb["small64"][:, 0:7]))
    b.wait("sync", EV)
    for c in range(NCH):
        vpref = b.dma("sync", fap(dr["pref_d"], [[1, 64], [1, 1]], off=c * 64),
                      sb["small64b"][:, c:c + 1], "st")
    vtot = b.dma("sync", dr["tot_d"].ap(), sb["small64"][:, 7:8], "st")
    b.op("vector", EV, lambda E: E.tensor_copy(sb["stage8"][0:64, 512:520], sb["totals"][:, :]))
    tap("cum", sb["cum"][:, :])

    ydc_tok = {}
    snap_tok = {}
    st8 = {}
    vz0 = b.op("vector", EV, lambda E: E.memset(sb["s_state"][:, :], 0.0))
    st8["srec"] = (EV, vz0)
    hbc_done = (EV, EV.v)
    for sp in range(2):
        if sp == 1:
            vsv = b.op("vector", EV, lambda E: E.tensor_copy(
                fap(sb["hbc_sav"], [[OC_CONV * 3, 128], [3, OC_CONV], [1, 3]]),
                fap(sb["hbc"], [[OC_CONV * 515, 128], [515, OC_CONV], [1, 3]], off=512)))
            wtok("sync", stage_tok)
            b.waitv("sync", EV, vsv)
            vrl = b.dma("sync", fap(sb["hbc"], [[OC_CONV * 515, 128], [515, OC_CONV], [1, 512]], off=3),
                        dr["hbc_d"].ap()[:, :, 3:515], "ld")
            wtok("vector", vrl)
            vh = b.op("vector", EV, lambda E: E.tensor_copy(
                fap(sb["hbc"], [[OC_CONV * 515, 128], [515, OC_CONV], [1, 3]]),
                sb["hbc_sav"][:, :]))
            hbc_ready = (EV, vh)
        else:
            hbc_ready = None
        for hs in range(2):
            lo = hs * 256
            cw = {}
            def cdsrc(fc):
                return fap(dr["conv_diag"], [[128, 128], [128 * 128, 4], [1, 128]],
                           off=fc * KCONV * 128 * 128)
            b.waitv("gpsimd", MM, MM.v)
            for fc in range(2):
                cw[fc] = b.dma("gpsimd", wt[fc % 2][:, :512], cdsrc(fc), "ld")
            conv_ev = {}
            for fc in range(OC_CONV):
                pc = pb[4 + fc % 2]
                wtok("tensor", cw[fc])
                wtok("tensor", hbc_ready)
                wtok("tensor", conv_ev.get(fc - 2))
                if sp == 0 and hs == 0 and fc < 2:
                    wtok("tensor", bank_tok[4])
                    wtok("tensor", hbc_done)
                def cmm(E, fc=fc, pc=pc, lo=lo):
                    last = None
                    for k in range(KCONV):
                        lw = wt[fc % 2][:, k * 128:(k + 1) * 128]
                        rhs = sb["hbc"][:, fc * 515 + lo + k: fc * 515 + lo + k + 256]
                        last = E.matmul(pc[:, :256], lw, rhs, start=(k == 0), stop=(k == KCONV - 1))
                    return last
                vc = b.op("tensor", MM, cmm)
                if fc + 2 < OC_CONV:
                    b.waitv("gpsimd", MM, vc)
                    cw[fc + 2] = b.dma("gpsimd", wt[fc % 2][:, :512], cdsrc(fc + 2), "ld")
                b.waitv("scalar", MM, vc)
                bias = sb["conv_b"][:, fc:fc + 1]
                if fc < 32:
                    wtok("scalar", st8.get("xconv_free_ev"))
                    wtok("scalar", st8.get("xconv_free_mm"))
                    ve = b.op("scalar", SC, lambda E, fc=fc, pc=pc, bias=bias: E.activation(
                        sb["x_conv"][:, fc * 256: fc * 256 + 256], pc[:, :256], AF.Silu, bias=bias))
                else:
                    tok0 = sp * 512 + lo
                    b.op("scalar", SC, lambda E, pc=pc, tok0=tok0: E.activation(
                        sb["bmat"][0:64, tok0:tok0 + 256], pc[0:64, :256], AF.Silu,
                        bias=sb["conv_b"][0:64, 32:33]))
                    ve = b.op("scalar", SC, lambda E, pc=pc, tok0=tok0: E.activation(
                        sb["cmat"][0:64, tok0:tok0 + 256], pc[64:128, :256], AF.Silu,
                        bias=sb["conv_b"][64:128, 32:33]))
                conv_ev[fc] = (SC, ve)
            if sp == 0 and hs == 0:
                tap("xconv", sb["x_conv"][:, :])
            conv_done = (SC, SC.v)
            if sp == 0 and hs == 0:
                b.op("vector", EV, lambda E: E.tensor_copy(sb["stage8"][0:64, 520:528], sb["totals"][:, :]))
            for c2 in range(2):
                ch = sp * 4 + hs * 2 + c2
                lo2 = c2 * 128
                emit_chunk_inpass(b, nc, meta, pb, pbt, pb7, ch, sp, lo2, conv_done,
                                  cum_wr, ydc_tok, snap_tok, wtok, tap, st8)
            st8["xconv_free_ev"] = (EV, EV.v)
            st8["xconv_free_mm"] = (MM, MM.v)
    meta["_mix"] = dict(ydc_tok=ydc_tok, snap_tok=snap_tok, vpref=vpref, vtot=vtot,
                        ag1_tok=ag1_tok)

YBANKS = [0, 1, 2, 3, 4, 5, 7, 0]

def emit_chunk_inpass(b, nc, meta, pb, pbt, pb7, ch, sp, lo2, conv_done,
                      cum_wr, ydc_tok, snap_tok, wtok, tap, st8):
    dr, sb, S = meta["dr"], meta["sb"], meta["sems"]
    MM, EV, SC, CC = S["mm"], S["ev"], S["sc"], S["cc"]
    tok0 = ch * 128

    prev = st8.get("xtok_free")
    for w in range(4):
        wtok("tensor", conv_done if w == 0 else None)
        wtok("tensor", st8.get("pbt_free"))
        vtr = b.op("tensor", MM, lambda E, w=w: [
            E.transpose(pbt[:, i * 128:(i + 1) * 128],
                        sb["x_conv"][:, (8 * w + i) * 256 + lo2:(8 * w + i) * 256 + lo2 + 128],
                        sb["ident_bf"][:, :]) for i in range(8)][-1])
        b.waitv("vector", MM, vtr)
        wtok("vector", prev)
        ve = b.op("vector", EV, lambda E, w=w: E.tensor_copy(
            sb["x_tok"][:, w * 1024:(w + 1) * 1024], pbt[:, :1024]))
        st8["pbt_free"] = (EV, ve)
    wtok("tensor", st8.get("pbt_free"))
    vtr = b.op("tensor", MM, lambda E: E.transpose(
        pbt[:, 0:64], sb["bmat"][0:64, tok0:tok0 + 128], sb["ident_bf"][0:64, 0:64]))
    b.waitv("vector", MM, vtr)
    vbt = b.op("vector", EV, lambda E: E.tensor_copy(sb["b_tok"][:, :], pbt[:, 0:64]))
    st8["pbt_free"] = (EV, vbt)
    wtok("tensor", st8.get("pb7_free"))
    vt2 = b.op("tensor", MM, lambda E: [
        E.transpose(pb7[:, 0:64], sb["cum"][0:64, tok0:tok0 + 128], sb["ident_f"][0:64, 0:64]),
        E.transpose(pb7[:, 64:128], sb["dt"][0:64, tok0:tok0 + 128], sb["ident_f"][0:64, 0:64])][-1])
    b.waitv("vector", MM, vt2)
    b.op("vector", EV, lambda E: E.tensor_copy(sb["cum_t"][:, :], pb7[:, 0:64]))
    vdt2 = b.op("vector", EV, lambda E: E.tensor_copy(sb["dt_tok"][:, :], pb7[:, 64:128]))
    b.op("vector", EV, lambda E: E.tensor_scalar(
        sb["dec_fm"][:, :], sb["cum"][0:64, tok0:tok0 + 128],
        sb["totals"][:, ch:ch + 1], None, OP.subtract))
    vdf = b.op("vector", EV, lambda E: E.tensor_scalar_min(
        sb["dec_fm"][:, :], sb["dec_fm"][:, :], 80.0))
    if ch == 0:
        b.op("vector", EV, lambda E: E.tensor_copy(sb["stage8"][0:64, 0:128], sb["dec_fm"][:, :]))
        b.op("vector", EV, lambda E: E.tensor_copy(sb["stage8"][0:64, 128:136], sb["totals"][:, :]))
        tap("decraw", sb["stage8"][0:64, 0:528])
    b.waitv("scalar", EV, vdf)
    vde = b.op("scalar", SC, lambda E: E.activation(sb["dec_fm"][:, :], sb["dec_fm"][:, :],
                                                    AF.Exp, scale=sb["cst"][0:64, 3:4]))
    b.waitv("tensor", SC, vde)
    b.waitv("tensor", EV, vdt2)
    vt3 = b.op("tensor", MM, lambda E: E.transpose(
        pb7[:, 128:192], sb["dec_fm"][0:64, :], sb["ident_f"][0:64, 0:64]))
    b.waitv("vector", MM, vt3)
    vdk = b.op("vector", EV, lambda E: E.tensor_copy(sb["dec_tok"][:, :], pb7[:, 128:192]))
    st8["pb7_free"] = (EV, vdk)

    b.waitv("vector", EV, vdt2)
    vdx = b.op("vector", EV, lambda E: E.tensor_tensor(
        sb["dtx"][:, :], sb["x_tok"][:, :],
        fap(sb["dt_tok"], [[64, 128], [1, 64], [0, 64]]), OP.mult))

    wtok("tensor", st8.get("pb7_free"))
    vg = b.op("tensor", MM, lambda E: E.matmul(
        pb7[:, 256:384], sb["bmat"][0:64, tok0:tok0 + 128],
        sb["cmat"][0:64, tok0:tok0 + 128], start=True, stop=True))
    b.waitv("vector", MM, vg)
    vgt = b.op("vector", EV, lambda E: E.tensor_tensor(
        sb["gtric"][:, :], pb7[:, 256:384], sb["tri_bf"][:, :], OP.mult))
    st8["pb7_free"] = (EV, vgt)

    def cum_load(sg2):
        ch2, q2 = divmod(sg2, 4)
        buf2 = sb["cum_rep"] if sg2 % 2 == 0 else sb["cum_repb"]
        wtok("scalar", cum_wr[ch2])
        wtok("scalar", st8.get(f"cumrep_free{sg2 % 2}"))
        if sg2 == 0:
            b.waitv("scalar", EV, S["ev"].v)
        st8[f"crld{sg2}"] = b.dma("scalar", buf2[:, :],
                    fap(dr["cum_d"], [[0, 128], [128, 16], [1, 128]],
                        off=ch2 * 64 * 128 + q2 * 16 * 128), "ld")
    if "crs" not in st8:
        st8["crs"] = 0
    while st8["crs"] <= min(ch * 4 + 1, 31):
        cum_load(st8["crs"]); st8["crs"] += 1
    ydone = {}
    for q in range(4):
        sg = ch * 4 + q
        crb = sb["cum_rep"] if sg % 2 == 0 else sb["cum_repb"]
        gmbuf = sb["gm"] if sg % 2 == 0 else sb["gmb"]
        wtok("vector", st8[f"crld{sg}"])
        b.waitv("vector", EV, vgt if q == 0 else EV.v)
        vd1 = b.op("vector", EV, lambda E, q=q, crb=crb: E.tensor_tensor(
            crb[:, :], crb[:, :],
            fap(sb["cum_t"], [[64, 128], [1, 16], [0, 128]], off=q * 16), OP.subtract))
        vd2 = b.op("vector", EV, lambda E, crb=crb: E.tensor_scalar(
            crb[:, :], crb[:, :], 0.0, -80.0, OP.min, OP.max))
        b.waitv("scalar", EV, vd2)
        wtok("scalar", st8.get(f"gm_free{sg % 2}"))
        if ch == 0 and q <= 1:
            b.waitv("scalar", EV, S["ev"].v)
        vex = b.op("scalar", SC, lambda E, crb=crb, gmbuf=gmbuf: E.activation(
            gmbuf[:, :], crb[:, :], AF.Exp))
        st8[f"cumrep_free{sg % 2}"] = (SC, vex)
        while st8["crs"] <= min(sg + 2, 31):
            cum_load(st8["crs"]); st8["crs"] += 1
        b.waitv("vector", SC, vex)
        vgm = b.op("vector", EV, lambda E, gmbuf=gmbuf: E.tensor_tensor(
            gmbuf[:, :], gmbuf[:, :],
            fap(sb["gtric"], [[128, 128], [0, 16], [1, 128]]), OP.mult))
        b.waitv("tensor", EV, vgm)
        if q == 0:
            b.waitv("tensor", EV, vdx)
        for gl in (2 * q, 2 * q + 1):
            pbank = pb[YBANKS[gl]] if YBANKS[gl] != 7 else pb7
            wtok("tensor", st8.get(f"yb{YBANKS[gl]}"))
            if gl == 6:
                wtok("tensor", st8.get("pb7_free"))
            if YBANKS[gl] == 5:
                wtok("tensor", st8.get("pb5_rec"))
            if gl == 7:
                wtok("tensor", ydone.get(0))
            def ymm(E, gl=gl, q=q, pbank=pbank, gmbuf=gmbuf):
                last = None
                for hh in range(8):
                    h = gl * 8 + hh
                    half = h % 2
                    pr = (h // 2) % 4
                    last = E.matmul(pbank[half * 64:(half + 1) * 64, pr * 128:(pr + 1) * 128],
                                    sb["dtx"][:, h * 64:(h + 1) * 64],
                                    gmbuf[:, (h - 16 * q) * 128:(h - 16 * q + 1) * 128],
                                    start=(hh < 2), stop=(hh >= 6), skip_group_check=True)
                return last
            vy = b.op("tensor", MM, ymm)
            b.waitv("vector", MM, vy)
            wtok("vector", st8.get("ystage_free"))
            def yev(E, gl=gl, pbank=pbank):
                last = None
                for pr in range(4):
                    hp = gl * 4 + pr
                    last = E.scalar_tensor_tensor(
                        sb["ystage"][:, pr * 128:(pr + 1) * 128],
                        sb["x_conv"][:, hp * 256 + lo2: hp * 256 + lo2 + 128],
                        sb["d_chan"][:, hp:hp + 1],
                        pbank[:, pr * 128:(pr + 1) * 128], OP.mult, OP.add)
                return last
            vye = b.op("vector", EV, yev)
            st8[f"yb{YBANKS[gl]}"] = (EV, vye)
            ydone[gl] = (EV, vye)
            b.wait("sync", EV)
            vst = b.dma("sync", dr["y_d"].ap()[ch, gl, :, :], sb["ystage"][:, :], "st")
            st8["ystage_free"] = vst
            ydc_tok[(ch, gl)] = vst
        st8[f"gm_free{sg % 2}"] = (MM, MM.v)
    if ch == 0:
        tap("ydc", dr["y_d"].ap()[0, :, :, :])

    b.waitv("vector", MM, MM.v)
    vdd = b.op("vector", EV, lambda E: E.tensor_tensor(
        sb["dtx"][:, :], sb["dtx"][:, :],
        fap(sb["dec_tok"], [[64, 128], [1, 64], [0, 64]]), OP.mult))
    st8["xtok_free"] = (EV, vdd)

    wtok("gpsimd", st8.get("srec"))
    vsn = b.dma("gpsimd", dr["snap_d"].ap()[ch, :, :], sb["s_state"][:, :], "st")
    snap_tok[ch] = vsn

    wtok("scalar", st8.get("lam_used"))
    wtok("scalar", cum_wr[ch])
    vlr = b.dma("scalar", sb["lam_rep"][:, :],
                fap(dr["lam_d"], [[0, 64], [1, 64]], off=ch * 64), "ld")
    wtok("scalar", vlr)
    wtok("vector", vlr)
    vcl = b.op("vector", EV, lambda E: E.tensor_scalar_max(sb["lam_rep"][:, :], sb["lam_rep"][:, :], -80.0))
    wtok("scalar", vsn)
    b.waitv("scalar", EV, vcl)
    vle = b.op("scalar", SC, lambda E: E.activation(sb["lam_rep"][:, :], sb["lam_rep"][:, :], AF.Exp))

    b.waitv("tensor", EV, vdd)
    last_rec = None
    for g8 in range(8):
        wtok("tensor", st8.get("pb5_rec"))
        if g8 == 0:
            wtok("tensor", st8.get("yb5"))
        def dsmm(E, g8=g8):
            last = None
            for pr in range(4):
                pair = g8 * 4 + pr
                last = E.matmul(pb[5][0:64, pr * 128:(pr + 1) * 128],
                                sb["b_tok"][:, :], sb["dtx"][:, pair * 128:(pair + 1) * 128],
                                start=(pr == 0), stop=(pr == 3))
            return last
        vds = b.op("tensor", MM, dsmm)
        b.waitv("vector", MM, vds)
        b.waitv("vector", SC, vle)
        sl = sb["s_state"][:, g8 * 512:(g8 + 1) * 512]
        b.op("vector", EV, lambda E, sl=sl, g8=g8: E.tensor_tensor(
            sb["s_tmp"][:, :], sl,
            fap(sb["lam_rep"], [[64, 64], [2, 4], [1, 2], [0, 64]], off=8 * g8), OP.mult))
        last_rec = b.op("vector", EV, lambda E, sl=sl: E.tensor_tensor(
            sl, sb["s_tmp"][:, :], pb[5][0:64, :], OP.add))
        st8["pb5_rec"] = (EV, last_rec)
    st8["srec"] = (EV, last_rec)
    st8["lam_used"] = (EV, last_rec)
    if ch == 0:
        tap("dtxd0", sb["dtx"][:, :])
        tap("lam0", sb["lam_rep"][:, :])
        tap("dec0", sb["dec_tok"][:, :])
        tap("sstate0", sb["s_state"][:, :])
        tap("btok0", sb["b_tok"][:, :])
    if ch == 1:
        tap("sstate1", sb["s_state"][:, :])
    if ch in (2, 3, 4, 5, 6, 7):
        tap(f"sstate{ch}", sb["s_state"][:, :])
    if ch == 2:
        tap("dtxd2", sb["dtx"][:, :])
        tap("dec2", sb["dec_tok"][:, :])
        tap("btok2", sb["b_tok"][:, :])

def emit_post(b, nc, meta, pb, pbt, pb7):
    dr, sb, taps, comm = meta["dr"], meta["sb"], meta["taps"], meta["comm"]
    S = meta["sems"]
    MM, EV, SC, CC = S["mm"], S["ev"], S["sc"], S["cc"]
    mix = meta["_mix"]

    def wtok(eng, tok):
        if tok is not None:
            b.waitv(eng, tok[0], tok[0].v if getattr(tok[0], "q", False) else tok[1])

    def tap(name, src):
        if name in taps:
            b.waitv("gpsimd", S["ev"], S["ev"].v)
            b.waitv("gpsimd", S["sc"], S["sc"].v)
            b.waitv("gpsimd", S["mm"], S["mm"].v)
            b.dma("gpsimd", taps[name].ap(), src, "st")

    tap("bmat", sb["bmat"][:, :])
    tap("cmat", sb["cmat"][:, :])
    tap("canary", sb["canary"][:, :])
    vsf = b.dma("sync", dr["ag2_in"].ap()[:, 0:4096], sb["s_state"][:, :], "st")
    wtok("scalar", mix["vtot"])
    b.waitv("scalar", EV, EV.v)
    vtr = b.dma("scalar", sb["lam_rep"][:, :], fap(dr["tot_d"], [[0, 64], [1, 64]]), "ld")
    wtok("sync", vtr)
    vtf = b.dma("sync", dr["ag2_in"].ap()[:, 4096:4160], sb["lam_rep"][:, :], "st")
    wtok("gpsimd", vsf)
    wtok("gpsimd", vtf)
    if comm:
        b.op("gpsimd", CC, lambda E: E.collective_compute(
            "AllGather", OP.bypass, replica_groups=[list(range(NCORE))],
            ins=[dr["ag2_in"].ap().opt()], outs=[dr["ag2_out"].ap().opt()]))
        ag2_tok = (CC, CC.v)
    else:
        vv = None
        for r in range(NCORE):
            vv = b.dma("gpsimd", dr["ag2_out"].ap()[r * 64:(r + 1) * 64, :], dr["ag2_in"].ap()[:, :], "st")
        ag2_tok = vv
    tap("sstate", sb["s_state"][:, :])

    wtok("sync", ag2_tok)
    vgv = b.dma("sync", sb["coef"][:, :],
                fap(dr["ag2_out"], [[64 * 4160, 8], [1, 64]], off=4096), "ld")
    wtok("tensor", vgv)
    vcm = b.op("tensor", MM, lambda E: E.matmul(pb[0][0:8, 0:64], sb["emask"][:, :],
                                                sb["coef"][:, :], start=True, stop=True))
    b.waitv("scalar", MM, vcm)
    vcc0 = b.op("scalar", SC, lambda E: E.copy(sb["coef"][:, :], pb[0][0:8, 0:64]))
    b.waitv("vector", SC, vcc0)
    vcc1 = b.op("vector", EV, lambda E: E.tensor_scalar_max(sb["coef"][:, :], sb["coef"][:, :], -80.0))
    b.waitv("scalar", EV, vcc1)
    vce = b.op("scalar", SC, lambda E: E.activation(sb["coef"][:, :], sb["coef"][:, :], AF.Exp))
    b.waitv("vector", SC, vce)
    vcm2 = b.op("vector", EV, lambda E: E.tensor_scalar(sb["coef"][:, :], sb["coef"][:, :],
                                                        sb["mmask"][:, :], None, OP.mult))
    b.wait("sync", EV)
    vcd = b.dma("sync", dr["coef_d"].ap(), sb["coef"][:, :], "st")
    wtok("scalar", vcd)
    vcr = b.dma("scalar", sb["coef_rep"][:, :], fap(dr["coef_d"], [[0, 64], [1, 512]]), "ld")

    vz = b.op("vector", EV, lambda E: E.memset(sb["carry"][:, :], 0.0))
    wtok("vector", vcr)
    fin_tok = None
    for j in range(NCORE):
        wtok("sync", fin_tok)
        vfj = b.dma("sync", sb["fin_b"][:, :], dr["ag2_out"].ap()[j * 64:(j + 1) * 64, 0:4096], "ld")
        wtok("vector", vfj)
        b.op("vector", EV, lambda E, j=j: E.tensor_tensor(
            sb["fin_b"][:, :], sb["fin_b"][:, :],
            fap(sb["coef_rep"], [[512, 64], [2, 32], [1, 2], [0, 64]], off=64 * j), OP.mult))
        vca = b.op("vector", EV, lambda E: E.tensor_add(sb["carry"][:, :], sb["carry"][:, :],
                                                        sb["fin_b"][:, :]))
        fin_tok = (EV, vca)
    tap("carry", sb["carry"][:, :])

    st8 = {}
    gate_tok = None
    ydc_tok, snap_tok = mix["ydc_tok"], mix["snap_tok"]
    for ch in range(NCH):
        sp, tok0 = ch // 4, ch * 128
        if ch % 4 == 0:
            wtok("sync", gate_tok)
            vgl = b.dma("sync", sb["gate_sp"][:, :], dr["gate_d"].ap()[sp, :, :, :], "ld")
            st8["gate_ld"] = vgl
        wtok("sync", st8.get("yld_free"))
        vyl = b.dma("sync", sb["y_ld"][:, :],
                    fap(dr["y_d"], [[512, 128], [128 * 512, 8], [1, 512]], off=ch * 8 * 128 * 512), "ld")
        wtok("sync", st8.get("snap_free"))
        wtok("sync", snap_tok[ch])
        vsl = b.dma("sync", sb["snap_ld"][:, :], dr["snap_d"].ap()[ch, :, :], "ld")
        wtok("scalar", mix["vpref"])
        wtok("scalar", st8.get("prep_used"))
        vpl = b.dma("scalar", sb["p_rep"][:, :], fap(dr["pref_d"], [[0, 64], [1, 64]], off=ch * 64), "ld")
        wtok("vector", vpl)
        vclp = b.op("vector", EV, lambda E: E.tensor_scalar_max(sb["p_rep"][:, :], sb["p_rep"][:, :], -80.0))
        wtok("scalar", vpl)
        b.waitv("scalar", EV, vclp)
        vpe = b.op("scalar", SC, lambda E: E.activation(sb["p_rep"][:, :], sb["p_rep"][:, :], AF.Exp))
        b.waitv("vector", SC, vpe)
        wtok("vector", vsl)
        wtok("vector", fin_tok)
        wtok("vector", st8.get("trueS_used"))
        b.op("vector", EV, lambda E: E.tensor_tensor(
            sb["trueS"][:, :], sb["carry"][:, :],
            fap(sb["p_rep"], [[64, 64], [2, 32], [1, 2], [0, 64]]), OP.mult))
        vts = b.op("vector", EV, lambda E: E.tensor_add(sb["trueS"][:, :], sb["trueS"][:, :],
                                                        sb["snap_ld"][:, :]))
        st8["snap_free"] = (EV, vts)
        st8["prep_used"] = (EV, vts)
        def cr2_load(sg2):
            ch2, q2 = divmod(sg2, 4)
            buf2 = sb["cum_rep2"] if sg2 % 2 == 0 else sb["cum_rep2b"]
            wtok("scalar", st8.get(f"cr2_free{sg2 % 2}"))
            if sg2 == 0:
                b.waitv("scalar", MM, MM.v)
                b.waitv("scalar", EV, S["ev"].v)
            st8[f"cr2ld{sg2}"] = b.dma("scalar", buf2[:, :],
                         fap(dr["cum_d"], [[0, 128], [128, 16], [1, 128]],
                             off=ch2 * 64 * 128 + q2 * 16 * 128), "ld")
        if "c2s" not in st8:
            st8["c2s"] = 0
        while st8["c2s"] <= min(ch * 4 + 1, 31):
            cr2_load(st8["c2s"]); st8["c2s"] += 1
        wtok("vector", vyl)
        for q in range(4):
            sg = ch * 4 + q
            crb2 = sb["cum_rep2"] if sg % 2 == 0 else sb["cum_rep2b"]
            exb = sb["expc"]
            wtok("vector", st8[f"cr2ld{sg}"])
            vclq = b.op("vector", EV, lambda E, crb2=crb2: E.tensor_scalar_max(
                crb2[0:64, :], crb2[0:64, :], -80.0))
            wtok("scalar", st8.get("expc_used"))
            b.waitv("scalar", EV, vclq)
            vex = b.op("scalar", SC, lambda E, crb2=crb2, exb=exb: E.activation(
                exb[:, :], crb2[0:64, :], AF.Exp))
            st8[f"cr2_free{sg % 2}"] = (SC, vex)
            while st8["c2s"] <= min(sg + 2, 31):
                cr2_load(st8["c2s"]); st8["c2s"] += 1
            b.waitv("vector", SC, vex)
            wtok("vector", st8.get("ctil_used"))
            vct = b.op("vector", EV, lambda E, tok0=tok0, exb=exb: E.tensor_tensor(
                sb["ctil"][:, :],
                fap(sb["cmat"], [[T, 64], [0, 16], [1, 128]], off=tok0),
                exb[:, :], OP.mult))
            b.waitv("tensor", EV, vct)
            if q == 0:
                b.waitv("tensor", EV, vts)
            for gl in (2 * q, 2 * q + 1):
                pbank = pb[YBANKS[gl]] if YBANKS[gl] != 7 else pb7
                wtok("tensor", st8.get(f"pyb{YBANKS[gl]}"))
                if gl == 7:
                    wtok("tensor", st8.get("pyb0"))
                def ymm2(E, gl=gl, q=q, pbank=pbank):
                    last = None
                    for hh in range(8):
                        h = gl * 8 + hh
                        half, pr = h % 2, (h // 2) % 4
                        last = E.matmul(pbank[half * 64:(half + 1) * 64, pr * 128:(pr + 1) * 128],
                                        sb["trueS"][0:64, h * 64:(h + 1) * 64],
                                        sb["ctil"][0:64, (h - 16 * q) * 128:(h - 16 * q + 1) * 128],
                                        start=(hh < 2), stop=(hh >= 6), skip_group_check=True)
                    return last
                vy2 = b.op("tensor", MM, ymm2)
                b.waitv("vector", MM, vy2)
                def yadd(E, gl=gl, pbank=pbank):
                    last = None
                    for pr in range(4):
                        hp = gl * 4 + pr
                        sl = sb["y_ld"][:, hp * 128:(hp + 1) * 128]
                        last = E.tensor_tensor(sl, sl, pbank[:, pr * 128:(pr + 1) * 128], OP.add)
                    return last
                vya = b.op("vector", EV, yadd)
                st8[f"pyb{YBANKS[gl]}"] = (EV, vya)
            st8["expc_used"] = (EV, EV.v)
            st8["ctil_used"] = (MM, MM.v)
        st8["trueS_used"] = (MM, MM.v)
        if ch == 0:
            tap("yld", sb["y_ld"][:, :])
        wtok("scalar", st8.get("yld_free"))
        sqv = []
        for hp in range(KC_I):
            sq = sb["sq_st"][:, (hp % 2) * 516:(hp % 2) * 516 + 128]
            if hp >= 2:
                b.waitv("scalar", MM, sqv[hp - 2])
            vsq = b.op("scalar", SC, lambda E, hp=hp, sq=sq: E.activation(
                sq, sb["y_ld"][:, hp * 128:(hp + 1) * 128], AF.Square))
            b.waitv("tensor", SC, vsq)
            if hp == 0:
                wtok("tensor", st8.get("pyb1"))
            vms = b.op("tensor", MM, lambda E, hp=hp, sq=sq: E.matmul(
                pb[1][:, 0:128], sb["ones_bf"][:, :], sq, start=(hp == 0), stop=(hp == KC_I - 1)))
            sqv.append(vms)
        b.waitv("scalar", MM, sqv[-1])
        v1 = b.op("scalar", SC, lambda E: E.activation(sb["stat_a"][:, 0:128], pb[1][:, 0:128],
                                                       AF.Copy, scale=sb["cst"][:, 5:6]))
        b.op("scalar", SC, lambda E: E.activation(sb["stat_b"][:, 0:128], sb["stat_a"][:, 0:128],
                                                  AF.Ln, bias=sb["cst"][:, 1:2]))
        vr = b.op("scalar", SC, lambda E: E.activation(sb["stat_c"][:, 0:128], sb["stat_b"][:, 0:128],
                                                       AF.Exp, scale=sb["cst"][:, 2:3]))
        st8["pyb1"] = (SC, v1)
        b.waitv("vector", SC, vr)
        wtok("vector", st8["gate_ld"])
        if ch == 0:
            tap("gatesp", sb["gate_sp"][:, :])
            tap("rrms", sb["stat_c"][:, 0:128])
        vlast = None
        for hp in range(KC_I):
            b.op("vector", EV, lambda E, hp=hp: E.scalar_tensor_tensor(
                sb["ystage"][:, 0:128], sb["y_ld"][:, hp * 128:(hp + 1) * 128],
                sb["norm_w"][:, hp:hp + 1], sb["stat_c"][:, 0:128], OP.mult, OP.mult))
            vlast = b.op("vector", EV, lambda E, hp=hp, ch=ch, sp=sp: E.tensor_tensor(
                sb["y_g"][:, hp * T + ch * 128: hp * T + (ch + 1) * 128],
                sb["ystage"][:, 0:128],
                sb["gate_sp"][:, hp * 512 + (ch % 4) * 128: hp * 512 + (ch % 4) * 128 + 128],
                OP.mult))
        st8["yld_free"] = (EV, vlast)
        if ch == 3:
            gate_tok = (EV, vlast)
    tap("y_g", sb["y_g"][:, :])
    meta["_post"] = dict(ygdone=(S["ev"], S["ev"].v))

def emit_tail(b, nc, meta, pb, pbt, pb7):
    dr, sb, taps, comm = meta["dr"], meta["sb"], meta["taps"], meta["comm"]
    S = meta["sems"]
    MM, EV, SC, CC = S["mm"], S["ev"], S["sc"], S["cc"]
    def wtok(eng, tok):
        if tok is not None:
            b.waitv(eng, tok[0], tok[0].v if getattr(tok[0], "q", False) else tok[1])
    def tap(name, src):
        if name in taps:
            b.waitv("gpsimd", S["ev"], S["ev"].v)
            b.waitv("gpsimd", S["sc"], S["sc"].v)
            b.waitv("gpsimd", S["mm"], S["mm"].v)
            b.dma("gpsimd", taps[name].ap(), src, "st")
    wt = [sb["wq"], sb["wq2"]]
    ygdone = meta["_post"]["ygdone"]

    wv = {}
    b.waitv("gpsimd", MM, MM.v)
    for h in range(2):
        wv[(0, h)] = b.dma("gpsimd", wt[h][:, :], dr["w_out"].ap()[0, :, h * 16:(h + 1) * 16, :], "ld")
    bank_tok = {}
    mo_ev = None
    for oc in range(16):
        pN = (pb[2], pb[3]) if oc % 2 == 0 else (pb[4], pb[5])
        wtok("tensor", wv[(oc, 1)])
        if oc == 0:
            wtok("tensor", ygdone)
        wtok("tensor", bank_tok.get(oc % 2))
        def omm(E, pN=pN):
            last = None
            for kc in range(KC_I):
                lw = wt[kc // 16][:, (kc % 16) * 128:(kc % 16 + 1) * 128]
                for nt in range(2):
                    last = E.matmul(pN[nt][:, :], lw,
                                    sb["y_g"][:, kc * T + nt * 512: kc * T + nt * 512 + 512],
                                    start=(kc == 0), stop=(kc == KC_I - 1))
            return last
        vo = b.op("tensor", MM, omm)
        if oc + 1 < 16:
            b.waitv("gpsimd", MM, vo)
            for h in range(2):
                wv[(oc + 1, h)] = b.dma("gpsimd", wt[h][:, :],
                                        dr["w_out"].ap()[oc + 1, :, h * 16:(h + 1) * 16, :], "ld")
        b.waitv("scalar", MM, vo)
        if oc == 0:
            wtok("scalar", ygdone)
        for nt in range(2):
            mo_ev = b.op("scalar", SC, lambda E, oc=oc, nt=nt, pN=pN: E.copy(
                sb["mo_bf"][:, oc * T + nt * 512: oc * T + nt * 512 + 512], pN[nt][:, :]))
        bank_tok[oc % 2] = (SC, mo_ev)
    tap("mo", sb["mo_bf"][:, :])

    def ln_stats(src_getter, nseg, kcn, dsize, pm, pq, csteps, cast_src=False):
        cstsc = sb["cst"][:, 4:5]
        """generic feature-major LN stats: returns per-seg tokens; stats in
        stat_a (mean) and stat_c (r), segment-sliced."""
        toks = []
        for si in range(nseg):
            lo = si * 512
            sqv = []
            for kc in range(kcn):
                xs = src_getter(kc, lo)
                sq = sb["sq_st"][:, (kc % 2) * 516:(kc % 2) * 516 + 512]
                if kc >= 2:
                    b.waitv("scalar", MM, sqv[kc - 2])
                if cast_src:
                    if kc >= 1:
                        b.waitv("scalar", MM, sqv[kc - 1])
                    b.op("scalar", SC, lambda E, xs=xs: E.activation(sb["ystage"][:, 0:512], xs, AF.Copy))
                    xs = sb["ystage"][:, 0:512]
                vsc = b.op("scalar", SC, lambda E, xs=xs, sq=sq: E.activation(sq, xs, AF.Square))
                b.waitv("tensor", SC, vsc)
                vm = b.op("tensor", MM, lambda E, xs=xs, sq=sq, kc=kc:
                          [E.matmul(pm[:, 0:512], sb["ones_bf"][:, :], xs,
                                    start=(kc == 0), stop=(kc == kcn - 1)),
                           E.matmul(pq[:, 0:512], sb["ones_bf"][:, :], sq,
                                    start=(kc == 0), stop=(kc == kcn - 1))][-1])
                sqv.append(vm)
            sa = sb["stat_a"][:, lo:lo + 512]; sb_ = sb["stat_b"][:, 0:512]
            sc_ = sb["stat_c"][:, lo:lo + 512]
            b.waitv("scalar", MM, sqv[-1])
            b.op("scalar", SC, lambda E, sa=sa: E.activation(sa, pm[:, 0:512], AF.Copy, scale=cstsc))
            v2 = b.op("scalar", SC, lambda E, sc_=sc_: E.activation(sc_, pq[:, 0:512], AF.Copy, scale=cstsc))
            b.waitv("vector", SC, v2)
            b.op("vector", EV, lambda E, sa=sa, sb_=sb_: E.tensor_mul(sb_, sa, sa))
            v3 = b.op("vector", EV, lambda E, sc_=sc_, sb_=sb_: E.tensor_sub(sc_, sc_, sb_))
            b.waitv("scalar", EV, v3)
            b.op("scalar", SC, lambda E, sb_=sb_, sc_=sc_: E.activation(sb_, sc_, AF.Ln, bias=csteps))
            v4 = b.op("scalar", SC, lambda E, sc_=sc_, sb_=sb_: E.activation(sc_, sb_, AF.Exp, scale=sb["cst"][:, 2:3]))
            toks.append(v4)
        return toks
    wtok("scalar", bank_tok.get(0))
    wtok("scalar", bank_tok.get(1))
    wtok("tensor", bank_tok.get(0))
    wtok("tensor", bank_tok.get(1))
    ln2 = ln_stats(lambda kc, lo: sb["mo_bf"][:, kc * T + lo: kc * T + lo + 512],
                   2, KC_H, HID, pb[0], pb[1], sb["cst"][:, 0:1])
    xv = {}
    x2_tr = {}
    for kc in range(KC_H):
        wtok("sync", x2_tr.get(kc - 1))
        if kc < 2:
            wtok("sync", ygdone)
            b.waitv("sync", MM, MM.v)
        xv[kc] = b.dma("sync", sb["xstage"][:, :],
                       fap(dr["x_fm"], [[T + 128, 128], [1, T]], off=kc * 128 * (T + 128) + 128), "ld")
        wtok("vector", xv[kc])
        if kc == 0:
            b.waitv("vector", SC, ln2[-1])
        vlast = None
        for si in range(2):
            lo = si * 512
            mo = sb["mo_bf"][:, kc * T + lo: kc * T + lo + 512]
            b.op("vector", EV, lambda E, mo=mo, lo=lo: E.tensor_sub(
                sb["y_tmp"][:, :], mo, sb["stat_a"][:, lo:lo + 512]))
            b.op("vector", EV, lambda E, lo=lo: E.tensor_mul(
                sb["y_tmp"][:, :], sb["y_tmp"][:, :], sb["stat_c"][:, lo:lo + 512]))
            vlast = b.op("vector", EV, lambda E, kc=kc, lo=lo: E.tensor_add(
                sb["x2_fm"][:, kc * T + lo: kc * T + lo + 512],
                sb["xstage"][:, lo:lo + 512], sb["y_tmp"][:, :]))
        x2_tr[kc] = (EV, vlast)
    tap("x2", sb["x2_fm"][:, :])
    xtd = None
    for kc in range(KC_H):
        for wv2 in range(2):
            wtok("tensor", x2_tr[kc])
            wtok("tensor", xtd)
            vt = b.op("tensor", MM, lambda E, kc=kc, wv2=wv2: [
                E.transpose(pb7[:, i * 128:(i + 1) * 128],
                            sb["x2_fm"][:, kc * T + (wv2 * 4 + i) * 128: kc * T + (wv2 * 4 + i + 1) * 128],
                            sb["ident_f"][:, :]) for i in range(4)][-1])
            b.waitv("vector", MM, vt)
            ve = b.op("vector", EV, lambda E: E.tensor_copy(sb["fstage2"][:, 0:512], pb7[:, :]))
            b.wait("sync", EV)
            vst = b.dma("sync",
                        fap(dr["x2_d"], [[HID, 128], [128 * HID, 4], [1, 128]],
                            off=wv2 * 4 * 128 * HID + kc * 128),
                        sb["fstage2"][:, 0:512], "st")
            xtd = vst
    x2d_tok = xtd
    if "x2d" in taps:
        wtok("gpsimd", x2d_tok)
        b.dma("gpsimd", taps["x2d"].ap(), dr["x2_d"].ap()[:, :, :], "st")
    b.waitv("scalar", EV, x2_tr[KC_H - 1][1])
    b.waitv("tensor", EV, x2_tr[KC_H - 1][1])
    sm_g = b.qsems[("gpsimd", "st")]
    b.waitv("vector", sm_g, sm_g.v)
    ln3 = ln_stats(lambda kc, lo: sb["x2_fm"][:, kc * T + lo: kc * T + lo + 512],
                   2, KC_H, HID, pb[0], pb[1], sb["cst"][:, 0:1], cast_src=True)
    b.waitv("vector", SC, ln3[-1])
    b.waitv("vector", MM, MM.v)
    for kc in range(KC_H):
        for si in range(2):
            lo = si * 512
            x2s = sb["x2_fm"][:, kc * T + lo: kc * T + lo + 512]
            b.op("vector", EV, lambda E, x2s=x2s, lo=lo: E.tensor_sub(
                sb["y_tmp"][:, :], x2s, sb["stat_a"][:, lo:lo + 512]))
            b.op("vector", EV, lambda E, kc=kc, lo=lo: E.tensor_mul(
                sb["xhat2"][:, kc * T + lo: kc * T + lo + 512],
                sb["y_tmp"][:, :], sb["stat_c"][:, lo:lo + 512]))
    tap("xhat2", sb["xhat2"][:, :])
    xh2_tok = (EV, S["ev"].v)

    wt4 = [sb["wq"], sb["wq2"], sb["wq3"], sb["wq4"]]
    ag1 = meta["_mix"]["ag1_tok"]
    wtok("vector", x2d_tok)
    halves = [(0, 22), (22, 21)]
    acc_tok = None
    for hi, (k0, nk) in enumerate(halves):
        guv = {}
        wtok("sync", ag1)
        b.waitv("sync", MM, MM.v)
        guv[0] = (b.dma("sync", wt4[0][:, :], dr["ag1g_out"].ap()[k0, :, :, :], "ld"),
                  b.dma("sync", wt4[1][:, :], dr["ag1u_out"].ap()[k0, :, :, :], "ld"))
        h_tok = {}
        vgu_prev = None
        for ol in range(nk):
            oc = k0 + ol
            pg, pu = (pb[0], pb[2]), (pb[1], pb[3])
            bg = wt4[(ol % 2) * 2]; bu = wt4[(ol % 2) * 2 + 1]
            wtok("tensor", guv[ol][1])
            wtok("tensor", h_tok.get(ol - 1))
            if hi == 0 and ol == 0:
                wtok("tensor", xh2_tok)
            def gmm(E, bg=bg, bu=bu):
                last = None
                for kc in range(KC_H):
                    for nt in range(2):
                        xs = sb["xhat2"][:, kc * T + nt * 512: kc * T + nt * 512 + 512]
                        E.matmul(pb[nt][:, :], bg[:, kc * 128:(kc + 1) * 128], xs,
                                 start=(kc == 0), stop=(kc == KC_H - 1))
                        last = E.matmul(pb[2 + nt][:, :], bu[:, kc * 128:(kc + 1) * 128], xs,
                                        start=(kc == 0), stop=(kc == KC_H - 1))
                return last
            vgu = b.op("tensor", MM, gmm)
            if ol + 1 < nk:
                b.waitv("sync", MM, vgu if ol == 0 else vgu_prev)
                guv[ol + 1] = (
                    b.dma("sync", wt4[((ol + 1) % 2) * 2][:, :], dr["ag1g_out"].ap()[oc + 1, :, :, :], "ld"),
                    b.dma("sync", wt4[((ol + 1) % 2) * 2 + 1][:, :], dr["ag1u_out"].ap()[oc + 1, :, :, :], "ld"))
            vgu_prev = vgu
            b.waitv("scalar", MM, vgu)
            if ol >= 1:
                wtok("scalar", h_tok[ol - 1])
            vh = None
            for nt in range(2):
                vsl = b.op("scalar", SC, lambda E, nt=nt: E.activation(
                    sb["sq_st"][:, nt * 516: nt * 516 + 512], pb[nt][:, :], AF.Silu))
                b.waitv("vector", SC, vsl)
                vh = b.op("vector", EV, lambda E, ol=ol, nt=nt: E.tensor_tensor(
                    sb["hmat"][:, ol * T + nt * 512: ol * T + nt * 512 + 512],
                    sb["sq_st"][:, nt * 516: nt * 516 + 512], pb[2 + nt][:, :], OP.mult))
            h_tok[ol] = (EV, vh)
            if hi == 0 and ol == 0:
                tap("h0", sb["hmat"][:, 0:T])
                tap("wg0", wt4[0][:, :])
        hdone = (EV, S["ev"].v)
        steps = [(cq, tbg, kl) for cq in range(4) for tbg in range(2) for kl in range(nk)]
        NS = len(steps)
        dq = lambda s: "sync" if s % 2 == 0 else "scalar"
        def dload(s):
            cq, tbg, kl = steps[s]
            return b.dma(dq(s), wt4[s % 4][:, 0:512],
                         dr["ag1d_out"].ap()[k0 + kl, cq, :, :], "ld")
        dv = {}
        vdm_tok = {}
        b.waitv("sync", MM, MM.v)
        b.waitv("scalar", MM, MM.v)
        for s in range(min(4, NS)):
            dv[s] = dload(s)
        ev_tok = None
        for s, (cq, tbg, kl) in enumerate(steps):
            wb = wt4[s % 4]
            wtok("tensor", dv[s])
            if kl == 0:
                if s == 0:
                    wtok("tensor", hdone)
                wtok("tensor", acc_tok)
            def dmm(E, wb=wb, kl=kl, tbg=tbg):
                last = None
                for tl in range(4):
                    tb = tbg * 4 + tl
                    last = E.matmul(pb[tl][:, :],
                                    sb["hmat"][:, kl * T + tb * 128:(kl * T + tb * 128) + 128],
                                    wb[:, 0:512], start=(kl == 0), stop=(kl == nk - 1))
                return last
            vdm = b.op("tensor", MM, dmm)
            vdm_tok[s] = vdm
            if s + 4 < NS:
                b.waitv(dq(s + 4), MM, vdm)
                dv[s + 4] = dload(s + 4)
            if kl == nk - 1:
                for tl in range(4):
                    tb = tbg * 4 + tl
                    dst = sb["out_acc"][:, tb * HID + cq * 512: tb * HID + cq * 512 + 512]
                    if hi == 0:
                        b.waitv("scalar", MM, vdm)
                        ev_tok = b.op("scalar", SC, lambda E, dst=dst, tl=tl: E.copy(dst, pb[tl][:, :]))
                    else:
                        b.waitv("vector", MM, vdm)
                        ev_tok = b.op("vector", EV, lambda E, dst=dst, tl=tl: E.tensor_add(
                            dst, dst, pb[tl][:, :]))
                acc_tok = ((SC, ev_tok) if hi == 0 else (EV, ev_tok))
    tap("outacc", sb["out_acc"][:, :])

    if "outacc" in taps:
        sg2 = b.qsems[("gpsimd", "st")]
        b.waitv("vector", sg2, sg2.v)
    x2l = None
    for tb in range(8):
        wtok("sync", x2d_tok)
        wtok("sync", x2l)
        vxf = b.dma("sync", sb["fstage"][:, :], dr["x2_d"].ap()[tb, :, :], "ld")
        osl = sb["out_acc"][:, tb * HID: (tb + 1) * HID]
        wtok("vector", acc_tok)
        vm_ = b.op("vector", EV, lambda E, osl=osl: E.reduce_sum(
            sb["stat_b"][:, 16:17], osl, mybir.AxisListType.X))
        b.waitv("vector", EV, vm_)
        vsq = None
        for si in range(4):
            if tb == 0 and si == 0:
                b.waitv("scalar", EV, vm_)
            vsq = b.op("scalar", SC, lambda E, osl=osl, si=si: E.activation(
                sb["sq_big"][:, si * 512:(si + 1) * 512],
                osl[:, si * 512:(si + 1) * 512], AF.Square))
        b.waitv("vector", SC, vsq)
        vq2b = b.op("vector", EV, lambda E: E.reduce_sum(
            sb["stat_b"][:, 8:9], sb["sq_big"][:, 0:2048], mybir.AxisListType.X))
        b.waitv("vector", EV, vq2b)
        b.waitv("scalar", EV, vq2b)
        vt1 = b.op("vector", EV, lambda E: E.tensor_scalar_mul(
            sb["stat_b"][:, 16:17], sb["stat_b"][:, 16:17], 1.0 / HID))
        b.waitv("vector", EV, vt1)
        vt2 = b.op("vector", EV, lambda E: E.tensor_scalar_mul(
            sb["stat_b"][:, 8:9], sb["stat_b"][:, 8:9], 1.0 / HID))
        b.waitv("vector", EV, vt2)
        vt3 = b.op("vector", EV, lambda E: E.tensor_mul(
            sb["stat_b"][:, 10:11], sb["stat_b"][:, 16:17], sb["stat_b"][:, 16:17]))
        b.waitv("vector", EV, vt3)
        vv2 = b.op("vector", EV, lambda E: E.tensor_sub(
            sb["stat_b"][:, 8:9], sb["stat_b"][:, 8:9], sb["stat_b"][:, 10:11]))
        b.waitv("scalar", EV, vv2)
        b.op("scalar", SC, lambda E: E.activation(
            sb["stat_b"][:, 9:10], sb["stat_b"][:, 8:9], AF.Ln, bias=sb["cst"][:, 0:1]))
        vr = b.op("scalar", SC, lambda E: E.activation(
            sb["stat_b"][:, 17:18], sb["stat_b"][:, 9:10], AF.Exp, scale=sb["cst"][:, 2:3]))
        b.waitv("vector", SC, vr)
        if "ln4" in taps:
            b.waitv("gpsimd", SC, vr)
            b.waitv("gpsimd", EV, S["ev"].v)
            b.dma("gpsimd", taps["ln4"].ap()[tb], sb["stat_b"][:, 0:24], "st")
            sgq = b.qsems[("gpsimd", "st")]
            b.waitv("vector", sgq, sgq.v)
        wtok("vector", vxf)
        vap1 = b.op("vector", EV, lambda E, osl=osl: E.tensor_scalar(
            osl, osl, sb["stat_b"][:, 16:17], sb["stat_b"][:, 17:18], OP.subtract, OP.mult))
        b.waitv("vector", EV, vap1)
        b.op("vector", EV, lambda E, osl=osl: E.tensor_add(osl, osl, sb["fstage"][:, :]))
        b.wait("sync", EV)
        vout = b.dma("sync", dr["out"].ap()[tb * 128:(tb + 1) * 128, :], osl, "st")
        x2l = vout
    b.waitv("gpsimd", b.qsems[("sync", "st")], b.qsems[("sync", "st")].v)
    b.wait("gpsimd", CC)

    for qn in ("sync", "scalar", "gpsimd"):
        for kk in ("ld", "st"):
            sm = b.qsems[(qn, kk)]
            b.waitv("sync", sm, sm.v)
            b.waitv("gpsimd", sm, sm.v)
    b.waitv("gpsimd", CC, CC.v)
    b.waitv("sync", CC, CC.v)

def finish_build(nc, meta):
    pb = [nc.alloc_psum_tensor(f"pb{i}", [128, 512], F32) for i in range(6)]
    pbt = nc.alloc_psum_tensor("pbt", [128, 1024], BF16)
    pb7 = nc.alloc_psum_tensor("pb7", [128, 512], F32)
    bld = Bld(nc, meta["sems"], meta["taps"])
    def body(b):
        emit_program(b, nc, meta, pb, pbt, pb7, None)
        emit_post(b, nc, meta, pb, pbt, pb7)
        emit_tail(b, nc, meta, pb, pbt, pb7)
    with nc.Block() as block:
        block.sync(lambda e: bld.walk("sync", body))
        block.scalar(lambda e: bld.walk("scalar", body))
        block.vector(lambda e: bld.walk("vector", body))
        block.tensor(lambda e: bld.walk("tensor", body))
        block.gpsimd(lambda e: bld.walk("gpsimd", body))
    return nc

_BUILT = {}

def kernel(**inputs):
    cores = prep_inputs(inputs)
    key = "main"
    if key not in _BUILT:
        nc, meta = build_nc(comm=True)
        finish_build(nc, meta)
        _BUILT[key] = nc
    nc = _BUILT[key]
    res = run_bass_kernel_spmd(nc, cores, core_ids=list(range(NCORE)))
    out = np.empty((2, 4096, HID), np.float32)
    for c in range(NCORE):
        bt, s = c // 4, c % 4
        out[bt, s * T:(s + 1) * T] = res.results[c]["out"]
    return out

# version 16 (final)
# speedup vs baseline: 1.2553x; 1.0044x over previous
import numpy as np
import ml_dtypes

import concourse.bass as bass
import concourse.mybir as mybir
from concourse.bass_utils import run_bass_kernel_spmd
from concourse.ap import AP

F32 = mybir.dt.float32
BF16 = mybir.dt.bfloat16
I32 = mybir.dt.int32
OP = mybir.AluOpType
AF = mybir.ActivationFunctionType
BFNP = ml_dtypes.bfloat16

HID = 2048; KC_H = 16
INTER = 4096; KC_I = 32
HEADS = 64; HD = 64; STN = 64
CONV = 4224; OC_CONV = 33
KCONV = 4
OC_P = 66; OC_DT = 33; OC_GATE0 = 34
T = 1024; TS = 512; NCH = 8; CH = 128
FI = 5504; FI_PAD = 6144; KC_F = 48; KC_FR = 43; PCF = 6
NCORE = 8
LN_EPS = 1e-5; RMS_EPS = 1e-6
ENGS = ("sync", "scalar", "vector", "tensor", "gpsimd")


def _tile_w_in(in_proj_w):
    W = np.asarray(in_proj_w, np.float32)
    perm = np.empty((OC_P * 128, HID), np.float32)
    perm[:CONV] = W[INTER:INTER + CONV]
    perm[CONV:CONV + 64] = W[INTER + CONV:]
    perm[CONV + 64:CONV + 128] = 0.0
    perm[CONV + 128:] = W[:INTER]
    return np.ascontiguousarray(perm.reshape(OC_P, 128, KC_H, 128).transpose(0, 3, 2, 1))

def _tile_w_out(out_proj_w):
    W = np.asarray(out_proj_w, np.float32)
    return np.ascontiguousarray(W.reshape(16, 128, KC_I, 128).transpose(0, 3, 2, 1))

def _tile_gu_idx(idx):
    p = np.zeros((FI_PAD, HID), np.int32)
    p[:FI] = np.asarray(idx, np.int32)
    return np.ascontiguousarray(p.reshape(KC_F, 128, KC_H, 128).transpose(0, 3, 2, 1))

def _tile_d_idx(idx):
    p = np.zeros((HID, FI_PAD), np.int32)
    p[:, :FI] = np.asarray(idx, np.int32)
    t = np.ascontiguousarray(p.T).reshape(KC_F, 128, 4, 512).transpose(0, 2, 1, 3)
    return np.ascontiguousarray(t)

def _conv_diag(conv_w):
    w = np.asarray(conv_w, np.float32)
    d = np.zeros((OC_CONV, KCONV, 128, 128), np.float32)
    ar = np.arange(128)
    for fc in range(OC_CONV):
        for k in range(KCONV):
            d[fc, k, ar, ar] = w[fc * 128:(fc + 1) * 128, k]
    return d

def prep_inputs(inputs):
    x = np.asarray(inputs["L0_attn_hidden_states"], np.float32)
    com = {}
    com["w_in"] = _tile_w_in(inputs["L0_attn_in_proj_w"])
    com["w_out"] = _tile_w_out(inputs["L0_attn_out_proj_w"])
    com["conv_diag"] = _conv_diag(inputs["L0_attn_conv_w"])
    com["conv_b"] = np.ascontiguousarray(
        np.asarray(inputs["L0_attn_conv_b"], np.float32).reshape(OC_CONV, 128).T)
    com["dt_bias"] = np.asarray(inputs["L0_attn_dt_bias"], np.float32).reshape(64, 1)
    com["a_log"] = np.asarray(inputs["L0_attn_A_log"], np.float32).reshape(64, 1)
    com["d_chan"] = np.ascontiguousarray(
        np.repeat(np.asarray(inputs["L0_attn_D"], np.float32), HD).reshape(KC_I, 128).T)
    com["norm_w"] = np.ascontiguousarray(
        np.asarray(inputs["L0_attn_norm_weight"], np.float32).reshape(KC_I, 128).T)
    com["lut"] = np.asarray(inputs["L0_ffn_lut"], np.float32).reshape(1, 16)
    gi = _tile_gu_idx(inputs["L0_ffn_gate_idx"])
    ui = _tile_gu_idx(inputs["L0_ffn_up_idx"])
    di = _tile_d_idx(inputs["L0_ffn_down_idx"])
    com["ident_bf"] = np.eye(128, dtype=BFNP)
    com["ident_f"] = np.eye(128, dtype=np.float32)
    com["ones_bf"] = np.ones((128, 128), dtype=BFNP)
    com["tri_bf"] = (np.arange(128)[:, None] <= np.arange(128)[None, :]).astype(BFNP)
    cores = []
    for c in range(NCORE):
        b, s = c // 4, c % 4
        t0 = s * T
        xf = np.zeros((HID, T + 128), np.float32)
        lo = max(0, t0 - 3)
        seg = x[b, lo:t0 + T].T
        xf[:, (T + 128) - seg.shape[1]:] = seg
        E = np.zeros((8, 8), np.float32)
        m = np.zeros((8, 1), np.float32)
        for j in range(4 * b, 4 * b + s):
            m[j] = 1.0
            for k in range(j + 1, 4 * b + s):
                E[j, k] = 1.0
        d = dict(com)
        d["x_fm"] = np.ascontiguousarray(xf)
        d["gidx"] = np.ascontiguousarray(gi[PCF * c:PCF * (c + 1)])
        d["uidx"] = np.ascontiguousarray(ui[PCF * c:PCF * (c + 1)])
        d["didx"] = np.ascontiguousarray(di[PCF * c:PCF * (c + 1)])
        d["emask"] = np.ascontiguousarray(E.T)
        d["mmask"] = m
        cores.append(d)
    return cores


class Sem:
    __slots__ = ("h", "v", "q")
    def __init__(self, h, q=False):
        self.h = h; self.v = 0; self.q = q

class Bld:
    def __init__(self, nc, sems, taps):
        self.nc = nc
        self.sems = sems
        self.taps = taps
        self.qsems = {(q, kk): sems[f"q_{q}_{kk}"] for q in ("sync", "scalar", "gpsimd")
                      for kk in ("ld", "st")}
        self.e = None
        self.eng = None
    def walk(self, eng_name, fn):
        self.e = eng_name
        self.eng = getattr(self.nc, eng_name)
        for s in self.sems.values():
            s.v = 0
        fn(self)
    def _go(self, eng):
        return self.e == eng
    def wait(self, eng, sem, val=None):
        if self._go(eng):
            self.eng.wait_ge(sem.h, sem.v if val is None else val)
    def waitv(self, eng, sem, val):
        if self._go(eng):
            self.eng.wait_ge(sem.h, val)
    def dma(self, q, out, in_, kind="ld", **kw):
        sem = self.qsems[(q, "st" if kind == "st" else "ld")]
        v = sem.v = sem.v + 16
        if self._go(q):
            self.eng.dma_start(out=out, in_=in_, **kw).then_inc(sem.h, 16)
        return (sem, v)
    def op(self, eng, sem, f, n=1):
        v = None
        if sem is not None:
            v = sem.v = sem.v + n
        if self._go(eng):
            ins = f(self.eng)
            if sem is not None:
                ins.then_inc(sem.h, n)
        return v
    def ops(self, eng, f):
        if self._go(eng):
            f(self.eng)

def ap_of(t):
    return t.ap() if hasattr(t, "ap") else t

def build_nc(comm=True, taps=()):
    nc = bass.Bass("TRN2", target_bir_lowering=False, detect_race_conditions=False)
    dr = {}
    def din(name, shape, dt):
        dr[name] = nc.dram_tensor(name, list(shape), dt, kind="ExternalInput")
    def dint(name, shape, dt, **kw):
        dr[name] = nc.dram_tensor(name, list(shape), dt, kind="Internal", **kw)
    din("x_fm", (HID, T + 128), F32)
    din("w_in", (OC_P, 128, KC_H, 128), F32)
    din("w_out", (16, 128, KC_I, 128), F32)
    din("conv_diag", (OC_CONV, KCONV, 128, 128), F32)
    din("conv_b", (128, OC_CONV), F32)
    din("dt_bias", (64, 1), F32)
    din("a_log", (64, 1), F32)
    din("d_chan", (128, KC_I), F32)
    din("norm_w", (128, KC_I), F32)
    din("lut", (1, 16), F32)
    din("gidx", (PCF, 128, KC_H, 128), I32)
    din("uidx", (PCF, 128, KC_H, 128), I32)
    din("didx", (PCF, 4, 128, 512), I32)
    din("ident_bf", (128, 128), BF16)
    din("ident_f", (128, 128), F32)
    din("ones_bf", (128, 128), BF16)
    din("tri_bf", (128, 128), BF16)
    din("emask", (8, 8), F32)
    din("mmask", (8, 1), F32)
    dr["out"] = nc.dram_tensor("out", [T, HID], F32, kind="ExternalOutput")

    dint("gate_d", (2, 128, 32, TS), BF16)
    dint("hbc_d", (128, OC_CONV, TS + 3), BF16)
    dint("y_d", (NCH, 8, 128, 512), BF16)
    dint("snap_d", (NCH, 64, 4096), BF16)
    dint("cum_d", (NCH, 64, 128), F32)
    dint("x2_d", (8, 128, HID), F32)
    dint("pref_d", (8, 64), F32)
    dint("lam_d", (NCH, 64), F32)
    dint("tot_d", (64, 1), F32)
    dint("coef_d", (8, 64), F32)
    dint("ag1g_in", (PCF, 128, KC_H, 128), BF16)
    dint("ag1u_in", (PCF, 128, KC_H, 128), BF16)
    dint("ag1d_in", (PCF, 4, 128, 512), BF16)
    dint("ag2_in", (64, 4160), F32)
    spc = dict(addr_space="Shared") if comm else {}
    dint("ag1g_out", (KC_F, 128, KC_H, 128), BF16, **spc)
    dint("ag1u_out", (KC_F, 128, KC_H, 128), BF16, **spc)
    dint("ag1d_out", (KC_F, 4, 128, 512), BF16, **spc)
    dint("ag2_out", (NCORE * 64, 4160), F32, **spc)

    tap_t = {}
    for tp in taps:
        tap_t[tp] = nc.dram_tensor("tap_" + tp, list(_TAPS[tp]), F32, kind="ExternalOutput")

    sb = {}
    def sbn(name, shape, dt):
        sb[name] = nc.alloc_sbuf_tensor("s_" + name, list(shape), dt)
        return sb[name]
    for nm, sh, dt in [
        ("ident_bf", (128, 128), BF16), ("ident_f", (128, 128), F32),
        ("ones_bf", (128, 128), BF16), ("tri_bf", (128, 128), BF16),
        ("conv_b", (128, OC_CONV), F32), ("dt_bias", (64, 1), F32),
        ("a_neg", (64, 1), F32), ("d_chan", (128, KC_I), F32),
        ("norm_w", (128, KC_I), F32), ("lut_rep", (128, 16), F32),
        ("emask", (8, 8), F32), ("mmask", (8, 1), F32),
        ("canary", (64, 184), F32),
        ("totals", (64, NCH), F32), ("s_state", (64, 4096), F32),
        ("s_tmp", (64, 512), F32), ("lam_rep", (64, 64), F32),
        ("stat_a", (128, T + 128), BF16), ("stat_b", (128, 520), F32),
        ("stat_c", (128, T + 128), BF16),
        ("bmat", (64, T), BF16), ("cmat", (64, T), BF16),
        ("small64", (64, 128), F32), ("small64b", (64, 128), F32),
        ("wq", (128, 2048), BF16), ("wq2", (128, 2048), BF16),
        ("ystage", (128, 512), BF16), ("sq_st", (128, 1032), BF16),
        ("cst", (128, 8), F32),
    ]:
        sbn(nm, sh, dt)
    from concourse.bass_primitives import MemorySpace
    base = int(nc.bytes_allocated(MemorySpace.SBUF, "left"))
    base = (base + 31) // 32 * 32
    cap = int(nc.sbuf_top)

    def at(name, shape, dt, off):
        t = nc.alloc_sbuf_tensor_at("s_" + name, list(shape), dt, offset=base + off)
        sb[name] = t
        return t
    K = 1024
    at("xhat", (128, KC_H * (T + 128)), BF16, 0)
    at("hbc", (128, OC_CONV * (TS + 3)), BF16, 37 * K)
    at("x_conv", (128, KC_I * 256), BF16, 71 * K)
    at("x_tok", (128, INTER), BF16, 87 * K)
    at("dtx", (128, INTER), BF16, 95 * K)
    at("cum_rep", (128, 16 * 128), F32, 103 * K)
    at("gm", (128, 16 * 128), BF16, 111 * K)
    at("dt_tok", (128, 64), F32, 115 * K)
    at("cum_t", (128, 64), F32, 116 * K)
    at("dec_fm", (64, 128), F32, 117 * K)
    at("dec_tok", (128, 64), F32, 118 * K)
    at("b_tok", (128, 64), BF16, 119 * K)
    at("gtric", (128, 128), BF16, 120 * K)
    at("stage8", (128, 1024), F32, 121 * K)
    at("hbc_sav", (128, OC_CONV * 3), BF16, 125 * K)
    at("dt", (64, T), F32, 126 * K)
    at("gvec", (64, T), F32, 130 * K)
    at("cum", (64, T), F32, 134 * K)
    at("gbuf_i", (128, 2048), BF16, 138 * K)
    at("gbuf_a", (128, 2048), BF16, 142 * K)
    at("gbuf_m", (128, 2048), BF16, 146 * K)
    at("gbuf_i2", (128, 2048), BF16, 150 * K)
    at("cum_repb", (128, 16 * 128), F32, 138 * K)
    at("gmb", (128, 16 * 128), BF16, 146 * K)
    at("y_g", (128, KC_I * T), BF16, 0)
    at("gate_sp", (128, 32 * TS), BF16, 64 * K)
    at("y_ld", (128, 4096), BF16, 96 * K)
    at("carry", (64, 4096), F32, 104 * K)
    at("fin_b", (64, 4096), F32, 120 * K)
    at("snap_ld", (64, 4096), BF16, 120 * K)
    at("trueS", (64, 4096), BF16, 136 * K)
    at("cum_rep2", (128, 16 * 128), F32, 144 * K)
    at("cum_rep2b", (128, 16 * 128), F32, 128 * K)
    at("expc", (64, 16 * 128), BF16, 152 * K)
    at("ctil", (64, 16 * 128), BF16, 156 * K)
    at("p_rep", (64, 64), F32, 160 * K)
    at("p_repb", (64, 64), F32, 160 * K + 512)
    at("coef", (8, 64), F32, 161 * K)
    at("coef_rep", (64, 512), F32, 162 * K)
    at("y_tmp", (128, 512), F32, 160 * K)
    at("mo_bf", (128, KC_H * T), BF16, 64 * K)
    at("x2_fm", (128, KC_H * T), F32, 96 * K)
    at("xhat2", (128, KC_H * T), BF16, 0)
    at("xstage", (128, 1024), F32, 32 * K)
    at("fstage2", (128, 512), F32, 36 * K)
    at("fstage2b", (128, 512), F32, 38 * K)
    at("hmat", (128, 22 * T), BF16, 32 * K)
    at("out_acc", (128, 8 * HID), F32, 76 * K)
    at("fstage", (128, 2048), F32, 140 * K)
    at("sq_big", (128, 2048), F32, 64 * K)
    at("wq3", (128, 2048), BF16, 148 * K)
    at("wq4", (128, 2048), BF16, 152 * K)
    assert base + 164 * K <= cap, (base, cap)

    sems = {}
    for s in ["mm", "ev", "sc", "gp", "cc", "ph",
              "q_sync_ld", "q_sync_st", "q_scalar_ld", "q_scalar_st",
              "q_gpsimd_ld", "q_gpsimd_st"]:
        sems[s] = Sem(nc.alloc_semaphore(s), q=s.startswith("q_"))

    meta = dict(dr=dr, sb=sb, sems=sems, taps=tap_t, comm=comm)
    return nc, meta


_TAPS = {
    "decraw": (64, 528),
    "x2d": (8, 128, HID),
    "canary": (64, 184),
    "gatesp": (128, 32 * TS),
    "rrms": (128, 128),
    "dtxd0": (128, INTER),
    "lam0": (64, 64),
    "dec0": (128, 64),
    "btok0": (128, 64),
    "sstate0": (64, 4096),
    "sstate1": (64, 4096),
    "sstate2": (64, 4096),
    "sstate3": (64, 4096),
    "sstate4": (64, 4096),
    "sstate5": (64, 4096),
    "sstate6": (64, 4096),
    "sstate7": (64, 4096),
    "dtxd2": (128, INTER),
    "dec2": (128, 64),
    "btok2": (128, 64),
    "cmat": (64, T),
    "xhat": (128, KC_H * (T + 128)),
    "hbc": (128, OC_CONV * (TS + 3)),
    "dt": (64, T),
    "gate0": (128, TS),
    "cum": (64, T),
    "xconv": (128, KC_I * 256),
    "bmat": (64, T),
    "sstate": (64, 4096),
    "ydc": (8, 128, 512),
    "carry": (64, 4096),
    "yld": (128, 4096),
    "y_g": (128, KC_I * T),
    "mo": (128, KC_H * T),
    "x2": (128, KC_H * T),
    "xhat2": (128, KC_H * T),
    "h0": (128, T),
    "outacc": (128, 8 * HID),
    "ln4": (8, 128, 24),
    "wg0": (128, 2048),
}


def fap(t, levels, off=0):
    return AP(t, off, [list(l) for l in levels])

def emit_program(b, nc, meta, pb, pbt, pb7, pfn):
    dr, sb, taps, comm = meta["dr"], meta["sb"], meta["taps"], meta["comm"]
    S = meta["sems"]
    MM, EV, SC, CC = S["mm"], S["ev"], S["sc"], S["cc"]
    wt = [sb["wq"], sb["wq2"]]

    def wtok(eng, tok):
        if tok is not None:
            b.waitv(eng, tok[0], tok[0].v if getattr(tok[0], "q", False) else tok[1])

    def tap(name, src):
        if name in taps:
            b.waitv("gpsimd", S["ev"], S["ev"].v)
            b.waitv("gpsimd", S["sc"], S["sc"].v)
            b.waitv("gpsimd", S["mm"], S["mm"].v)
            b.dma("gpsimd", taps[name].ap(), src, "st")

    for nm in ["ident_bf", "ident_f", "ones_bf", "tri_bf", "conv_b", "dt_bias",
               "d_chan", "norm_w", "emask", "mmask"]:
        b.dma("sync", sb[nm][:, :], dr[nm].ap(), "ld")
    b.dma("sync", sb["small64"][:, 0:1], dr["a_log"].ap(), "ld")
    consts_tok = b.dma("sync", sb["lut_rep"][:, :], fap(dr["lut"], [[0, 128], [1, 16]]), "ld")
    xsrc = fap(dr["x_fm"], [[T + 128, 128], [128 * (T + 128), KC_H], [1, T + 128]])
    vx = b.dma("gpsimd", sb["xhat"][:, :], xsrc, "ld")
    b.op("vector", EV, lambda E: E.memset(sb["canary"][:, :], 7.25))
    for ci, cv in enumerate([LN_EPS, RMS_EPS, -0.5, -1.0, 1.0 / HID, 1.0 / INTER, 1.0]):
        b.op("vector", EV, lambda E, ci=ci, cv=cv: E.memset(sb["cst"][:, ci:ci + 1], cv))
    cst_tok = (S["ev"], S["ev"].v)
    meta["cst_tok"] = cst_tok
    b.waitv("scalar", EV, cst_tok[1])
    wtok("scalar", consts_tok)
    va = b.op("scalar", SC, lambda E: E.activation(sb["small64b"][:, 0:1], sb["small64"][:, 0:1], AF.Exp))
    b.waitv("vector", SC, va)
    vneg = b.op("vector", EV, lambda E: E.tensor_scalar_mul(sb["a_neg"][:, :], sb["small64b"][:, 0:1], -1.0))

    gtiles = []
    for mat, (inm, onm) in enumerate([("gidx", "ag1g_in"), ("uidx", "ag1u_in"),
                                      ("didx", "ag1d_in")]):
        for ti in range(PCF):
            gtiles.append((mat, inm, onm, ti))
    gst = {"j": 0, "gv": {}, "st": {}, "ld": {}, "cc": 0}
    ibufs = ["gbuf_i", "gbuf_i2"]
    def emit_gcc(nm):
        b.waitv("gpsimd", b.qsems[("sync", "st")], b.qsems[("sync", "st")].v)
        if comm:
            b.op("gpsimd", CC, lambda E, nm=nm: E.collective_compute(
                "AllGather", OP.bypass, replica_groups=[list(range(NCORE))],
                ins=[dr[f"ag1{nm}_in"].ap().opt()], outs=[dr[f"ag1{nm}_out"].ap().opt()]))
            return (CC, CC.v)
        vv = None
        for r in range(NCORE):
            vv = b.dma("gpsimd", dr[f"ag1{nm}_out"].ap()[r * PCF:(r + 1) * PCF, :, :, :],
                  dr[f"ag1{nm}_in"].ap()[:, :, :, :], "st")
        return vv
    def emit_gjob():
        j = gst["j"]
        if j >= 72:
            return None
        gst["j"] = j + 1
        tn, q = j // 4, j % 4
        mat, inm, onm, ti = gtiles[tn]
        ib = sb[ibufs[tn % 2]]
        if q == 0:
            if tn >= 2:
                b.waitv("gpsimd", EV, gst["gv"][(tn - 2) * 4 + 3])
            if mat < 2:
                isrc = dr[inm].ap()[ti, :, :, :]
            else:
                isrc = fap(dr[inm], [[512, 128], [65536, 4], [1, 512]],
                           off=ti * 4 * 128 * 512)
            gst["ld"][tn] = b.dma("gpsimd", ib[:, :], isrc, "ld")
        h = (j % 2) * 512
        if mat < 2:
            dst = dr[onm].ap()[ti, :, 4 * q:4 * q + 4, :]
        else:
            dst = dr[onm].ap()[ti, q, :, :]
        wtok("vector", gst["ld"][tn])
        if j == 0:
            wtok("vector", consts_tok)
        if j >= 2:
            wtok("vector", gst["st"][j - 2])
        def gops(E, h=h, q=q, ib=ib):
            ii = ib[:, q * 512:q * 512 + 512]; aa = sb["gbuf_a"][:, h:h + 512]
            mm_ = sb["gbuf_m"][:, h:h + 512]
            E.tensor_scalar(aa, ii, 0.0, sb["lut_rep"][:, 0:1], OP.is_equal, OP.mult)
            for v in range(1, 16):
                E.tensor_scalar(mm_, ii, float(v), sb["lut_rep"][:, v:v + 1], OP.is_equal, OP.mult)
                last = E.tensor_add(aa, aa, mm_)
            return last
        gst["gv"][j] = b.op("vector", EV, gops)
        b.wait("sync", EV)
        gst["st"][j] = b.dma("sync", dst, sb["gbuf_a"][:, h:h + 512], "st")
        if j + 1 in (24, 48, 72):
            gst["cc"] = emit_gcc("gud"[(j + 1) // 24 - 1])
        return None
    def finish_gather():
        while gst["j"] < 72:
            emit_gjob()
        return gst["cc"]

    segs = [(128, 512), (640, 512), (0, 128)]
    wtok("tensor", vx)
    wtok("scalar", vx)
    for si, (lo, n) in enumerate(segs):
        p_s, p_q = pb[0], pb[1]
        sqv = []
        for kc in range(KC_H):
            xs = sb["xhat"][:, kc * (T + 128) + lo: kc * (T + 128) + lo + n]
            sq = sb["sq_st"][:, (kc % 2) * 516:(kc % 2) * 516 + n]
            if kc >= 2:
                b.waitv("scalar", MM, sqv[kc - 2])
            vsc = b.op("scalar", SC, lambda E, xs=xs, sq=sq: E.activation(sq, xs, AF.Square))
            b.waitv("tensor", SC, vsc)
            vm = b.op("tensor", MM, lambda E, xs=xs, sq=sq, kc=kc, n=n:
                      [E.matmul(p_s[:, :n], sb["ones_bf"][:, :], xs,
                                start=(kc == 0), stop=(kc == KC_H - 1)),
                       E.matmul(p_q[:, :n], sb["ones_bf"][:, :], sq,
                                start=(kc == 0), stop=(kc == KC_H - 1))][-1])
            sqv.append(vm)
        sa = sb["stat_a"][:, lo:lo + n]; sb_ = sb["stat_b"][:, 0:n]; sc_ = sb["stat_c"][:, lo:lo + n]
        b.waitv("scalar", MM, sqv[-1])
        b.op("scalar", SC, lambda E, sa=sa, n=n: E.activation(sa, p_s[:, :n], AF.Copy, scale=sb["cst"][:, 4:5]))
        v2 = b.op("scalar", SC, lambda E, sc_=sc_, n=n: E.activation(sc_, p_q[:, :n], AF.Copy, scale=sb["cst"][:, 4:5]))
        b.waitv("vector", SC, v2)
        b.op("vector", EV, lambda E, sa=sa, sb_=sb_: E.tensor_mul(sb_, sa, sa))
        v3 = b.op("vector", EV, lambda E, sc_=sc_, sb_=sb_: E.tensor_sub(sc_, sc_, sb_))
        b.waitv("scalar", EV, v3)
        b.op("scalar", SC, lambda E, sb_=sb_, sc_=sc_: E.activation(sb_, sc_, AF.Ln, bias=sb["cst"][:, 0:1]))
        v4 = b.op("scalar", SC, lambda E, sc_=sc_, sb_=sb_: E.activation(sc_, sb_, AF.Exp, scale=sb["cst"][:, 2:3]))
        b.waitv("vector", SC, v4)
        for kc in range(KC_H):
            xs = sb["xhat"][:, kc * (T + 128) + lo: kc * (T + 128) + lo + n]
            stg = sb["ystage"][:, :n]
            b.op("vector", EV, lambda E, xs=xs, stg=stg, sa=sa: E.tensor_sub(stg, xs, sa))
            b.op("vector", EV, lambda E, xs=xs, stg=stg, sc_=sc_: E.tensor_mul(xs, stg, sc_))
    tap("xhat", sb["xhat"][:, :])
    ln1_done = (EV, EV.v)

    bank_tok = {0: ln1_done, 1: ln1_done, 2: None, 3: None, 4: None}
    mm_tok = {}
    wv = {}
    for oc in range(2):
        wv[oc] = b.dma("gpsimd", wt[oc % 2][:, :], dr["w_in"].ap()[oc, :, :, :], "ld")
    stage_tok = None
    dtB = None
    for oc in range(OC_P):
        pA, pB = pb[oc % 2], pb[2 + oc % 2]
        wtok("tensor", wv[oc])
        wtok("tensor", bank_tok[oc % 2])
        wtok("tensor", bank_tok[2 + oc % 2])
        if oc <= OC_DT:
            wtok("tensor", bank_tok[4])
        wtile = wt[oc % 2]
        def mmrun(E, wtile=wtile, pA=pA, pB=pB):
            last = None
            for kc in range(KC_H):
                lw = wtile[:, kc * 128:(kc + 1) * 128]
                xa = sb["xhat"][:, kc * (T + 128) + 128: kc * (T + 128) + 640]
                xb2 = sb["xhat"][:, kc * (T + 128) + 640: kc * (T + 128) + 1152]
                E.matmul(pA[:, :], lw, xa, start=(kc == 0), stop=(kc == KC_H - 1))
                last = E.matmul(pB[:, :], lw, xb2, start=(kc == 0), stop=(kc == KC_H - 1))
            return last
        vmm = b.op("tensor", MM, mmrun)
        if oc <= OC_DT:
            def mmh(E, wtile=wtile):
                last = None
                for kc in range(KC_H):
                    lw = wtile[:, kc * 128:(kc + 1) * 128]
                    xh = sb["xhat"][:, kc * (T + 128) + 125: kc * (T + 128) + 128]
                    last = E.matmul(pb[4][:, :3], lw, xh, start=(kc == 0), stop=(kc == KC_H - 1))
                return last
            vmm = b.op("tensor", MM, mmh)
        mm_tok[oc] = vmm
        if oc + 2 < OC_P:
            b.waitv("gpsimd", MM, mm_tok[oc])
            wv[oc + 2] = b.dma("gpsimd", wt[(oc + 2) % 2][:, :], dr["w_in"].ap()[oc + 2, :, :, :], "ld")
        if oc < OC_CONV:
            b.waitv("vector", MM, vmm)
            tA = b.op("vector", EV, lambda E, oc=oc, pA=pA: E.tensor_copy(
                sb["hbc"][:, oc * 515 + 3: oc * 515 + 515], pA[:, :]))
            tH = b.op("vector", EV, lambda E, oc=oc: E.tensor_copy(
                sb["hbc"][:, oc * 515: oc * 515 + 3], pb[4][:, :3]))
            wtok("vector", stage_tok)
            tB = b.op("vector", EV, lambda E, pB=pB: E.tensor_copy(sb["sq_st"][:, :512], pB[:, :]))
            b.wait("sync", EV)
            vst = b.dma("sync", dr["hbc_d"].ap()[:, oc, 3:515], sb["sq_st"][:, :512], "st")
            stage_tok = vst
            bank_tok[oc % 2] = (EV, tA)
            bank_tok[2 + oc % 2] = (EV, tB)
            bank_tok[4] = (EV, tH)
        elif oc == OC_DT:
            b.waitv("scalar", MM, vmm)
            dtA = b.op("scalar", SC, lambda E, pA=pA: E.copy(sb["dt"][0:64, 0:512], pA[0:64, :]))
            dtB = b.op("scalar", SC, lambda E, pB=pB: E.copy(sb["dt"][0:64, 512:1024], pB[0:64, :]))
            bank_tok[oc % 2] = (SC, dtA)
            bank_tok[2 + oc % 2] = (SC, dtB)
            bank_tok[4] = (SC, dtB)
        else:
            hp = oc - OC_GATE0
            b.waitv("scalar", MM, vmm)
            for (ps, sp) in ((pA, 0), (pB, 1)):
                wtok("scalar", stage_tok)
                tg = b.op("scalar", SC, lambda E, ps=ps: E.activation(
                    sb["sq_st"][:, :512], ps[:, :], AF.Silu))
                b.wait("sync", SC)
                vst = b.dma("sync", dr["gate_d"].ap()[sp, :, hp, :], sb["sq_st"][:, :512], "st")
                stage_tok = vst
                bank_tok[(0 if sp == 0 else 2) + oc % 2] = (SC, tg)
        emit_gjob()
        if oc % 10 == 9:
            emit_gjob()
    ag1_tok = finish_gather()
    tap("hbc", sb["hbc"][:, :])

    b.waitv("scalar", SC, dtB)
    b.waitv("vector", SC, dtB)
    b.op("vector", EV, lambda E: E.tensor_scalar(sb["dt"][:, :], sb["dt"][:, :],
                                                 sb["dt_bias"][:, :], None, OP.add))
    b.op("vector", EV, lambda E: E.tensor_scalar_max(sb["gvec"][:, :], sb["dt"][:, :], 0.0))
    b.op("vector", EV, lambda E: E.tensor_scalar_min(sb["dt"][:, :], sb["dt"][:, :], 0.0))
    b.op("vector", EV, lambda E: E.tensor_sub(sb["dt"][:, :], sb["dt"][:, :], sb["gvec"][:, :]))
    vfix = b.op("vector", EV, lambda E: E.tensor_scalar_max(sb["dt"][:, :], sb["dt"][:, :], -80.0))
    b.waitv("scalar", EV, vfix)
    b.op("scalar", SC, lambda E: E.activation(sb["dt"][:, :], sb["dt"][:, :], AF.Exp))
    vln = b.op("scalar", SC, lambda E: E.activation(sb["dt"][:, :], sb["dt"][:, :], AF.Ln,
                                                    bias=sb["cst"][0:64, 6:7]))
    b.waitv("vector", SC, vln)
    vdt = b.op("vector", EV, lambda E: E.tensor_add(sb["dt"][:, :], sb["dt"][:, :], sb["gvec"][:, :]))
    vdt_ev = vdt
    tap("dt", sb["dt"][:, :])
    b.op("vector", EV, lambda E: E.tensor_scalar(sb["gvec"][:, :], sb["dt"][:, :],
                                                 sb["a_neg"][:, :], None, OP.mult))
    cum_wr = []
    for c in range(NCH):
        g = sb["gvec"][:, c * 128:(c + 1) * 128]
        vscn = b.op("vector", EV, lambda E, g=g, c=c: E.tensor_tensor_scan(
            sb["cum"][:, c * 128:(c + 1) * 128], g, g, 0.0, OP.add, OP.bypass))
        b.waitv("vector", EV, vscn)
        vtc = b.op("vector", EV, lambda E, c=c: E.tensor_copy(
            sb["totals"][:, c:c + 1], sb["cum"][:, c * 128 + 127: c * 128 + 128]))
        if c == NCH - 1:
            b.waitv("vector", EV, vtc)
        b.wait("sync", EV)
        b.dma("sync", fap(dr["lam_d"], [[1, 64], [1, 1]], off=c * 64),
              sb["totals"][:, c:c + 1], "st")
        cum_wr.append(b.dma("sync", dr["cum_d"].ap()[c, :, :], sb["cum"][:, c * 128:(c + 1) * 128], "st"))
    vscn2 = b.op("vector", EV, lambda E: E.tensor_tensor_scan(
        sb["small64"][:, 0:8], sb["totals"][:, :], sb["totals"][:, :], 0.0, OP.add, OP.bypass))
    b.waitv("vector", EV, vscn2)
    b.op("vector", EV, lambda E: E.memset(sb["small64b"][:, 0:1], 0.0))
    b.op("vector", EV, lambda E: E.tensor_copy(sb["small64b"][:, 1:8], sb["small64"][:, 0:7]))
    b.wait("sync", EV)
    for c in range(NCH):
        vpref = b.dma("sync", fap(dr["pref_d"], [[1, 64], [1, 1]], off=c * 64),
                      sb["small64b"][:, c:c + 1], "st")
    vtot = b.dma("sync", dr["tot_d"].ap(), sb["small64"][:, 7:8], "st")
    b.op("vector", EV, lambda E: E.tensor_copy(sb["stage8"][0:64, 512:520], sb["totals"][:, :]))
    tap("cum", sb["cum"][:, :])

    ydc_tok = {}
    snap_tok = {}
    st8 = {}
    vz0 = b.op("vector", EV, lambda E: E.memset(sb["s_state"][:, :], 0.0))
    st8["srec"] = (EV, vz0)
    hbc_done = (EV, EV.v)
    for sp in range(2):
        if sp == 1:
            vsv = b.op("vector", EV, lambda E: E.tensor_copy(
                fap(sb["hbc_sav"], [[OC_CONV * 3, 128], [3, OC_CONV], [1, 3]]),
                fap(sb["hbc"], [[OC_CONV * 515, 128], [515, OC_CONV], [1, 3]], off=512)))
            wtok("sync", stage_tok)
            b.waitv("sync", EV, vsv)
            vrl = b.dma("sync", fap(sb["hbc"], [[OC_CONV * 515, 128], [515, OC_CONV], [1, 512]], off=3),
                        dr["hbc_d"].ap()[:, :, 3:515], "ld")
            wtok("vector", vrl)
            vh = b.op("vector", EV, lambda E: E.tensor_copy(
                fap(sb["hbc"], [[OC_CONV * 515, 128], [515, OC_CONV], [1, 3]]),
                sb["hbc_sav"][:, :]))
            hbc_ready = (EV, vh)
        else:
            hbc_ready = None
        for hs in range(2):
            lo = hs * 256
            cw = {}
            def cdsrc(fc):
                return fap(dr["conv_diag"], [[128, 128], [128 * 128, 4], [1, 128]],
                           off=fc * KCONV * 128 * 128)
            b.waitv("gpsimd", MM, MM.v)
            for fc in range(2):
                cw[fc] = b.dma("gpsimd", wt[fc % 2][:, :512], cdsrc(fc), "ld")
            conv_ev = {}
            for fc in range(OC_CONV):
                pc = pb[4 + fc % 2]
                wtok("tensor", cw[fc])
                wtok("tensor", hbc_ready)
                wtok("tensor", conv_ev.get(fc - 2))
                if sp == 0 and hs == 0 and fc < 2:
                    wtok("tensor", bank_tok[4])
                    wtok("tensor", hbc_done)
                def cmm(E, fc=fc, pc=pc, lo=lo):
                    last = None
                    for k in range(KCONV):
                        lw = wt[fc % 2][:, k * 128:(k + 1) * 128]
                        rhs = sb["hbc"][:, fc * 515 + lo + k: fc * 515 + lo + k + 256]
                        last = E.matmul(pc[:, :256], lw, rhs, start=(k == 0), stop=(k == KCONV - 1))
                    return last
                vc = b.op("tensor", MM, cmm)
                if fc + 2 < OC_CONV:
                    b.waitv("gpsimd", MM, vc)
                    cw[fc + 2] = b.dma("gpsimd", wt[fc % 2][:, :512], cdsrc(fc + 2), "ld")
                b.waitv("scalar", MM, vc)
                bias = sb["conv_b"][:, fc:fc + 1]
                if fc < 32:
                    wtok("scalar", st8.get("xconv_free_ev"))
                    wtok("scalar", st8.get("xconv_free_mm"))
                    ve = b.op("scalar", SC, lambda E, fc=fc, pc=pc, bias=bias: E.activation(
                        sb["x_conv"][:, fc * 256: fc * 256 + 256], pc[:, :256], AF.Silu, bias=bias))
                else:
                    tok0 = sp * 512 + lo
                    b.op("scalar", SC, lambda E, pc=pc, tok0=tok0: E.activation(
                        sb["bmat"][0:64, tok0:tok0 + 256], pc[0:64, :256], AF.Silu,
                        bias=sb["conv_b"][0:64, 32:33]))
                    ve = b.op("scalar", SC, lambda E, pc=pc, tok0=tok0: E.activation(
                        sb["cmat"][0:64, tok0:tok0 + 256], pc[64:128, :256], AF.Silu,
                        bias=sb["conv_b"][64:128, 32:33]))
                conv_ev[fc] = (SC, ve)
            if sp == 0 and hs == 0:
                tap("xconv", sb["x_conv"][:, :])
            conv_done = (SC, SC.v)
            if sp == 0 and hs == 0:
                b.op("vector", EV, lambda E: E.tensor_copy(sb["stage8"][0:64, 520:528], sb["totals"][:, :]))
            for c2 in range(2):
                ch = sp * 4 + hs * 2 + c2
                lo2 = c2 * 128
                emit_chunk_inpass(b, nc, meta, pb, pbt, pb7, ch, sp, lo2, conv_done,
                                  cum_wr, ydc_tok, snap_tok, wtok, tap, st8)
            st8["xconv_free_ev"] = (EV, EV.v)
            st8["xconv_free_mm"] = (MM, MM.v)
    meta["_mix"] = dict(ydc_tok=ydc_tok, snap_tok=snap_tok, vpref=vpref, vtot=vtot,
                        ag1_tok=ag1_tok)

YBANKS = [0, 1, 2, 3, 4, 5, 7, 0]

def emit_chunk_inpass(b, nc, meta, pb, pbt, pb7, ch, sp, lo2, conv_done,
                      cum_wr, ydc_tok, snap_tok, wtok, tap, st8):
    dr, sb, S = meta["dr"], meta["sb"], meta["sems"]
    MM, EV, SC, CC = S["mm"], S["ev"], S["sc"], S["cc"]
    tok0 = ch * 128

    prev = st8.get("xtok_free")
    for w in range(4):
        wtok("tensor", conv_done if w == 0 else None)
        wtok("tensor", st8.get("pbt_free"))
        vtr = b.op("tensor", MM, lambda E, w=w: [
            E.transpose(pbt[:, i * 128:(i + 1) * 128],
                        sb["x_conv"][:, (8 * w + i) * 256 + lo2:(8 * w + i) * 256 + lo2 + 128],
                        sb["ident_bf"][:, :]) for i in range(8)][-1])
        b.waitv("vector", MM, vtr)
        wtok("vector", prev)
        ve = b.op("vector", EV, lambda E, w=w: E.tensor_copy(
            sb["x_tok"][:, w * 1024:(w + 1) * 1024], pbt[:, :1024]))
        st8["pbt_free"] = (EV, ve)
    wtok("tensor", st8.get("pbt_free"))
    vtr = b.op("tensor", MM, lambda E: E.transpose(
        pbt[:, 0:64], sb["bmat"][0:64, tok0:tok0 + 128], sb["ident_bf"][0:64, 0:64]))
    b.waitv("vector", MM, vtr)
    vbt = b.op("vector", EV, lambda E: E.tensor_copy(sb["b_tok"][:, :], pbt[:, 0:64]))
    st8["pbt_free"] = (EV, vbt)
    wtok("tensor", st8.get("pb7_free"))
    vt2 = b.op("tensor", MM, lambda E: [
        E.transpose(pb7[:, 0:64], sb["cum"][0:64, tok0:tok0 + 128], sb["ident_f"][0:64, 0:64]),
        E.transpose(pb7[:, 64:128], sb["dt"][0:64, tok0:tok0 + 128], sb["ident_f"][0:64, 0:64])][-1])
    b.waitv("vector", MM, vt2)
    b.op("vector", EV, lambda E: E.tensor_copy(sb["cum_t"][:, :], pb7[:, 0:64]))
    vdt2 = b.op("vector", EV, lambda E: E.tensor_copy(sb["dt_tok"][:, :], pb7[:, 64:128]))
    b.op("vector", EV, lambda E: E.tensor_scalar(
        sb["dec_fm"][:, :], sb["cum"][0:64, tok0:tok0 + 128],
        sb["totals"][:, ch:ch + 1], None, OP.subtract))
    vdf = b.op("vector", EV, lambda E: E.tensor_scalar_min(
        sb["dec_fm"][:, :], sb["dec_fm"][:, :], 80.0))
    if ch == 0:
        b.op("vector", EV, lambda E: E.tensor_copy(sb["stage8"][0:64, 0:128], sb["dec_fm"][:, :]))
        b.op("vector", EV, lambda E: E.tensor_copy(sb["stage8"][0:64, 128:136], sb["totals"][:, :]))
        tap("decraw", sb["stage8"][0:64, 0:528])
    b.waitv("scalar", EV, vdf)
    vde = b.op("scalar", SC, lambda E: E.activation(sb["dec_fm"][:, :], sb["dec_fm"][:, :],
                                                    AF.Exp, scale=sb["cst"][0:64, 3:4]))
    b.waitv("tensor", SC, vde)
    b.waitv("tensor", EV, vdt2)
    vt3 = b.op("tensor", MM, lambda E: E.transpose(
        pb7[:, 128:192], sb["dec_fm"][0:64, :], sb["ident_f"][0:64, 0:64]))
    b.waitv("vector", MM, vt3)
    vdk = b.op("vector", EV, lambda E: E.tensor_copy(sb["dec_tok"][:, :], pb7[:, 128:192]))
    st8["pb7_free"] = (EV, vdk)

    b.waitv("vector", EV, vdt2)
    vdx = b.op("vector", EV, lambda E: E.tensor_tensor(
        sb["dtx"][:, :], sb["x_tok"][:, :],
        fap(sb["dt_tok"], [[64, 128], [1, 64], [0, 64]]), OP.mult))

    wtok("tensor", st8.get("pb7_free"))
    vg = b.op("tensor", MM, lambda E: E.matmul(
        pb7[:, 256:384], sb["bmat"][0:64, tok0:tok0 + 128],
        sb["cmat"][0:64, tok0:tok0 + 128], start=True, stop=True))
    b.waitv("vector", MM, vg)
    vgt = b.op("vector", EV, lambda E: E.tensor_tensor(
        sb["gtric"][:, :], pb7[:, 256:384], sb["tri_bf"][:, :], OP.mult))
    st8["pb7_free"] = (EV, vgt)

    def cum_load(sg2):
        ch2, q2 = divmod(sg2, 4)
        buf2 = sb["cum_rep"] if sg2 % 2 == 0 else sb["cum_repb"]
        wtok("scalar", cum_wr[ch2])
        wtok("scalar", st8.get(f"cumrep_free{sg2 % 2}"))
        if sg2 == 0:
            b.waitv("scalar", EV, S["ev"].v)
        st8[f"crld{sg2}"] = b.dma("scalar", buf2[:, :],
                    fap(dr["cum_d"], [[0, 128], [128, 16], [1, 128]],
                        off=ch2 * 64 * 128 + q2 * 16 * 128), "ld")
    if "crs" not in st8:
        st8["crs"] = 0
    while st8["crs"] <= min(ch * 4 + 1, 31):
        cum_load(st8["crs"]); st8["crs"] += 1
    ydone = {}
    for q in range(4):
        sg = ch * 4 + q
        crb = sb["cum_rep"] if sg % 2 == 0 else sb["cum_repb"]
        gmbuf = sb["gm"] if sg % 2 == 0 else sb["gmb"]
        wtok("vector", st8[f"crld{sg}"])
        b.waitv("vector", EV, vgt if q == 0 else EV.v)
        vd1 = b.op("vector", EV, lambda E, q=q, crb=crb: E.tensor_tensor(
            crb[:, :], crb[:, :],
            fap(sb["cum_t"], [[64, 128], [1, 16], [0, 128]], off=q * 16), OP.subtract))
        vd2 = b.op("vector", EV, lambda E, crb=crb: E.tensor_scalar(
            crb[:, :], crb[:, :], 0.0, -80.0, OP.min, OP.max))
        b.waitv("scalar", EV, vd2)
        wtok("scalar", st8.get(f"gm_free{sg % 2}"))
        if ch == 0 and q <= 1:
            b.waitv("scalar", EV, S["ev"].v)
        vex = b.op("scalar", SC, lambda E, crb=crb, gmbuf=gmbuf: E.activation(
            gmbuf[:, :], crb[:, :], AF.Exp))
        st8[f"cumrep_free{sg % 2}"] = (SC, vex)
        while st8["crs"] <= min(sg + 2, 31):
            cum_load(st8["crs"]); st8["crs"] += 1
        b.waitv("vector", SC, vex)
        vgm = b.op("vector", EV, lambda E, gmbuf=gmbuf: E.tensor_tensor(
            gmbuf[:, :], gmbuf[:, :],
            fap(sb["gtric"], [[128, 128], [0, 16], [1, 128]]), OP.mult))
        b.waitv("tensor", EV, vgm)
        if q == 0:
            b.waitv("tensor", EV, vdx)
        for gl in (2 * q, 2 * q + 1):
            pbank = pb[YBANKS[gl]] if YBANKS[gl] != 7 else pb7
            wtok("tensor", st8.get(f"yb{YBANKS[gl]}"))
            if gl == 6:
                wtok("tensor", st8.get("pb7_free"))
            if YBANKS[gl] == 5:
                wtok("tensor", st8.get("pb5_rec"))
            if gl == 7:
                wtok("tensor", ydone.get(0))
            def ymm(E, gl=gl, q=q, pbank=pbank, gmbuf=gmbuf):
                last = None
                for hh in range(8):
                    h = gl * 8 + hh
                    half = h % 2
                    pr = (h // 2) % 4
                    last = E.matmul(pbank[half * 64:(half + 1) * 64, pr * 128:(pr + 1) * 128],
                                    sb["dtx"][:, h * 64:(h + 1) * 64],
                                    gmbuf[:, (h - 16 * q) * 128:(h - 16 * q + 1) * 128],
                                    start=(hh < 2), stop=(hh >= 6), skip_group_check=True)
                return last
            vy = b.op("tensor", MM, ymm)
            b.waitv("vector", MM, vy)
            wtok("vector", st8.get("ystage_free"))
            def yev(E, gl=gl, pbank=pbank):
                last = None
                for pr in range(4):
                    hp = gl * 4 + pr
                    last = E.scalar_tensor_tensor(
                        sb["ystage"][:, pr * 128:(pr + 1) * 128],
                        sb["x_conv"][:, hp * 256 + lo2: hp * 256 + lo2 + 128],
                        sb["d_chan"][:, hp:hp + 1],
                        pbank[:, pr * 128:(pr + 1) * 128], OP.mult, OP.add)
                return last
            vye = b.op("vector", EV, yev)
            st8[f"yb{YBANKS[gl]}"] = (EV, vye)
            ydone[gl] = (EV, vye)
            b.wait("sync", EV)
            vst = b.dma("sync", dr["y_d"].ap()[ch, gl, :, :], sb["ystage"][:, :], "st")
            st8["ystage_free"] = vst
            ydc_tok[(ch, gl)] = vst
        st8[f"gm_free{sg % 2}"] = (MM, MM.v)
    if ch == 0:
        tap("ydc", dr["y_d"].ap()[0, :, :, :])

    b.waitv("vector", MM, MM.v)
    vdd = b.op("vector", EV, lambda E: E.tensor_tensor(
        sb["dtx"][:, :], sb["dtx"][:, :],
        fap(sb["dec_tok"], [[64, 128], [1, 64], [0, 64]]), OP.mult))
    st8["xtok_free"] = (EV, vdd)

    wtok("gpsimd", st8.get("srec"))
    vsn = b.dma("gpsimd", dr["snap_d"].ap()[ch, :, :], sb["s_state"][:, :], "st")
    snap_tok[ch] = vsn

    wtok("scalar", st8.get("lam_used"))
    wtok("scalar", cum_wr[ch])
    vlr = b.dma("scalar", sb["lam_rep"][:, :],
                fap(dr["lam_d"], [[0, 64], [1, 64]], off=ch * 64), "ld")
    wtok("scalar", vlr)
    wtok("vector", vlr)
    vcl = b.op("vector", EV, lambda E: E.tensor_scalar_max(sb["lam_rep"][:, :], sb["lam_rep"][:, :], -80.0))
    wtok("scalar", vsn)
    b.waitv("scalar", EV, vcl)
    vle = b.op("scalar", SC, lambda E: E.activation(sb["lam_rep"][:, :], sb["lam_rep"][:, :], AF.Exp))

    b.waitv("tensor", EV, vdd)
    last_rec = None
    for g8 in range(8):
        wtok("tensor", st8.get("pb5_rec"))
        if g8 == 0:
            wtok("tensor", st8.get("yb5"))
        def dsmm(E, g8=g8):
            last = None
            for pr in range(4):
                pair = g8 * 4 + pr
                last = E.matmul(pb[5][0:64, pr * 128:(pr + 1) * 128],
                                sb["b_tok"][:, :], sb["dtx"][:, pair * 128:(pair + 1) * 128],
                                start=(pr == 0), stop=(pr == 3))
            return last
        vds = b.op("tensor", MM, dsmm)
        b.waitv("vector", MM, vds)
        b.waitv("vector", SC, vle)
        sl = sb["s_state"][:, g8 * 512:(g8 + 1) * 512]
        b.op("vector", EV, lambda E, sl=sl, g8=g8: E.tensor_tensor(
            sb["s_tmp"][:, :], sl,
            fap(sb["lam_rep"], [[64, 64], [2, 4], [1, 2], [0, 64]], off=8 * g8), OP.mult))
        last_rec = b.op("vector", EV, lambda E, sl=sl: E.tensor_tensor(
            sl, sb["s_tmp"][:, :], pb[5][0:64, :], OP.add))
        st8["pb5_rec"] = (EV, last_rec)
    st8["srec"] = (EV, last_rec)
    st8["lam_used"] = (EV, last_rec)
    if ch == 0:
        tap("dtxd0", sb["dtx"][:, :])
        tap("lam0", sb["lam_rep"][:, :])
        tap("dec0", sb["dec_tok"][:, :])
        tap("sstate0", sb["s_state"][:, :])
        tap("btok0", sb["b_tok"][:, :])
    if ch == 1:
        tap("sstate1", sb["s_state"][:, :])
    if ch in (2, 3, 4, 5, 6, 7):
        tap(f"sstate{ch}", sb["s_state"][:, :])
    if ch == 2:
        tap("dtxd2", sb["dtx"][:, :])
        tap("dec2", sb["dec_tok"][:, :])
        tap("btok2", sb["b_tok"][:, :])

def emit_post(b, nc, meta, pb, pbt, pb7):
    dr, sb, taps, comm = meta["dr"], meta["sb"], meta["taps"], meta["comm"]
    S = meta["sems"]
    MM, EV, SC, CC = S["mm"], S["ev"], S["sc"], S["cc"]
    mix = meta["_mix"]

    def wtok(eng, tok):
        if tok is not None:
            b.waitv(eng, tok[0], tok[0].v if getattr(tok[0], "q", False) else tok[1])

    def tap(name, src):
        if name in taps:
            b.waitv("gpsimd", S["ev"], S["ev"].v)
            b.waitv("gpsimd", S["sc"], S["sc"].v)
            b.waitv("gpsimd", S["mm"], S["mm"].v)
            b.dma("gpsimd", taps[name].ap(), src, "st")

    tap("bmat", sb["bmat"][:, :])
    tap("cmat", sb["cmat"][:, :])
    tap("canary", sb["canary"][:, :])
    vsf = b.dma("sync", dr["ag2_in"].ap()[:, 0:4096], sb["s_state"][:, :], "st")
    wtok("scalar", mix["vtot"])
    b.waitv("scalar", EV, EV.v)
    vtr = b.dma("scalar", sb["lam_rep"][:, :], fap(dr["tot_d"], [[0, 64], [1, 64]]), "ld")
    wtok("sync", vtr)
    vtf = b.dma("sync", dr["ag2_in"].ap()[:, 4096:4160], sb["lam_rep"][:, :], "st")
    wtok("gpsimd", vsf)
    wtok("gpsimd", vtf)
    if comm:
        b.op("gpsimd", CC, lambda E: E.collective_compute(
            "AllGather", OP.bypass, replica_groups=[list(range(NCORE))],
            ins=[dr["ag2_in"].ap().opt()], outs=[dr["ag2_out"].ap().opt()]))
        ag2_tok = (CC, CC.v)
    else:
        vv = None
        for r in range(NCORE):
            vv = b.dma("gpsimd", dr["ag2_out"].ap()[r * 64:(r + 1) * 64, :], dr["ag2_in"].ap()[:, :], "st")
        ag2_tok = vv
    tap("sstate", sb["s_state"][:, :])

    wtok("sync", ag2_tok)
    vgv = b.dma("sync", sb["coef"][:, :],
                fap(dr["ag2_out"], [[64 * 4160, 8], [1, 64]], off=4096), "ld")
    wtok("tensor", vgv)
    vcm = b.op("tensor", MM, lambda E: E.matmul(pb[0][0:8, 0:64], sb["emask"][:, :],
                                                sb["coef"][:, :], start=True, stop=True))
    b.waitv("scalar", MM, vcm)
    vcc0 = b.op("scalar", SC, lambda E: E.copy(sb["coef"][:, :], pb[0][0:8, 0:64]))
    b.waitv("vector", SC, vcc0)
    vcc1 = b.op("vector", EV, lambda E: E.tensor_scalar_max(sb["coef"][:, :], sb["coef"][:, :], -80.0))
    b.waitv("scalar", EV, vcc1)
    vce = b.op("scalar", SC, lambda E: E.activation(sb["coef"][:, :], sb["coef"][:, :], AF.Exp))
    b.waitv("vector", SC, vce)
    vcm2 = b.op("vector", EV, lambda E: E.tensor_scalar(sb["coef"][:, :], sb["coef"][:, :],
                                                        sb["mmask"][:, :], None, OP.mult))
    b.wait("sync", EV)
    vcd = b.dma("sync", dr["coef_d"].ap(), sb["coef"][:, :], "st")
    wtok("scalar", vcd)
    vcr = b.dma("scalar", sb["coef_rep"][:, :], fap(dr["coef_d"], [[0, 64], [1, 512]]), "ld")

    vz = b.op("vector", EV, lambda E: E.memset(sb["carry"][:, :], 0.0))
    wtok("vector", vcr)
    fin_tok = None
    for j in range(NCORE):
        wtok("sync", fin_tok)
        vfj = b.dma("sync", sb["fin_b"][:, :], dr["ag2_out"].ap()[j * 64:(j + 1) * 64, 0:4096], "ld")
        wtok("vector", vfj)
        b.op("vector", EV, lambda E, j=j: E.tensor_tensor(
            sb["fin_b"][:, :], sb["fin_b"][:, :],
            fap(sb["coef_rep"], [[512, 64], [2, 32], [1, 2], [0, 64]], off=64 * j), OP.mult))
        vca = b.op("vector", EV, lambda E: E.tensor_add(sb["carry"][:, :], sb["carry"][:, :],
                                                        sb["fin_b"][:, :]))
        fin_tok = (EV, vca)
    tap("carry", sb["carry"][:, :])

    st8 = {}
    gate_tok = None
    ydc_tok, snap_tok = mix["ydc_tok"], mix["snap_tok"]
    for ch in range(NCH):
        sp, tok0 = ch // 4, ch * 128
        if ch % 4 == 0:
            wtok("sync", gate_tok)
            vgl = b.dma("sync", sb["gate_sp"][:, :], dr["gate_d"].ap()[sp, :, :, :], "ld")
            st8["gate_ld"] = vgl
        wtok("sync", st8.get("yld_free"))
        vyl = b.dma("sync", sb["y_ld"][:, :],
                    fap(dr["y_d"], [[512, 128], [128 * 512, 8], [1, 512]], off=ch * 8 * 128 * 512), "ld")
        wtok("sync", st8.get("snap_free"))
        wtok("sync", snap_tok[ch])
        vsl = b.dma("sync", sb["snap_ld"][:, :], dr["snap_d"].ap()[ch, :, :], "ld")
        def pl_load(c2):
            buf = sb["p_rep"] if c2 % 2 == 0 else sb["p_repb"]
            wtok("scalar", mix["vpref"])
            wtok("scalar", st8.get(f"prep_used{c2 % 2}"))
            st8[f"plld{c2}"] = b.dma("scalar", buf[:, :],
                fap(dr["pref_d"], [[0, 64], [1, 64]], off=c2 * 64), "ld")
        if ch == 0:
            pl_load(0); pl_load(1)
        prb = sb["p_rep"] if ch % 2 == 0 else sb["p_repb"]
        vpl = st8[f"plld{ch}"]
        wtok("vector", vpl)
        vclp = b.op("vector", EV, lambda E, prb=prb: E.tensor_scalar_max(prb[:, :], prb[:, :], -80.0))
        wtok("scalar", vpl)
        b.waitv("scalar", EV, vclp)
        vpe = b.op("scalar", SC, lambda E, prb=prb: E.activation(prb[:, :], prb[:, :], AF.Exp))
        b.waitv("vector", SC, vpe)
        wtok("vector", vsl)
        wtok("vector", fin_tok)
        wtok("vector", st8.get("trueS_used"))
        b.op("vector", EV, lambda E, prb=prb: E.tensor_tensor(
            sb["trueS"][:, :], sb["carry"][:, :],
            fap(prb, [[64, 64], [2, 32], [1, 2], [0, 64]]), OP.mult))
        vts = b.op("vector", EV, lambda E: E.tensor_add(sb["trueS"][:, :], sb["trueS"][:, :],
                                                        sb["snap_ld"][:, :]))
        st8["snap_free"] = (EV, vts)
        st8[f"prep_used{ch % 2}"] = (EV, vts)
        if ch + 2 < NCH:
            pl_load(ch + 2)
        def cr2_load(sg2):
            ch2, q2 = divmod(sg2, 4)
            buf2 = sb["cum_rep2"] if sg2 % 2 == 0 else sb["cum_rep2b"]
            wtok("scalar", st8.get(f"cr2_free{sg2 % 2}"))
            if sg2 == 0:
                b.waitv("scalar", MM, MM.v)
                b.waitv("scalar", EV, S["ev"].v)
            st8[f"cr2ld{sg2}"] = b.dma("scalar", buf2[:, :],
                         fap(dr["cum_d"], [[0, 128], [128, 16], [1, 128]],
                             off=ch2 * 64 * 128 + q2 * 16 * 128), "ld")
        if "c2s" not in st8:
            st8["c2s"] = 0
        while st8["c2s"] <= min(ch * 4 + 1, 31):
            cr2_load(st8["c2s"]); st8["c2s"] += 1
        wtok("vector", vyl)
        for q in range(4):
            sg = ch * 4 + q
            crb2 = sb["cum_rep2"] if sg % 2 == 0 else sb["cum_rep2b"]
            exb = sb["expc"]
            wtok("vector", st8[f"cr2ld{sg}"])
            vclq = b.op("vector", EV, lambda E, crb2=crb2: E.tensor_scalar_max(
                crb2[0:64, :], crb2[0:64, :], -80.0))
            wtok("scalar", st8.get("expc_used"))
            b.waitv("scalar", EV, vclq)
            vex = b.op("scalar", SC, lambda E, crb2=crb2, exb=exb: E.activation(
                exb[:, :], crb2[0:64, :], AF.Exp))
            st8[f"cr2_free{sg % 2}"] = (SC, vex)
            while st8["c2s"] <= min(sg + 2, 31):
                cr2_load(st8["c2s"]); st8["c2s"] += 1
            b.waitv("vector", SC, vex)
            wtok("vector", st8.get("ctil_used"))
            vct = b.op("vector", EV, lambda E, tok0=tok0, exb=exb: E.tensor_tensor(
                sb["ctil"][:, :],
                fap(sb["cmat"], [[T, 64], [0, 16], [1, 128]], off=tok0),
                exb[:, :], OP.mult))
            b.waitv("tensor", EV, vct)
            if q == 0:
                b.waitv("tensor", EV, vts)
            for gl in (2 * q, 2 * q + 1):
                pbank = pb[YBANKS[gl]] if YBANKS[gl] != 7 else pb7
                wtok("tensor", st8.get(f"pyb{YBANKS[gl]}"))
                if gl == 7:
                    wtok("tensor", st8.get("pyb0"))
                def ymm2(E, gl=gl, q=q, pbank=pbank):
                    last = None
                    for hh in range(8):
                        h = gl * 8 + hh
                        half, pr = h % 2, (h // 2) % 4
                        last = E.matmul(pbank[half * 64:(half + 1) * 64, pr * 128:(pr + 1) * 128],
                                        sb["trueS"][0:64, h * 64:(h + 1) * 64],
                                        sb["ctil"][0:64, (h - 16 * q) * 128:(h - 16 * q + 1) * 128],
                                        start=(hh < 2), stop=(hh >= 6), skip_group_check=True)
                    return last
                vy2 = b.op("tensor", MM, ymm2)
                b.waitv("vector", MM, vy2)
                def yadd(E, gl=gl, pbank=pbank):
                    last = None
                    for pr in range(4):
                        hp = gl * 4 + pr
                        sl = sb["y_ld"][:, hp * 128:(hp + 1) * 128]
                        last = E.tensor_tensor(sl, sl, pbank[:, pr * 128:(pr + 1) * 128], OP.add)
                    return last
                vya = b.op("vector", EV, yadd)
                st8[f"pyb{YBANKS[gl]}"] = (EV, vya)
            st8["expc_used"] = (EV, EV.v)
            st8["ctil_used"] = (MM, MM.v)
        st8["trueS_used"] = (MM, MM.v)
        if ch == 0:
            tap("yld", sb["y_ld"][:, :])
        wtok("scalar", st8.get("yld_free"))
        sqv = []
        for hp in range(KC_I):
            sq = sb["sq_st"][:, (hp % 2) * 516:(hp % 2) * 516 + 128]
            if hp >= 2:
                b.waitv("scalar", MM, sqv[hp - 2])
            vsq = b.op("scalar", SC, lambda E, hp=hp, sq=sq: E.activation(
                sq, sb["y_ld"][:, hp * 128:(hp + 1) * 128], AF.Square))
            b.waitv("tensor", SC, vsq)
            if hp == 0:
                wtok("tensor", st8.get("pyb1"))
            vms = b.op("tensor", MM, lambda E, hp=hp, sq=sq: E.matmul(
                pb[1][:, 0:128], sb["ones_bf"][:, :], sq, start=(hp == 0), stop=(hp == KC_I - 1)))
            sqv.append(vms)
        b.waitv("scalar", MM, sqv[-1])
        v1 = b.op("scalar", SC, lambda E: E.activation(sb["stat_a"][:, 0:128], pb[1][:, 0:128],
                                                       AF.Copy, scale=sb["cst"][:, 5:6]))
        b.op("scalar", SC, lambda E: E.activation(sb["stat_b"][:, 0:128], sb["stat_a"][:, 0:128],
                                                  AF.Ln, bias=sb["cst"][:, 1:2]))
        vr = b.op("scalar", SC, lambda E: E.activation(sb["stat_c"][:, 0:128], sb["stat_b"][:, 0:128],
                                                       AF.Exp, scale=sb["cst"][:, 2:3]))
        st8["pyb1"] = (SC, v1)
        b.waitv("vector", SC, vr)
        wtok("vector", st8["gate_ld"])
        if ch == 0:
            tap("gatesp", sb["gate_sp"][:, :])
            tap("rrms", sb["stat_c"][:, 0:128])
        vlast = None
        for hp in range(KC_I):
            b.op("vector", EV, lambda E, hp=hp: E.scalar_tensor_tensor(
                sb["ystage"][:, 0:128], sb["y_ld"][:, hp * 128:(hp + 1) * 128],
                sb["norm_w"][:, hp:hp + 1], sb["stat_c"][:, 0:128], OP.mult, OP.mult))
            vlast = b.op("vector", EV, lambda E, hp=hp, ch=ch, sp=sp: E.tensor_tensor(
                sb["y_g"][:, hp * T + ch * 128: hp * T + (ch + 1) * 128],
                sb["ystage"][:, 0:128],
                sb["gate_sp"][:, hp * 512 + (ch % 4) * 128: hp * 512 + (ch % 4) * 128 + 128],
                OP.mult))
        st8["yld_free"] = (EV, vlast)
        if ch == 3:
            gate_tok = (EV, vlast)
    tap("y_g", sb["y_g"][:, :])
    meta["_post"] = dict(ygdone=(S["ev"], S["ev"].v))

def emit_tail(b, nc, meta, pb, pbt, pb7):
    dr, sb, taps, comm = meta["dr"], meta["sb"], meta["taps"], meta["comm"]
    S = meta["sems"]
    MM, EV, SC, CC = S["mm"], S["ev"], S["sc"], S["cc"]
    def wtok(eng, tok):
        if tok is not None:
            b.waitv(eng, tok[0], tok[0].v if getattr(tok[0], "q", False) else tok[1])
    def tap(name, src):
        if name in taps:
            b.waitv("gpsimd", S["ev"], S["ev"].v)
            b.waitv("gpsimd", S["sc"], S["sc"].v)
            b.waitv("gpsimd", S["mm"], S["mm"].v)
            b.dma("gpsimd", taps[name].ap(), src, "st")
    wt = [sb["wq"], sb["wq2"]]
    ygdone = meta["_post"]["ygdone"]

    wv = {}
    b.waitv("gpsimd", MM, MM.v)
    for h in range(2):
        wv[(0, h)] = b.dma("gpsimd", wt[h][:, :], dr["w_out"].ap()[0, :, h * 16:(h + 1) * 16, :], "ld")
    bank_tok = {}
    mo_ev = None
    for oc in range(16):
        pN = (pb[2], pb[3]) if oc % 2 == 0 else (pb[4], pb[5])
        wtok("tensor", wv[(oc, 1)])
        if oc == 0:
            wtok("tensor", ygdone)
        wtok("tensor", bank_tok.get(oc % 2))
        def omm(E, pN=pN):
            last = None
            for kc in range(KC_I):
                lw = wt[kc // 16][:, (kc % 16) * 128:(kc % 16 + 1) * 128]
                for nt in range(2):
                    last = E.matmul(pN[nt][:, :], lw,
                                    sb["y_g"][:, kc * T + nt * 512: kc * T + nt * 512 + 512],
                                    start=(kc == 0), stop=(kc == KC_I - 1))
            return last
        vo = b.op("tensor", MM, omm)
        if oc + 1 < 16:
            b.waitv("gpsimd", MM, vo)
            for h in range(2):
                wv[(oc + 1, h)] = b.dma("gpsimd", wt[h][:, :],
                                        dr["w_out"].ap()[oc + 1, :, h * 16:(h + 1) * 16, :], "ld")
        b.waitv("scalar", MM, vo)
        if oc == 0:
            wtok("scalar", ygdone)
        for nt in range(2):
            mo_ev = b.op("scalar", SC, lambda E, oc=oc, nt=nt, pN=pN: E.copy(
                sb["mo_bf"][:, oc * T + nt * 512: oc * T + nt * 512 + 512], pN[nt][:, :]))
        bank_tok[oc % 2] = (SC, mo_ev)
    tap("mo", sb["mo_bf"][:, :])

    def ln_stats(src_getter, nseg, kcn, dsize, pm, pq, csteps, cast_src=False):
        cstsc = sb["cst"][:, 4:5]
        """generic feature-major LN stats: returns per-seg tokens; stats in
        stat_a (mean) and stat_c (r), segment-sliced."""
        toks = []
        for si in range(nseg):
            lo = si * 512
            sqv = []
            for kc in range(kcn):
                xs = src_getter(kc, lo)
                sq = sb["sq_st"][:, (kc % 2) * 516:(kc % 2) * 516 + 512]
                if kc >= 2:
                    b.waitv("scalar", MM, sqv[kc - 2])
                if cast_src:
                    if kc >= 1:
                        b.waitv("scalar", MM, sqv[kc - 1])
                    b.op("scalar", SC, lambda E, xs=xs: E.activation(sb["ystage"][:, 0:512], xs, AF.Copy))
                    xs = sb["ystage"][:, 0:512]
                vsc = b.op("scalar", SC, lambda E, xs=xs, sq=sq: E.activation(sq, xs, AF.Square))
                b.waitv("tensor", SC, vsc)
                vm = b.op("tensor", MM, lambda E, xs=xs, sq=sq, kc=kc:
                          [E.matmul(pm[:, 0:512], sb["ones_bf"][:, :], xs,
                                    start=(kc == 0), stop=(kc == kcn - 1)),
                           E.matmul(pq[:, 0:512], sb["ones_bf"][:, :], sq,
                                    start=(kc == 0), stop=(kc == kcn - 1))][-1])
                sqv.append(vm)
            sa = sb["stat_a"][:, lo:lo + 512]; sb_ = sb["stat_b"][:, 0:512]
            sc_ = sb["stat_c"][:, lo:lo + 512]
            b.waitv("scalar", MM, sqv[-1])
            b.op("scalar", SC, lambda E, sa=sa: E.activation(sa, pm[:, 0:512], AF.Copy, scale=cstsc))
            v2 = b.op("scalar", SC, lambda E, sc_=sc_: E.activation(sc_, pq[:, 0:512], AF.Copy, scale=cstsc))
            b.waitv("vector", SC, v2)
            b.op("vector", EV, lambda E, sa=sa, sb_=sb_: E.tensor_mul(sb_, sa, sa))
            v3 = b.op("vector", EV, lambda E, sc_=sc_, sb_=sb_: E.tensor_sub(sc_, sc_, sb_))
            b.waitv("scalar", EV, v3)
            b.op("scalar", SC, lambda E, sb_=sb_, sc_=sc_: E.activation(sb_, sc_, AF.Ln, bias=csteps))
            v4 = b.op("scalar", SC, lambda E, sc_=sc_, sb_=sb_: E.activation(sc_, sb_, AF.Exp, scale=sb["cst"][:, 2:3]))
            toks.append(v4)
        return toks
    wtok("scalar", bank_tok.get(0))
    wtok("scalar", bank_tok.get(1))
    wtok("tensor", bank_tok.get(0))
    wtok("tensor", bank_tok.get(1))
    ln2 = ln_stats(lambda kc, lo: sb["mo_bf"][:, kc * T + lo: kc * T + lo + 512],
                   2, KC_H, HID, pb[0], pb[1], sb["cst"][:, 0:1])
    xv = {}
    x2_tr = {}
    for kc in range(KC_H):
        wtok("sync", x2_tr.get(kc - 1))
        if kc < 2:
            wtok("sync", ygdone)
            b.waitv("sync", MM, MM.v)
        xv[kc] = b.dma("sync", sb["xstage"][:, :],
                       fap(dr["x_fm"], [[T + 128, 128], [1, T]], off=kc * 128 * (T + 128) + 128), "ld")
        wtok("vector", xv[kc])
        if kc == 0:
            b.waitv("vector", SC, ln2[-1])
        vlast = None
        for si in range(2):
            lo = si * 512
            mo = sb["mo_bf"][:, kc * T + lo: kc * T + lo + 512]
            b.op("vector", EV, lambda E, mo=mo, lo=lo: E.tensor_sub(
                sb["y_tmp"][:, :], mo, sb["stat_a"][:, lo:lo + 512]))
            b.op("vector", EV, lambda E, lo=lo: E.tensor_mul(
                sb["y_tmp"][:, :], sb["y_tmp"][:, :], sb["stat_c"][:, lo:lo + 512]))
            vlast = b.op("vector", EV, lambda E, kc=kc, lo=lo: E.tensor_add(
                sb["x2_fm"][:, kc * T + lo: kc * T + lo + 512],
                sb["xstage"][:, lo:lo + 512], sb["y_tmp"][:, :]))
        x2_tr[kc] = (EV, vlast)
    tap("x2", sb["x2_fm"][:, :])
    xtd_par = {}
    ve_prev = None
    vst = None
    for kc in range(KC_H):
        for wv2 in range(2):
            w = kc * 2 + wv2
            stg = sb["fstage2"] if w % 2 == 0 else sb["fstage2b"]
            wtok("tensor", x2_tr[kc])
            if ve_prev is not None:
                b.waitv("tensor", EV, ve_prev)
            vt = b.op("tensor", MM, lambda E, kc=kc, wv2=wv2: [
                E.transpose(pb7[:, i * 128:(i + 1) * 128],
                            sb["x2_fm"][:, kc * T + (wv2 * 4 + i) * 128: kc * T + (wv2 * 4 + i + 1) * 128],
                            sb["ident_f"][:, :]) for i in range(4)][-1])
            b.waitv("vector", MM, vt)
            wtok("vector", xtd_par.get(w % 2))
            ve = b.op("vector", EV, lambda E, stg=stg: E.tensor_copy(stg[:, 0:512], pb7[:, :]))
            ve_prev = ve
            b.wait("sync", EV)
            vst = b.dma("sync",
                        fap(dr["x2_d"], [[HID, 128], [128 * HID, 4], [1, 128]],
                            off=wv2 * 4 * 128 * HID + kc * 128),
                        stg[:, 0:512], "st")
            xtd_par[w % 2] = vst
    x2d_tok = vst
    if "x2d" in taps:
        wtok("gpsimd", x2d_tok)
        b.dma("gpsimd", taps["x2d"].ap(), dr["x2_d"].ap()[:, :, :], "st")
    b.waitv("scalar", EV, x2_tr[KC_H - 1][1])
    b.waitv("tensor", EV, x2_tr[KC_H - 1][1])
    sm_g = b.qsems[("gpsimd", "st")]
    b.waitv("vector", sm_g, sm_g.v)
    ln3 = ln_stats(lambda kc, lo: sb["x2_fm"][:, kc * T + lo: kc * T + lo + 512],
                   2, KC_H, HID, pb[0], pb[1], sb["cst"][:, 0:1], cast_src=True)
    b.waitv("vector", SC, ln3[-1])
    b.waitv("vector", MM, MM.v)
    for kc in range(KC_H):
        for si in range(2):
            lo = si * 512
            x2s = sb["x2_fm"][:, kc * T + lo: kc * T + lo + 512]
            b.op("vector", EV, lambda E, x2s=x2s, lo=lo: E.tensor_sub(
                sb["y_tmp"][:, :], x2s, sb["stat_a"][:, lo:lo + 512]))
            b.op("vector", EV, lambda E, kc=kc, lo=lo: E.tensor_mul(
                sb["xhat2"][:, kc * T + lo: kc * T + lo + 512],
                sb["y_tmp"][:, :], sb["stat_c"][:, lo:lo + 512]))
    tap("xhat2", sb["xhat2"][:, :])
    xh2_tok = (EV, S["ev"].v)

    wt4 = [sb["wq"], sb["wq2"], sb["wq3"], sb["wq4"]]
    ag1 = meta["_mix"]["ag1_tok"]
    wtok("vector", x2d_tok)
    halves = [(0, 22), (22, 21)]
    acc_tok = None
    for hi, (k0, nk) in enumerate(halves):
        guv = {}
        wtok("sync", ag1)
        b.waitv("sync", MM, MM.v)
        guv[0] = (b.dma("sync", wt4[0][:, :], dr["ag1g_out"].ap()[k0, :, :, :], "ld"),
                  b.dma("sync", wt4[1][:, :], dr["ag1u_out"].ap()[k0, :, :, :], "ld"))
        h_tok = {}
        vgu_prev = None
        for ol in range(nk):
            oc = k0 + ol
            pg, pu = (pb[0], pb[2]), (pb[1], pb[3])
            bg = wt4[(ol % 2) * 2]; bu = wt4[(ol % 2) * 2 + 1]
            wtok("tensor", guv[ol][1])
            wtok("tensor", h_tok.get(ol - 1))
            if hi == 0 and ol == 0:
                wtok("tensor", xh2_tok)
            def gmm(E, bg=bg, bu=bu):
                last = None
                for kc in range(KC_H):
                    for nt in range(2):
                        xs = sb["xhat2"][:, kc * T + nt * 512: kc * T + nt * 512 + 512]
                        E.matmul(pb[nt][:, :], bg[:, kc * 128:(kc + 1) * 128], xs,
                                 start=(kc == 0), stop=(kc == KC_H - 1))
                        last = E.matmul(pb[2 + nt][:, :], bu[:, kc * 128:(kc + 1) * 128], xs,
                                        start=(kc == 0), stop=(kc == KC_H - 1))
                return last
            vgu = b.op("tensor", MM, gmm)
            if ol + 1 < nk:
                b.waitv("sync", MM, vgu if ol == 0 else vgu_prev)
                guv[ol + 1] = (
                    b.dma("sync", wt4[((ol + 1) % 2) * 2][:, :], dr["ag1g_out"].ap()[oc + 1, :, :, :], "ld"),
                    b.dma("sync", wt4[((ol + 1) % 2) * 2 + 1][:, :], dr["ag1u_out"].ap()[oc + 1, :, :, :], "ld"))
            vgu_prev = vgu
            b.waitv("scalar", MM, vgu)
            if ol >= 1:
                wtok("scalar", h_tok[ol - 1])
            vh = None
            for nt in range(2):
                vsl = b.op("scalar", SC, lambda E, nt=nt: E.activation(
                    sb["sq_st"][:, nt * 516: nt * 516 + 512], pb[nt][:, :], AF.Silu))
                b.waitv("vector", SC, vsl)
                vh = b.op("vector", EV, lambda E, ol=ol, nt=nt: E.tensor_tensor(
                    sb["hmat"][:, ol * T + nt * 512: ol * T + nt * 512 + 512],
                    sb["sq_st"][:, nt * 516: nt * 516 + 512], pb[2 + nt][:, :], OP.mult))
            h_tok[ol] = (EV, vh)
            if hi == 0 and ol == 0:
                tap("h0", sb["hmat"][:, 0:T])
                tap("wg0", wt4[0][:, :])
        hdone = (EV, S["ev"].v)
        steps = [(cq, tbg, kl) for cq in range(4) for tbg in range(2) for kl in range(nk)]
        NS = len(steps)
        dq = lambda s: "sync" if s % 2 == 0 else "scalar"
        def dload(s):
            cq, tbg, kl = steps[s]
            return b.dma(dq(s), wt4[s % 4][:, 0:512],
                         dr["ag1d_out"].ap()[k0 + kl, cq, :, :], "ld")
        dv = {}
        vdm_tok = {}
        b.waitv("sync", MM, MM.v)
        b.waitv("scalar", MM, MM.v)
        for s in range(min(4, NS)):
            dv[s] = dload(s)
        ev_tok = None
        for s, (cq, tbg, kl) in enumerate(steps):
            wb = wt4[s % 4]
            wtok("tensor", dv[s])
            if kl == 0:
                if s == 0:
                    wtok("tensor", hdone)
                wtok("tensor", acc_tok)
            def dmm(E, wb=wb, kl=kl, tbg=tbg):
                last = None
                for tl in range(4):
                    tb = tbg * 4 + tl
                    last = E.matmul(pb[tl][:, :],
                                    sb["hmat"][:, kl * T + tb * 128:(kl * T + tb * 128) + 128],
                                    wb[:, 0:512], start=(kl == 0), stop=(kl == nk - 1))
                return last
            vdm = b.op("tensor", MM, dmm)
            vdm_tok[s] = vdm
            if s + 4 < NS:
                b.waitv(dq(s + 4), MM, vdm)
                dv[s + 4] = dload(s + 4)
            if kl == nk - 1:
                for tl in range(4):
                    tb = tbg * 4 + tl
                    dst = sb["out_acc"][:, tb * HID + cq * 512: tb * HID + cq * 512 + 512]
                    if hi == 0:
                        b.waitv("scalar", MM, vdm)
                        ev_tok = b.op("scalar", SC, lambda E, dst=dst, tl=tl: E.copy(dst, pb[tl][:, :]))
                    else:
                        b.waitv("vector", MM, vdm)
                        ev_tok = b.op("vector", EV, lambda E, dst=dst, tl=tl: E.tensor_add(
                            dst, dst, pb[tl][:, :]))
                acc_tok = ((SC, ev_tok) if hi == 0 else (EV, ev_tok))
    tap("outacc", sb["out_acc"][:, :])

    if "outacc" in taps:
        sg2 = b.qsems[("gpsimd", "st")]
        b.waitv("vector", sg2, sg2.v)
    x2l = None
    for tb in range(8):
        wtok("sync", x2d_tok)
        wtok("sync", x2l)
        vxf = b.dma("sync", sb["fstage"][:, :], dr["x2_d"].ap()[tb, :, :], "ld")
        osl = sb["out_acc"][:, tb * HID: (tb + 1) * HID]
        wtok("vector", acc_tok)
        vm_ = b.op("vector", EV, lambda E, osl=osl: E.reduce_sum(
            sb["stat_b"][:, 16:17], osl, mybir.AxisListType.X))
        b.waitv("vector", EV, vm_)
        vsq = None
        for si in range(4):
            if tb == 0 and si == 0:
                b.waitv("scalar", EV, vm_)
            vsq = b.op("scalar", SC, lambda E, osl=osl, si=si: E.activation(
                sb["sq_big"][:, si * 512:(si + 1) * 512],
                osl[:, si * 512:(si + 1) * 512], AF.Square))
        b.waitv("vector", SC, vsq)
        vq2b = b.op("vector", EV, lambda E: E.reduce_sum(
            sb["stat_b"][:, 8:9], sb["sq_big"][:, 0:2048], mybir.AxisListType.X))
        b.waitv("vector", EV, vq2b)
        b.waitv("scalar", EV, vq2b)
        vt1 = b.op("vector", EV, lambda E: E.tensor_scalar_mul(
            sb["stat_b"][:, 16:17], sb["stat_b"][:, 16:17], 1.0 / HID))
        b.waitv("vector", EV, vt1)
        vt2 = b.op("vector", EV, lambda E: E.tensor_scalar_mul(
            sb["stat_b"][:, 8:9], sb["stat_b"][:, 8:9], 1.0 / HID))
        b.waitv("vector", EV, vt2)
        vt3 = b.op("vector", EV, lambda E: E.tensor_mul(
            sb["stat_b"][:, 10:11], sb["stat_b"][:, 16:17], sb["stat_b"][:, 16:17]))
        b.waitv("vector", EV, vt3)
        vv2 = b.op("vector", EV, lambda E: E.tensor_sub(
            sb["stat_b"][:, 8:9], sb["stat_b"][:, 8:9], sb["stat_b"][:, 10:11]))
        b.waitv("scalar", EV, vv2)
        b.op("scalar", SC, lambda E: E.activation(
            sb["stat_b"][:, 9:10], sb["stat_b"][:, 8:9], AF.Ln, bias=sb["cst"][:, 0:1]))
        vr = b.op("scalar", SC, lambda E: E.activation(
            sb["stat_b"][:, 17:18], sb["stat_b"][:, 9:10], AF.Exp, scale=sb["cst"][:, 2:3]))
        b.waitv("vector", SC, vr)
        if "ln4" in taps:
            b.waitv("gpsimd", SC, vr)
            b.waitv("gpsimd", EV, S["ev"].v)
            b.dma("gpsimd", taps["ln4"].ap()[tb], sb["stat_b"][:, 0:24], "st")
            sgq = b.qsems[("gpsimd", "st")]
            b.waitv("vector", sgq, sgq.v)
        wtok("vector", vxf)
        vap1 = b.op("vector", EV, lambda E, osl=osl: E.tensor_scalar(
            osl, osl, sb["stat_b"][:, 16:17], sb["stat_b"][:, 17:18], OP.subtract, OP.mult))
        b.waitv("vector", EV, vap1)
        b.op("vector", EV, lambda E, osl=osl: E.tensor_add(osl, osl, sb["fstage"][:, :]))
        b.wait("sync", EV)
        vout = b.dma("sync", dr["out"].ap()[tb * 128:(tb + 1) * 128, :], osl, "st")
        x2l = vout
    b.waitv("gpsimd", b.qsems[("sync", "st")], b.qsems[("sync", "st")].v)
    b.wait("gpsimd", CC)

    for qn in ("sync", "scalar", "gpsimd"):
        for kk in ("ld", "st"):
            sm = b.qsems[(qn, kk)]
            b.waitv("sync", sm, sm.v)
            b.waitv("gpsimd", sm, sm.v)
    b.waitv("gpsimd", CC, CC.v)
    b.waitv("sync", CC, CC.v)

def finish_build(nc, meta):
    pb = [nc.alloc_psum_tensor(f"pb{i}", [128, 512], F32) for i in range(6)]
    pbt = nc.alloc_psum_tensor("pbt", [128, 1024], BF16)
    pb7 = nc.alloc_psum_tensor("pb7", [128, 512], F32)
    bld = Bld(nc, meta["sems"], meta["taps"])
    def body(b):
        emit_program(b, nc, meta, pb, pbt, pb7, None)
        emit_post(b, nc, meta, pb, pbt, pb7)
        emit_tail(b, nc, meta, pb, pbt, pb7)
    with nc.Block() as block:
        block.sync(lambda e: bld.walk("sync", body))
        block.scalar(lambda e: bld.walk("scalar", body))
        block.vector(lambda e: bld.walk("vector", body))
        block.tensor(lambda e: bld.walk("tensor", body))
        block.gpsimd(lambda e: bld.walk("gpsimd", body))
    return nc

_BUILT = {}

def kernel(**inputs):
    cores = prep_inputs(inputs)
    key = "main"
    if key not in _BUILT:
        nc, meta = build_nc(comm=True)
        finish_build(nc, meta)
        _BUILT[key] = nc
    nc = _BUILT[key]
    res = run_bass_kernel_spmd(nc, cores, core_ids=list(range(NCORE)))
    out = np.empty((2, 4096, HID), np.float32)
    for c in range(NCORE):
        bt, s = c // 4, c % 4
        out[bt, s * T:(s + 1) * T] = res.results[c]["out"]
    return out